# version 03
# baseline (speedup 1.0000x reference)
import numpy as np
import ml_dtypes
import concourse.bass as bass
import concourse.mybir as mybir
from concourse.bass_utils import run_bass_kernel_spmd

F32 = mybir.dt.float32
BF16 = mybir.dt.bfloat16
AF = mybir.ActivationFunctionType
ALU = mybir.AluOpType
AX = mybir.AxisListType
bf16 = ml_dtypes.bfloat16

D = 1024
NCH = 8
H = 16
DH = 64
OWN = 4096
HALO = 2432
BUF = HALO + OWN
NS = BUF // 128
OWN0 = HALO // 128
PREF = 12288
NPS = PREF // 128
DI = 2816
NIC = DI // 128
DF = 4096
NOC = DF // 128
BIG = np.float32(3.0e38)


def ts(i, n=128):
    return slice(i * n, (i + 1) * n)


class EM:

    ENGS = ("pe", "dve", "act", "pool", "sy")
    NSYD = 10
    NGPD = 8
    TAGS = (("pe", "dve", "act", "pool", "sy")
            + tuple(f"syd{i}" for i in range(10))
            + tuple(f"gpd{i}" for i in range(8)))

    def __init__(self):
        self.stages = {e: [] for e in self.ENGS}
        self.cnt = {t: 0 for t in self.TAGS}
        self.seen = {e: {} for e in self.ENGS}
        self.buf = {}
        self.dma_rr = {"syd": 0, "gpd": 0}

    def _b(self, k):
        if k not in self.buf:
            self.buf[k] = {"w": [], "r": {}}
        return self.buf[k]

    def stage(self, e, fn, reads=(), writes=(), accw=(), dma=False, steps=1):
        waits = {}
        seen = self.seen[e]

        def need(tag, val):
            if seen.get(tag, 0) < val:
                if waits.get(tag, 0) < val:
                    waits[tag] = val

        rdkeys = list(reads)
        for k in reads:
            b = self._b(k)
            for wt, wv in b["w"]:
                need(wt, wv)
                if wt.startswith(("syd", "gpd")):
                    rdkeys.append("SEM_" + wt)
        for k in writes:
            b = self._b(k)
            for wt, wv in b["w"]:
                need(wt, wv)
            for t, v in b["r"].items():
                need(t, v)
        if dma:
            q = "syd" if e == "sy" else "gpd"
            n = self.NSYD if q == "syd" else self.NGPD
            tag = f"{q}{self.dma_rr[q] % n}"
            self.dma_rr[q] += 1
            inc = 16
            assert steps == 1
            b = self._b("SEM_" + tag)
            for wt, wv in b["w"]:
                need(wt, wv)
            for t, v in b["r"].items():
                need(t, v)
        else:
            tag = e
            inc = steps
        for t, v in waits.items():
            seen[t] = v
        self.cnt[tag] += inc
        v = self.cnt[tag]
        if dma:
            b = self._b("SEM_" + tag)
            b["w"] = [(tag, v)]
            b["r"] = {}
        for k in writes:
            b = self._b(k)
            b["w"] = [(tag, v)]
            b["r"] = {}
        for k in accw:
            b = self._b(k)
            b["w"].append((tag, v))
        for k in rdkeys:
            self._b(k)["r"][tag] = v
        self.stages[e].append((sorted(waits.items()), fn, tag, inc, steps))

    def alias(self, dst, src):
        self.buf[dst] = self._b(src)

    def barrier(self):
        snap = dict(self.cnt)
        for e in self.ENGS:
            waits = sorted((t, v) for t, v in snap.items() if v > 0 and t != e)
            seen = self.seen[e]
            for t, v in waits:
                if seen.get(t, 0) < v:
                    seen[t] = v
            waits = [(t, v) for t, v in waits]
            self.cnt[e] += 1
            self.stages[e].append((waits, lambda eng: eng.nop(), e, 1, 1))

    def replay(self, nc):
        sems = {}
        ctxs = []
        for t in self.TAGS:
            h = nc.semaphore(f"s_{t}")
            sems[t] = h.__enter__()
            ctxs.append(h)
        block = nc.Block().__enter__()
        engmap = {"pe": "tensor", "dve": "vector", "act": "scalar",
                  "pool": "gpsimd", "sy": "sync"}
        rcnt = {t: 0 for t in self.TAGS}
        for e in self.ENGS:
            def body(eng, e=e):
                for waits, fn, tag, inc, steps in self.stages[e]:
                    for t, v in waits:
                        eng.wait_ge(sems[t], v)
                    if steps == 1:
                        inst = fn(eng)
                        inst.then_inc(sems[tag], inc)
                    else:
                        base = rcnt[tag]
                        nb = [0]

                        def bump(pinst):
                            nb[0] += 1
                            pinst.then_inc(sems[tag], 1)
                            eng.wait_ge(sems[tag], base + nb[0])
                        inst = fn(eng, bump)
                        assert nb[0] == steps - 1, (e, nb[0], steps)
                        inst.then_inc(sems[tag], 1)
                    rcnt[tag] += inc
                if e == "sy":
                    for t in self.TAGS:
                        if t.startswith(("syd", "gpd")) and self.cnt[t] > 0:
                            eng.wait_ge(sems[t], self.cnt[t])
            getattr(block, engmap[e])(body)
        block.__exit__(None, None, None)
        for h in ctxs:
            h.__exit__(None, None, None)


def build_program(nlayers=6, do_a=True, dumps=(), bffn=True, dbg_sub=None, dbg_suba=None):
    nc = bass.Bass("TRN2", target_bir_lowering=False)
    dram = {}

    def din(name, shape, dt=F32):
        dram[name] = nc.dram_tensor(name, list(shape), dt, kind="ExternalInput")

    din("xin", [PREF + OWN, D])
    din("xbuf", [BUF, D])
    din("rotcat", [NS, 128, 256])
    din("pvalidT", [128, NPS])
    din("hvalrow", [128, NS])
    din("maskA_t", [128, 128], BF16)
    din("maskC_t", [128, 128], BF16)
    din("m0_t", [128, 128], BF16)
    din("m1_t", [128, 128], BF16)
    din("m2_t", [128, 128], BF16)
    din("triA", [128, 128], BF16)
    din("identf", [128, 128])
    din("identb", [128, 128], BF16)
    din("dupI", [64, 128], BF16)
    din("Wqkv", [6, D, 3 * D]); din("Wo6", [6, D, D])
    din("W1p", [6, D, 2 * DI]); din("W2p", [6, DI, D])
    din("b1pr", [6, 128, 2 * NIC])
    din("lnbr", [6, 128, NCH])
    din("lngr", [6, 128, NCH]); din("b2r", [6, 128, NCH])
    din("qsrep", [6, 128, DH]); din("ksrep", [6, 128, DH])
    din("Wq", [D, D]); din("Wk", [D, D]); din("Wv", [D, D]); din("WoA", [D, D])
    din("Wf1", [D, DF]); din("Wf2", [DF, D])
    din("g1t", [128, D]); din("b1t", [128, D])
    din("g2r", [128, NCH]); din("b2ar", [128, NCH])
    dram["out"] = nc.dram_tensor("out", [OWN, D], F32, kind="ExternalOutput")
    dram["ysc0"] = nc.dram_tensor("ysc0", [NS, NCH, 128, 128], F32)
    dram["ysc1"] = nc.dram_tensor("ysc1", [NS, NCH, 128, 128], F32)
    dram["xasc"] = nc.dram_tensor("xasc", [32, NCH, 128, 128], F32)
    dram["cvec"] = nc.dram_tensor("cvec", [DF], F32)
    for d in dumps:
        nm, src = d[0], d[1]
        shp = list(dram[src].shape)
        dram[nm] = nc.dram_tensor(nm, shp, F32, kind="ExternalOutput")

    sbuf = {}

    def S(name, shape, dt=F32):
        sbuf[name] = nc.alloc_sbuf_tensor(name, list(shape), dt).ap()

    S("identf_s", [128, 128]); S("identb_s", [128, 128], BF16)
    S("maskA_s", [128, 128], BF16); S("maskC_s", [128, 128], BF16)
    S("m0_s", [128, 128], BF16); S("m1_s", [128, 128], BF16); S("m2_s", [128, 128], BF16)
    S("triA_s", [128, 128], BF16); S("dupI_s", [64, 128], BF16)
    S("ones_c", [128, 1], BF16); S("ones_r", [1, 128], BF16)
    S("pval_s", [128, NPS]); S("hval_s", [128, NS])
    S("rot", [128, 2 * 256])
    S("qs_s", [128, DH]); S("ks_s", [128, DH])
    S("lng_s", [128, NCH]); S("lnb_s", [128, NCH], BF16); S("b2r_s", [128, NCH])
    S("b1p_s", [128, 2 * NIC]); S("cb1_s", [128, 2 * NIC])
    S("g2r_s", [128, NCH]); S("b2a_s", [128, NCH], BF16); S("cbA_s", [128, NOC])
    S("trow", [1, 1024]); S("trow2", [1, 1024])
    S("mrow", [1, 512], BF16); S("rrow", [1, 512], BF16)
    S("ssum", [128, H]); S("rn", [128, H]); S("tt", [128, H])
    S("nmx", [128, H]); S("rsb", [128, H])
    S("den_t", [128, H]); S("rA", [128, H])
    S("ycar", [128, 4])
    S("ceps", [128, 1])
    AR = nc.alloc_sbuf_tensor("arena", [128, 67584], BF16).ap()
    WK = nc.alloc_sbuf_tensor("warena", [128, 29696], BF16).ap()
    sbuf["arena"] = AR

    def view(base, byte_off, dt, ncols, npart=128):
        assert byte_off % 2 == 0
        c0 = byte_off // 2
        if dt == BF16:
            ap = base[0:npart, c0:c0 + ncols]
        else:
            ap = base[0:npart, c0:c0 + 2 * ncols].bitcast(F32)
        return ap

    sbuf["ytile"] = view(AR, 49152, F32, 8 * NCH * 128)
    sbuf["afm"] = view(AR, 81920, BF16, NCH * 512)
    sbuf["kfm"] = view(AR, 106496, BF16, 4 * D)
    sbuf["vkm"] = view(AR, 114688, BF16, 4 * D)
    sbuf["Pt"] = view(AR, 122880, BF16, 2 * 4 * 384)
    sbuf["ptsb"] = view(AR, 129024, BF16, 2 * 12 * 128)
    sbuf["xag"] = view(AR, 49152, F32, NCH * 512)
    sbuf["aAfm"] = view(AR, 65536, BF16, NCH * 512)
    sbuf["xld"] = view(AR, 73728, F32, 2 * D)
    sbuf["xnfm"] = view(AR, 81920, F32, NCH * 128)
    sbuf["xnt"] = view(AR, 86016, F32, D)
    sbuf["Qfm"] = view(AR, 106496, BF16, D)
    sbuf["Kfm"] = view(AR, 108544, BF16, D)
    sbuf["Kkm"] = view(AR, 110592, BF16, D)
    sbuf["Vkm2"] = view(AR, 112640, BF16, D)
    sbuf["h1b"] = view(AR, 114688, BF16, D)
    sbuf["h1fm"] = view(AR, 116736, BF16, NCH * 128)
    sbuf["aAkm"] = view(AR, 118784, BF16, D)
    sbuf["ATs"] = view(AR, 120832, BF16, 2 * 128)
    sbuf["SbD"] = view(AR, 121344, BF16, H * 64)
    sbuf["zbD"] = view(AR, 123392, BF16, H)
    sbuf["Sacc"] = view(AR, 123456, F32, H * 64, npart=64)
    sbuf["zacc"] = view(AR, 127552, F32, H, npart=64)
    sbuf["SaccB"] = view(AR, 127616, BF16, H * 64, npart=64)
    sbuf["zaccB"] = view(AR, 129664, BF16, H, npart=64)
    sbuf["prodb"] = view(AR, 131072, BF16, D)
    sbuf["qA"] = view(WK, 0, F32, 2 * D)
    sbuf["kA"] = view(WK, 8192, F32, 2 * D)
    sbuf["t1"] = view(WK, 16384, F32, D)
    sbuf["t2"] = view(WK, 20480, F32, D)
    sbuf["t3"] = view(WK, 24576, F32, D)
    sbuf["qD"] = view(WK, 28672, BF16, D)
    sbuf["kD"] = view(WK, 30720, BF16, D)
    sbuf["qfm"] = view(WK, 32768, BF16, D)
    sbuf["ubf"] = view(WK, 34816, BF16, 2 * NCH * 128)
    sbuf["akm"] = view(WK, 38912, BF16, D)
    sbuf["ymg"] = view(WK, 0, F32, 2 * NCH * 384)
    sbuf["u2g"] = view(WK, 24576, BF16, NCH * 384)
    sbuf["fsb"] = view(WK, 30720, BF16, NIC * 384)
    sbuf["sqt"] = view(WK, 47616, BF16, 2 * 384)
    sbuf["tf1"] = view(WK, 49152, F32, D)
    sbuf["tf2"] = view(WK, 53248, F32, D)
    sbuf["g1t_s"] = view(WK, 40960, F32, D)
    sbuf["b1t_s"] = view(WK, 45056, F32, D)
    sbuf["xag2"] = view(WK, 28672, F32, NCH * 128)
    sbuf["yg"] = view(WK, 32768, F32, NCH * 128)
    sbuf["xkm"] = view(WK, 36864, F32, D)
    sbuf["fA"] = view(WK, 40960, BF16, NOC * 128)
    sbuf["xab"] = view(WK, 49152, BF16, NCH * 128)
    sbuf["xac"] = view(WK, 51200, BF16, NCH * 128)
    sbuf["prkm"] = view(WK, 53248, BF16, D)
    sbuf["sqA"] = view(WK, 55296, BF16, 2 * 128)
    sbuf["ssb"] = sbuf["xag2"]
    sbuf["gsb"] = sbuf["yg"]
    sbuf["outk"] = sbuf["xkm"]

    ps = {}
    ps["P1"] = nc.alloc_psum_tensor("psP1", [128, 1536], F32)
    ps["P2"] = nc.alloc_psum_tensor("psP2", [128, 1536], F32)
    ps["trE"] = nc.alloc_psum_tensor("psE", [128, 1024], BF16)
    ps["trF"] = nc.alloc_psum_tensor("psF", [128, 1024], BF16)

    em = EM()
    st = dict(em=em, T=sbuf, ps=ps, dram=dram)
    st["bffn"] = bffn
    st["dbg_sub"] = dbg_sub
    st["dbg_suba"] = dbg_suba
    if dbg_suba is not None:
        for nm, shp in (("daA", [128, D]), ("dQf", [128, D]), ("dKf", [128, D]),
                        ("dVk", [128, D]), ("dSb", [128, H * 64]), ("dzb", [128, H])):
            dram[nm] = nc.dram_tensor(nm, shp, F32, kind="ExternalOutput")
    if dbg_sub is not None:
        for nm, cols in (("dakm", D), ("dqfm", D), ("dkfm", 3 * D), ("dvkm", 3 * D)):
            dram[nm] = nc.dram_tensor(nm, [128, cols], F32, kind="ExternalOutput")
    _p_consts(st)
    _p_layer0(st)
    for l in range(1, nlayers + 1):
        _p_blayer(st, l)
    if do_a:
        _p_abranch(st, nlayers)
    for d in dumps:
        _p_dump(st, d[0], d[1], d[2] if len(d) > 2 else ())
    em.replay(nc)
    return nc


def _p_consts(st):
    em, T, dram = st["em"], st["T"], st["dram"]
    for dst, src in (("identf_s", "identf"), ("identb_s", "identb"),
                     ("maskA_s", "maskA_t"), ("maskC_s", "maskC_t"),
                     ("m0_s", "m0_t"), ("m1_s", "m1_t"), ("m2_s", "m2_t"),
                     ("triA_s", "triA"), ("pval_s", "pvalidT"), ("hval_s", "hvalrow"),
                     ("g2r_s", "g2r")):
        em.stage("sy", lambda e, d=dst, s=src: e.dma_start(out=T[d][:], in_=dram[s].ap()),
                 writes=[dst], dma=True)
    em.stage("pool", lambda e: e.dma_start(out=T["dupI_s"][:], in_=dram["dupI"].ap()),
             writes=["dupI_s"], dma=True)
    em.stage("pool", lambda e: e.dma_start(out=T["b2a_s"][:], in_=dram["b2ar"].ap()),
             writes=["b2a_s"], dma=True)
    em.stage("pool", lambda e: e.memset(T["ones_c"][:], 1.0), writes=["ones_c"])
    em.stage("pool", lambda e: e.memset(T["ones_r"][:], 1.0), writes=["ones_r"])
    em.stage("pool", lambda e: e.memset(T["ceps"][:], 1e-5), writes=["ceps"])


def _p_dump(st, nm, src, keys=()):
    em, dram = st["em"], st["dram"]
    em.stage("sy", lambda e: e.dma_start(out=dram[nm].ap().flatten(),
                                         in_=dram[src].ap().flatten()),
             reads=list(keys), dma=True)


def _p_ln(st, src_t, dst_t, ksrc, kdst, g=None, b=None):
    em, T = st["em"], st["T"]
    su, tt, rn = T["ssum"], T["tt"], T["rn"]

    em.stage("pool", lambda e: e.tensor_mul(T["t3"][:], src_t, src_t),
             reads=[ksrc], writes=["t3"])

    def f1(e, bump):
        e.reduce_sum(su[:, 0:1].unsqueeze(1), src_t.unsqueeze(1), AX.X)
        i = e.reduce_sum(su[:, 1:2].unsqueeze(1), T["t3"][:].unsqueeze(1), AX.X)
        bump(i)
        e.tensor_scalar_mul(su[:, 2:3], su[:, 0:1], 1.0 / 1024.0)
        i = e.tensor_scalar_mul(su[:, 3:4], su[:, 1:2], 1.0 / 1024.0)
        bump(i)
        i = e.tensor_mul(tt[:, 0:1], su[:, 2:3], su[:, 2:3])
        bump(i)
        return e.tensor_sub(tt[:, 1:2], su[:, 3:4], tt[:, 0:1])
    em.stage("dve", f1, reads=[ksrc, "t3"], writes=["lnstat"], steps=4)

    def f2(e, bump):
        i = e.activation(tt[:, 2:3], tt[:, 1:2], AF.Ln, bias=T["ceps"][:])
        bump(i)
        return e.activation(rn[:, 0:1], tt[:, 2:3], AF.Exp, scale=-0.5)
    em.stage("act", f2, reads=["lnstat", "ceps"], writes=["lnr"], steps=2)

    def f3(e, bump):
        i = e.tensor_scalar_mul(tt[:, 3:4], su[:, 2:3], -1.0)
        bump(i)
        return e.tensor_scalar(dst_t, src_t, tt[:, 3:4], rn[:, 0:1], ALU.add, ALU.mult)
    em.stage("dve", f3, reads=["lnstat", "lnr", ksrc], writes=[kdst], steps=2)

    if g is not None:
        def f4(e, bump):
            i = e.tensor_mul(T["t3"][:], dst_t, g)
            bump(i)
            return e.tensor_add(dst_t, T["t3"][:], b)
        em.stage("pool", f4, reads=[kdst, "g1t_s", "b1t_s"], writes=[kdst, "t3"],
                 steps=2)


def _p_transpose8(st, src_t, kin, pkey, dt="b"):
    em, T, ps = st["em"], st["T"], st["ps"]
    ident = T["identb_s"] if dt == "b" else T["identf_s"]
    ikey = "identb_s" if dt == "b" else "identf_s"

    def f(e):
        for c in range(NCH):
            i = e.transpose(ps[pkey][:, ts(c)], src_t[:, ts(c)], ident[:])
        return i
    em.stage("pe", f, reads=[kin, ikey], writes=[pkey])


def _p_layer0(st):
    em, T, ps, dram = st["em"], st["T"], st["ps"], st["dram"]
    xbuf = dram["xbuf"].ap()
    y0, y1 = dram["ysc0"].ap(), dram["ysc1"].ap()
    for m in range(NS):
        slot = m % 2
        xl = T["xld"][:, slot * D:(slot + 1) * D]
        em.stage("sy", lambda e, m=m, xl=xl: e.dma_start(out=xl, in_=xbuf[ts(m), :]),
                 writes=[f"xld{slot}"], dma=True)
        _p_ln(st, xl[:, :], T["xnt"][:, :], f"xld{slot}", "xnt")
        _p_transpose8(st, T["xnt"][:], "xnt", "P1", dt="f")
        em.stage("dve", lambda e: e.tensor_copy(T["xnfm"][:], ps["P1"][:, 0:1024]),
                 reads=["P1"], writes=["xnfm"])
        em.stage("sy", lambda e, m=m: e.dma_start(
            out=y0[m].rearrange("c p t -> p c t"),
            in_=T["xnfm"][:].rearrange("p (c t) -> p c t", c=NCH)),
            reads=["xnfm"], writes=[f"ysrc_1_{m}"], dma=True)
        if m < OWN0:
            em.stage("sy", lambda e, m=m: e.dma_start(
                out=y1[m].rearrange("c p t -> p c t"),
                in_=T["xnfm"][:].rearrange("p (c t) -> p c t", c=NCH)),
                reads=["xnfm"], writes=[f"y1init_{m}"], dma=True)


def _p_blayer(st, l):
    em, T, ps, dram = st["em"], st["T"], st["ps"], st["dram"]
    em.barrier()
    G = 3 * l + 1
    kv0 = G - 3
    q0 = G - 1
    ysrc = dram["ysc0" if l % 2 == 1 else "ysc1"].ap()
    ydst = dram["ysc1" if l % 2 == 1 else "ysc0"].ap()
    A = T["arena"]
    WQKV, WO = 0, 45056
    yt = T["ytile"][:].rearrange("p (s c t) -> p s c t", s=8, c=NCH)

    for dst, src in (("qs_s", "qsrep"), ("ks_s", "ksrep"), ("lng_s", "lngr"),
                     ("b2r_s", "b2r"), ("b1p_s", "b1pr")):
        em.stage("sy", lambda e, d=dst, s=src: e.dma_start(
            out=T[d][:], in_=dram[s].ap()[l - 1]), writes=[dst], dma=True)
    em.stage("pool", lambda e: e.dma_start(
        out=T["lnb_s"][:], in_=dram["lnbr"].ap()[l - 1]), writes=["lnb_s"], dma=True)

    wq, wo = dram["Wqkv"].ap()[l - 1], dram["Wo6"].ap()[l - 1]
    for c in range(NCH):
        em.stage("pool", lambda e, c=c: e.dma_start(
            out=A[:, WQKV + c * 3072: WQKV + (c + 1) * 3072], in_=wq[ts(c), :]),
            writes=["slotW1"] if c == 0 else [], accw=["slotW1"] if c > 0 else [],
            dma=True)
    for c in range(NCH):
        em.stage("pool", lambda e, c=c: e.dma_start(
            out=A[:, WO + c * 1024: WO + (c + 1) * 1024], in_=wo[ts(c), :]),
            writes=["slotW2"] if c == 0 else [], accw=["slotW2"] if c > 0 else [],
            dma=True)

    def load_y(mm):
        if mm >= NS:
            return
        sl = (mm - G) % 8
        if mm == kv0 - 1 and l > 1:
            key = f"ymid_{l - 1}_{mm}"
        else:
            key = f"ysrc_{l}_{mm}"
        em.stage("sy", lambda e, mm=mm, sl=sl: e.dma_start(
            out=yt[:, sl],
            in_=ysrc[mm].rearrange("c p t -> p c t")),
            reads=[key], writes=[f"yt{sl}"], dma=True)

    def build_u(mm):
        if mm >= NS:
            return
        ysl = (mm - G) % 8
        psl = (mm - 1 - G) % 8
        usl = mm % 2
        ub = T["ubf"][:, usl * 1024:(usl + 1) * 1024].rearrange("p (c t) -> p c t", c=NCH)

        def fu(e, ub=ub, ysl=ysl, psl=psl, mm=mm):
            e.tensor_copy(ub[:, 0:4], yt[:, ysl, 0:4])
            e.tensor_copy(ub[:, 4:8, 1:128], yt[:, ysl, 4:8, 0:127])
            return e.tensor_scalar_mul(ub[:, 4:8, 0:1], yt[:, psl, 4:8, 127:128],
                                       T["hval_s"][:, mm:mm + 1])
        em.stage("pool", fu, reads=[f"yt{ysl}", f"yt{psl}", "hval_s"],
                 writes=[f"ubf{usl}"])

    def emit_rot(mm):
        if mm >= NS:
            return
        us = mm % 2
        em.stage("pool", lambda e, mm=mm, us=us: e.dma_start(
            out=T["rot"][:, us * 256:(us + 1) * 256], in_=dram["rotcat"].ap()[mm]),
            writes=[f"rot{us}"], dma=True)

    def emit_qkv_part(mm, part):
        if mm >= NS:
            return
        us = mm % 2
        ubv = T["ubf"][:, us * 1024:(us + 1) * 1024].rearrange("p (c t) -> p c t",
                                                              c=NCH)
        off = 0 if part == "q" else 1536

        def fmm(e, off=off, ubv=ubv):
            for c in range(NCH):
                for sl in range(3):
                    i = e.matmul(ps["P1"][:, sl * 512:(sl + 1) * 512],
                                 ubv[:, c],
                                 A[:, WQKV + c * 3072 + off + sl * 512:
                                    WQKV + c * 3072 + off + (sl + 1) * 512],
                                 start=(c == 0), stop=(c == NCH - 1))
            return i
        em.stage("pe", fmm, reads=[f"ubf{us}", "slotW1"], writes=["P1"])
        if part == "q":
            em.stage("dve", lambda e, us=us: e.tensor_copy(
                T["qA"][:, us * D:(us + 1) * D], ps["P1"][:, 0:1024]),
                reads=["P1"], writes=[f"qA{us}"])
            em.stage("act", lambda e, us=us: e.copy(
                T["kA"][:, us * D:us * D + 512], ps["P1"][:, 1024:1536]),
                reads=["P1"], writes=[f"kAlo{us}"])
        else:
            em.stage("act", lambda e, us=us: e.copy(
                T["kA"][:, us * D + 512:(us + 1) * D], ps["P1"][:, 0:512]),
                reads=["P1"], writes=[f"kA{us}"])
            vkmr = T["vkm"][:, (mm % 4) * D:((mm % 4) + 1) * D]
            em.stage("dve", lambda e, v=vkmr, mm=mm: e.tensor_copy(
                v[:], ps["P1"][:, 512:1536]), reads=["P1"], writes=[f"vkm{mm % 4}"])

    def emit_chain(m, which):
        us = m % 2
        rot = T["rot"][:, us * 256:(us + 1) * 256]
        sk = "qA" if which == "q" else "kA"
        skkey = f"{sk}{us}"
        skeys = [skkey] + ([f"kAlo{us}"] if which == "k" else [])
        dst = "qD" if which == "q" else "kD"
        sct = "qs_s" if which == "q" else "ks_s"
        coff = 0 if which == "q" else 128
        soff = 64 if which == "q" else 192
        src_t = T[sk][:, us * D:(us + 1) * D]
        seg = src_t.rearrange("p (h d) -> p h d", h=H)

        em.stage("pool", lambda e, s=src_t: e.tensor_mul(T["t3"][:], s, s),
                 reads=skeys, writes=["t3"])

        def fr(e, bump):
            i = e.reduce_sum(T["ssum"][:].unsqueeze(2),
                             T["t3"][:].rearrange("p (h d) -> p h d", h=H), AX.X)
            bump(i)
            return e.tensor_scalar_max(T["ssum"][:], T["ssum"][:], 1e-24)
        em.stage("dve", fr, reads=["t3"], writes=["ssum"], steps=2)

        def fe(e, bump):
            i = e.activation(T["tt"][:], T["ssum"][:], AF.Ln)
            bump(i)
            return e.activation(T["rn"][:], T["tt"][:], AF.Exp, scale=-0.5)
        em.stage("act", fe, reads=["ssum"], writes=["rn"], steps=2)

        def fm1(e, seg=seg):
            return e.tensor_tensor(
                T["t1"][:].rearrange("p (h d) -> p h d", h=H), seg,
                T["rn"][:].unsqueeze(2).broadcast_to([128, H, DH]), ALU.mult)
        em.stage("dve", fm1, reads=skeys + ["rn"], writes=["t1"])

        def fm2(e, bump, sct=sct, coff=coff, rot=rot):
            t1s = T["t1"][:].rearrange("p (h d) -> p h d", h=H)
            i = e.tensor_tensor(t1s, t1s,
                                T[sct][:].unsqueeze(1).broadcast_to([128, H, DH]),
                                ALU.mult)
            bump(i)
            return e.tensor_tensor(
                T["t2"][:].rearrange("p (h d) -> p h d", h=H), t1s,
                rot[:, coff:coff + 64].unsqueeze(1).broadcast_to([128, H, DH]),
                ALU.mult)
        em.stage("pool", fm2, reads=["t1", sct, f"rot{us}"],
                 writes=["t1", "t2"], steps=2)

        def fm3(e, bump, dst=dst, soff=soff, rot=rot):
            sg = T[dst][:].rearrange("p (h d) -> p h d", h=H)
            t1s = T["t1"][:].rearrange("p (h d) -> p h d", h=H)
            t2s = T["t2"][:].rearrange("p (h d) -> p h d", h=H)
            t3s = T["t3"][:].rearrange("p (h d) -> p h d", h=H)
            e.tensor_tensor(t3s[:, :, 0:32], t1s[:, :, 32:64],
                            rot[:, soff:soff + 32].unsqueeze(1).broadcast_to([128, H, 32]),
                            ALU.mult)
            i = e.tensor_tensor(t3s[:, :, 32:64], t1s[:, :, 0:32],
                                rot[:, soff + 32:soff + 64].unsqueeze(1).broadcast_to([128, H, 32]),
                                ALU.mult)
            bump(i)
            return e.tensor_add(sg, t2s, t3s)
        em.stage("dve", fm3, reads=["t1", "t2", f"rot{us}"],
                 writes=[dst, "t3"], steps=2)

    for m in range(kv0, NS):
        yslot = (m - G) % 8
        if m == kv0:
            for mm in (m - 1, m, m + 1, m + 2):
                load_y(mm)
            build_u(m)
            emit_rot(m)
            emit_qkv_part(m, "q")
            emit_qkv_part(m, "kv")
        else:
            load_y(m + 2)
        build_u(m + 1)
        emit_rot(m + 1)

        uslot = m % 2
        kfmr = T["kfm"][:, (m % 4) * D:((m % 4) + 1) * D]

        emit_qkv_part(m + 1, "q")
        emit_chain(m, "q")
        emit_qkv_part(m + 1, "kv")
        emit_chain(m, "k")

        _p_transpose8(st, T["qD"][:], "qD", "trE")
        em.stage("dve", lambda e: e.tensor_copy(T["qfm"][:], ps["trE"][:]),
                 reads=["trE"], writes=["qfm"])
        _p_transpose8(st, T["kD"][:], "kD", "trF")
        em.stage("act", lambda e, k=kfmr: e.copy(k[:], ps["trF"][:]),
                 reads=["trF"], writes=[f"kfm{m % 4}"])

        if m < q0:
            continue

        for w in range(4):
            pkey = "P1" if w % 2 == 0 else "P2"
            pw = ps[pkey]
            wslot = w % 2
            kreads = ["qfm"] + [f"kfm{(m - 2 + kc) % 4}" for kc in range(3)]

            def fsim(e, w=w, pw=pw, m=m):
                km0 = (T["m0_s"] if m == OWN0 else
                       (T["m2_s"] if m == OWN0 + 1 else T["maskA_s"]))
                km2 = T["maskC_s"]
                for hi in range(4):
                    h = 4 * w + hi
                    pb = (h % 2) * 64
                    pc = (h // 2) * 128

                    def kv(kc):
                        return T["kfm"][:, ((m - 2 + kc) % 4) * D:
                                        (((m - 2 + kc) % 4) + 1) * D][pb:pb + 64,
                                                                     pc:pc + 128]
                    qv = T["qfm"][pb:pb + 64, pc:pc + 128]
                    e.matmul(pw[:, hi * 384: hi * 384 + 128], qv, kv(0),
                             start=True, stop=False)
                    e.matmul(pw[:, hi * 384: hi * 384 + 128], km0[:], T["identb_s"][:],
                             start=False, stop=True)
                    if m == OWN0:
                        e.matmul(pw[:, hi * 384 + 128: hi * 384 + 256], qv, kv(1),
                                 start=True, stop=False)
                        e.matmul(pw[:, hi * 384 + 128: hi * 384 + 256], T["m1_s"][:],
                                 T["identb_s"][:], start=False, stop=True)
                    else:
                        e.matmul(pw[:, hi * 384 + 128: hi * 384 + 256], qv, kv(1),
                                 start=True, stop=True)
                    e.matmul(pw[:, hi * 384 + 256: hi * 384 + 384], qv, kv(2),
                             start=True, stop=False)
                    i = e.matmul(pw[:, hi * 384 + 256: hi * 384 + 384], km2[:],
                                 T["identb_s"][:], start=False, stop=True)
                return i
            em.stage("pe", fsim, reads=kreads + ["maskA_s", "maskC_s", "m0_s", "m1_s",
                                                 "m2_s", "identb_s"], writes=[pkey])

            em.stage("dve", lambda e, w=w, pw=pw: e.reduce_max(
                T["nmx"][:, 4 * w:4 * w + 4].unsqueeze(2),
                pw[:, 0:1536].rearrange("p (h x) -> p h x", h=4), AX.X, negate=True),
                reads=[pkey], writes=["nmx"])

            def fexp(e, w=w, pw=pw, wslot=wslot):
                for hi in range(4):
                    i = e.activation(T["Pt"][:, wslot * 1536:(wslot + 1) * 1536][:, hi * 384:(hi + 1) * 384],
                                     pw[:, hi * 384:(hi + 1) * 384], AF.Exp,
                                     bias=T["nmx"][:, 4 * w + hi:4 * w + hi + 1])
                return i
            em.stage("act", fexp, reads=[pkey, "nmx"], writes=[f"Pt{wslot}"])

            for half in range(2):
                tk = "trE" if half == 0 else "trF"

                def ftr(e, half=half, wslot=wslot, tk=tk):
                    for j in range(6):
                        i = e.transpose(ps[tk][:, j * 128:(j + 1) * 128],
                                        T["Pt"][:, wslot * 1536:(wslot + 1) * 1536][:, half * 768 + j * 128:
                                                       half * 768 + (j + 1) * 128],
                                        T["identb_s"][:])
                    return i
                em.stage("pe", ftr, reads=[f"Pt{wslot}", "identb_s"], writes=[tk])
                em.stage("act", lambda e, half=half, wslot=wslot, tk=tk: e.copy(
                    T["ptsb"][:, wslot * 1536:(wslot + 1) * 1536][:, half * 768:(half + 1) * 768], ps[tk][:, 0:768]),
                    reads=[tk], writes=[f"ptsb{wslot}"])

            vreads = [f"ptsb{wslot}"] + [f"vkm{(m - 2 + kc) % 4}" for kc in range(3)]

            def fav(e, w=w, pw=pw, wslot=wslot, m=m):
                for hi in range(4):
                    for kc in range(3):
                        pt = T["ptsb"][:, wslot * 1536:(wslot + 1) * 1536][:, (hi * 3 + kc) * 128:(hi * 3 + kc + 1) * 128]
                        vk = T["vkm"][:, ((m - 2 + kc) % 4) * D:(((m - 2 + kc) % 4) + 1) * D][:].rearrange(
                            "p (h d) -> p h d", h=H)[:, 4 * w + hi]
                        e.matmul(pw[:, hi * 64:(hi + 1) * 64], pt, vk,
                                 start=(kc == 0), stop=(kc == 2))
                    for kc in range(3):
                        pt = T["ptsb"][:, wslot * 1536:(wslot + 1) * 1536][:, (hi * 3 + kc) * 128:(hi * 3 + kc + 1) * 128]
                        i = e.matmul(pw[:, 256 + hi:256 + hi + 1], pt, T["ones_c"][:],
                                     start=(kc == 0), stop=(kc == 2))
                return i
            em.stage("pe", fav, reads=vreads + ["ones_c"], writes=[pkey])

            def fdiv(e, bump, w=w, pw=pw):
                i = e.reciprocal(T["rsb"][:, 4 * w:4 * w + 4], pw[:, 256:260])
                bump(i)
                return e.tensor_tensor(
                    T["akm"][:].rearrange("p (h d) -> p h d", h=H)[:, 4 * w:4 * w + 4],
                    pw[:, 0:256].rearrange("p (h d) -> p h d", h=4),
                    T["rsb"][:, 4 * w:4 * w + 4].unsqueeze(2).broadcast_to([128, 4, DH]),
                    ALU.mult)
            em.stage("dve", fdiv, reads=[pkey], writes=["akm", "rsb"], steps=2)

        if st.get("dbg_sub") == m:
            em.stage("pool", lambda e: e.dma_start(out=dram["dakm"].ap(), in_=T["akm"][:]),
                     reads=["akm"], dma=True)
            em.stage("pool", lambda e: e.dma_start(out=dram["dqfm"].ap(), in_=T["qfm"][:]),
                     reads=["qfm"], dma=True)
            for kc in range(3):
                sl = (m - 2 + kc) % 4
                em.stage("pool", lambda e, kc=kc, sl=sl: e.dma_start(
                    out=dram["dkfm"].ap()[:, kc * D:(kc + 1) * D],
                    in_=T["kfm"][:, sl * D:(sl + 1) * D]),
                    reads=[f"kfm{sl}"], dma=True)
                em.stage("pool", lambda e, kc=kc, sl=sl: e.dma_start(
                    out=dram["dvkm"].ap()[:, kc * D:(kc + 1) * D],
                    in_=T["vkm"][:, sl * D:(sl + 1) * D]),
                    reads=[f"vkm{sl}"], dma=True)
        gi = (m - G) % 4
        _p_transpose8(st, T["akm"][:], "akm", "trF")
        em.stage("dve", lambda e, gi=gi: e.tensor_copy(
            T["afm"][:].rearrange("p (c g t) -> p c g t", c=NCH, g=4)[:, :, gi],
            ps["trF"][:].rearrange("p (c t) -> p c t", c=NCH)),
            reads=["trF"], writes=["afm"])

        if m == q0 or (m - G) % 4 == 3 or m == NS - 1:
            ng = 1 if m == q0 else (m - G) % 4 + 1
            gi0 = (m - ng + 1 - G) % 4
            af = T["afm"][:].rearrange("p (c t) -> p c t", c=NCH)
            g0s = (m - ng + 1 - G) % 8
            yrd = [f"yt{(g0s + j) % 8}" for j in range(ng)]
            for op in range(4):
                pk = "P1" if op % 2 == 0 else "P2"

                def fwo(e, op=op, pk=pk, ng=ng, af=af, gi0=gi0):
                    for oc in (2 * op, 2 * op + 1):
                        half = oc % 2
                        for ic in range(NCH):
                            i = e.matmul(ps[pk][:, half * 512: half * 512 + ng * 128],
                                         A[:, WO + ic * 1024 + oc * 128:
                                            WO + ic * 1024 + (oc + 1) * 128],
                                         af[:, ic, gi0 * 128:(gi0 + ng) * 128],
                                         start=(ic == 0), stop=(ic == NCH - 1))
                    return i
                em.stage("pe", fwo, reads=["afm", "slotW2"], writes=[pk])

                def fadd(e, op=op, pk=pk, ng=ng, g0s=g0s):
                    for oc in (2 * op, 2 * op + 1):
                        half = oc % 2
                        i = e.tensor_tensor(
                            yt[:, g0s:g0s + ng, oc],
                            ps[pk][:, half * 512: half * 512 + ng * 128]
                            .rearrange("p (s t) -> p s t", s=ng),
                            yt[:, g0s:g0s + ng, oc],
                            ALU.add)
                    return i
                em.stage("dve", fadd, reads=[pk] + yrd, writes=yrd)

            for j in range(ng):
                mm = m - ng + 1 + j
                sl = (g0s + j) % 8
                em.stage("sy", lambda e, mm=mm, sl=sl: e.dma_start(
                    out=ydst[mm].rearrange("c p t -> p c t"), in_=yt[:, sl]),
                    reads=yrd, writes=[f"ymid_{l}_{mm}"], dma=True)

    if st.get("bffn", True):
        _p_bffn(st, l, ydst)


def _p_bffn(st, l, ydst):
    em, T, ps, dram = st["em"], st["T"], st["ps"], st["dram"]
    em.barrier()
    G = 3 * l + 1
    A = T["arena"]
    W1, W2 = 0, 45056
    w1, w2 = dram["W1p"].ap()[l - 1], dram["W2p"].ap()[l - 1]

    for c in range(NCH):
        em.stage("pool", lambda e, c=c: e.dma_start(
            out=A[:, W1 + c * 5632: W1 + (c + 1) * 5632], in_=w1[ts(c), :]),
            writes=["slotW1"] if c == 0 else [], accw=["slotW1"] if c > 0 else [],
            dma=True)
    for c in range(NIC):
        em.stage("pool", lambda e, c=c: e.dma_start(
            out=A[:, W2 + c * 1024: W2 + (c + 1) * 1024], in_=w2[ts(c), :]),
            writes=["slotW2"] if c == 0 else [], accw=["slotW2"] if c > 0 else [],
            dma=True)

    for seg in range(11):
        pk, pk2 = ("P1", "P2") if seg % 2 == 0 else ("P2", "P1")

        def fseg(e, seg=seg, pk=pk):
            for c in range(NCH):
                i = e.matmul(ps[pk][0:1, 0:512], T["lnb_s"][:, c:c + 1],
                             A[:, W1 + c * 5632 + seg * 512: W1 + c * 5632 + (seg + 1) * 512],
                             start=(c == 0), stop=(c == NCH - 1))
            return i
        em.stage("pe", fseg, reads=["lnb_s", "slotW1"], writes=[pk])
        em.stage("act", lambda e, pk=pk: e.copy(T["mrow"][0:1, 0:512],
                                                ps[pk][0:1, 0:512]),
                 reads=[pk], writes=["mrow"])

        def fsp2(e, pk2=pk2):
            for u in range(4):
                i = e.matmul(ps[pk2][:, u:u + 1],
                             T["mrow"][0:1, u * 128:(u + 1) * 128],
                             T["ones_c"][0:1, 0:1], start=True, stop=True)
            return i
        em.stage("pe", fsp2, reads=["mrow", "ones_c"], writes=[pk2])
        em.stage("dve", lambda e, seg=seg, pk2=pk2: e.tensor_copy(
            T["cb1_s"][:, seg * 4:(seg * 4) + 4], ps[pk2][:, 0:4]),
            reads=[pk2], writes=["cb1_s"] if seg == 10 else [])
    em.stage("dve", lambda e: e.tensor_add(T["cb1_s"][:], T["cb1_s"][:], T["b1p_s"][:]),
             reads=["cb1_s", "b1p_s"], writes=["cb1_s"])

    TG = 3
    ngroups = (NS - G + TG - 1) // TG

    def grp(jg):
        s0 = G + jg * TG
        nsub = min(TG, NS - s0)
        return s0, nsub, nsub * 128, jg % 2

    def emit_u2_loads(jg):
        s0, nsub, TC, gslot = grp(jg)
        u2 = T["u2g"][:].rearrange("p (c t) -> p c t", c=NCH, t=384)
        ym = T["ymg"][:, gslot * (NCH * 384):(gslot + 1) * (NCH * 384)].rearrange(
            "p (c t) -> p c t", c=NCH, t=384)
        ymk = [f"ymid_{l}_{s0 + j}" for j in range(nsub)]
        for c4 in range(4):
            em.stage("pool", lambda e, s0=s0, nsub=nsub, c4=c4, u2=u2: e.dma_start(
                out=u2[:, c4, 0:nsub * 128].rearrange("p (s t) -> p s t", s=nsub),
                in_=ydst[s0:s0 + nsub, c4].rearrange("s p t -> p s t")),
                reads=ymk, writes=["u2g"] if c4 == 0 else [],
                accw=["u2g"] if c4 > 0 else [], dma=True)
        for c4 in range(4, 8):
            em.stage("pool", lambda e, s0=s0, nsub=nsub, c4=c4, u2=u2: e.dma_start(
                out=u2[:, c4, 0:nsub * 128].rearrange("p (s t) -> p s t", s=nsub)[:, :, 1:128],
                in_=ydst[s0:s0 + nsub, c4].rearrange("s p t -> p s t")[:, :, 0:127]),
                reads=ymk, accw=["u2g"], dma=True)
        if nsub > 1:
            for c4 in range(4, 8):
                em.stage("pool", lambda e, s0=s0, nsub=nsub, c4=c4, u2=u2: e.dma_start(
                    out=u2[:, c4, 128:nsub * 128].rearrange(
                        "p (s t) -> p s t", s=nsub - 1)[:, :, 0:1],
                    in_=ydst[s0:s0 + nsub - 1, c4].rearrange(
                        "s p t -> p s t")[:, :, 127:128],
                    allow_slow_non_contiguous=True),
                    reads=ymk, accw=["u2g"], dma=True)
        if jg == 0:
            def fu2d(e, s0=s0, u2=u2):
                return e.dma_start(
                    out=u2[:, 4:8, 0:1],
                    in_=ydst[s0 - 1, 4:8].rearrange("c p t -> p c t")[:, :, 127:128],
                    allow_slow_non_contiguous=True)
            em.stage("pool", fu2d, reads=[f"ymid_{l}_{s0 - 1}"], accw=["u2g"],
                     dma=True)
        else:
            em.stage("pool", lambda e, u2=u2: e.tensor_copy(
                u2[:, 4:8, 0:1], T["ycar"][:].unsqueeze(2)),
                reads=["ycar"], accw=["u2g"])
        if s0 <= OWN0 < s0 + nsub:
            o = (OWN0 - s0) * 128
            em.stage("pool", lambda e, o=o, u2=u2: e.tensor_scalar_mul(
                u2[:, 4:8, o:o + 1], u2[:, 4:8, o:o + 1],
                T["hval_s"][:, OWN0:OWN0 + 1]),
                reads=["u2g", "hval_s"], writes=["u2g"])
        for c8 in range(NCH):
            em.stage("sy", lambda e, s0=s0, nsub=nsub, ym=ym, c8=c8: e.dma_start(
                out=ym[:, c8, 0:nsub * 128].rearrange("p (s t) -> p s t", s=nsub),
                in_=ydst[s0:s0 + nsub, c8].rearrange("s p t -> p s t")),
                reads=ymk, writes=[f"ymg{gslot}"] if c8 == 0 else [],
                accw=[f"ymg{gslot}"] if c8 > 0 else [], dma=True)
        em.stage("pool", lambda e, ym=ym, TC=TC: e.tensor_copy(
            T["ycar"][:].unsqueeze(2), ym[:, 4:8, TC - 1:TC]),
            reads=[f"ymg{gslot}"], writes=["ycar"])

    def emit_stats_u2rg(jg):
        s0, nsub, TC, gslot = grp(jg)
        u2 = T["u2g"][:].rearrange("p (c t) -> p c t", c=NCH, t=384)

        def fst1(e, TC=TC, u2=u2):
            for c in range(NCH):
                i = e.matmul(ps["P2"][0:1, 0:TC], T["ones_c"][:], u2[:, c, 0:TC],
                             start=(c == 0), stop=(c == NCH - 1))
            return i
        em.stage("pe", fst1, reads=["u2g", "ones_c"], writes=["P2"])
        for c in range(NCH):
            sq = T["sqt"][:, (c % 2) * 384:((c % 2) + 1) * 384]
            em.stage("pool", lambda e, c=c, sq=sq, TC=TC, u2=u2: e.tensor_mul(
                sq[:, 0:TC], u2[:, c, 0:TC], u2[:, c, 0:TC]),
                reads=["u2g"], writes=[f"sqt{c % 2}"])
            em.stage("pe", lambda e, c=c, sq=sq, TC=TC: e.matmul(
                ps["P2"][0:1, 512:512 + TC], T["ones_c"][:], sq[:, 0:TC],
                start=(c == 0), stop=(c == NCH - 1)),
                reads=[f"sqt{c % 2}", "ones_c"], writes=["P2"])

        def fst2(e, bump, TC=TC):
            e.tensor_scalar_mul(T["trow"][0:1, 0:TC], ps["P2"][0:1, 0:TC], 1.0 / 1024.0)
            i = e.tensor_scalar_mul(T["trow"][0:1, 512:512 + TC],
                                    ps["P2"][0:1, 512:512 + TC], 1.0 / 1024.0)
            bump(i)
            i = e.tensor_mul(T["trow2"][0:1, 0:TC], T["trow"][0:1, 0:TC],
                             T["trow"][0:1, 0:TC])
            bump(i)
            e.tensor_sub(T["trow2"][0:1, 512:512 + TC], T["trow"][0:1, 512:512 + TC],
                         T["trow2"][0:1, 0:TC])
            return e.tensor_copy(T["mrow"][0:1, 0:TC], T["trow"][0:1, 0:TC])
        em.stage("dve", fst2, reads=["P2"], writes=["trow", "mrow"], steps=3)

        def fst3(e, bump, TC=TC):
            i = e.activation(T["trow2"][0:1, 0:TC], T["trow2"][0:1, 512:512 + TC],
                             AF.Ln, bias=T["ceps"][0:1, :])
            bump(i)
            return e.activation(T["rrow"][0:1, 0:TC], T["trow2"][0:1, 0:TC], AF.Exp,
                                scale=-0.5)
        em.stage("act", fst3, reads=["trow", "ceps"], writes=["rrow"], steps=2)

        def fbc(e, TC=TC):
            e.matmul(ps["P2"][:, 0:TC], T["ones_r"][:], T["mrow"][0:1, 0:TC],
                     start=True, stop=True)
            return e.matmul(ps["P2"][:, 512:512 + TC], T["ones_r"][:],
                            T["rrow"][0:1, 0:TC], start=True, stop=True)
        em.stage("pe", fbc, reads=["mrow", "rrow", "ones_r"], writes=["P2"])

        def fu2rg(e, bump, TC=TC, u2=u2):
            for c in range(NCH):
                i = e.tensor_sub(T["tf1"][:, 0:TC], u2[:, c, 0:TC], ps["P2"][:, 0:TC])
                bump(i)
                i = e.scalar_tensor_tensor(u2[:, c, 0:TC], T["tf1"][:, 0:TC],
                                           T["lng_s"][:, c:c + 1],
                                           ps["P2"][:, 512:512 + TC], ALU.mult, ALU.mult)
                if c < NCH - 1:
                    bump(i)
            return i
        em.stage("dve", fu2rg, reads=["P2", "u2g", "lng_s"], writes=["u2g", "tf1"],
                 steps=16)

    def emit_w1(jg):
        s0, nsub, TC, gslot = grp(jg)
        u2 = T["u2g"][:].rearrange("p (c t) -> p c t", c=NCH, t=384)
        fv = T["fsb"][:].rearrange("p (k t) -> p k t", k=NIC, t=384)
        for pc in range(NIC):
            pk = "P1" if pc % 2 == 0 else "P2"
            pw = ps[pk]

            def fw1(e, pc=pc, pw=pw, TC=TC, u2=u2):
                for c in range(NCH):
                    e.matmul(pw[:, 0:TC],
                             A[:, W1 + c * 5632 + 2 * pc * 128: W1 + c * 5632 + (2 * pc + 1) * 128],
                             u2[:, c, 0:TC], start=(c == 0), stop=(c == NCH - 1))
                for c in range(NCH):
                    i = e.matmul(pw[:, 512:512 + TC],
                                 A[:, W1 + c * 5632 + (2 * pc + 1) * 128: W1 + c * 5632 + (2 * pc + 2) * 128],
                                 u2[:, c, 0:TC], start=(c == 0), stop=(c == NCH - 1))
                return i
            em.stage("pe", fw1, reads=["u2g", "slotW1"], writes=[pk])

            em.stage("act", lambda e, pc=pc, pw=pw, TC=TC: e.activation(
                T["tf2"][:, 0:TC], pw[:, 512:512 + TC], AF.Gelu,
                bias=T["cb1_s"][:, 2 * pc + 1:2 * pc + 2]),
                reads=[pk, "cb1_s"], writes=["tf2"])

            em.stage("dve", lambda e, pc=pc, pw=pw, TC=TC, fv=fv: e.scalar_tensor_tensor(
                fv[:, pc, 0:TC], pw[:, 0:TC], T["cb1_s"][:, 2 * pc:2 * pc + 1],
                T["tf2"][:, 0:TC], ALU.add, ALU.mult),
                reads=[pk, "tf2", "cb1_s"], writes=["fsb"])

    def emit_w2(jg):
        s0, nsub, TC, gslot = grp(jg)
        ym = T["ymg"][:, gslot * (NCH * 384):(gslot + 1) * (NCH * 384)].rearrange(
            "p (c t) -> p c t", c=NCH, t=384)
        fv = T["fsb"][:].rearrange("p (k t) -> p k t", k=NIC, t=384)
        for oc in range(NCH):
            def fw2(e, oc=oc, TC=TC, fv=fv):
                for k in range(NIC):
                    i = e.matmul(ps["P1"][:, 0:TC],
                                 A[:, W2 + k * 1024 + oc * 128: W2 + k * 1024 + (oc + 1) * 128],
                                 fv[:, k, 0:TC], start=(k == 0), stop=(k == NIC - 1))
                return i
            em.stage("pe", fw2, reads=["fsb", "slotW2"], writes=["P1"])
            em.stage("dve", lambda e, oc=oc, TC=TC, ym=ym: e.scalar_tensor_tensor(
                ym[:, oc, 0:TC], ps["P1"][:, 0:TC], T["b2r_s"][:, oc:oc + 1],
                ym[:, oc, 0:TC], ALU.add, ALU.add),
                reads=["P1", f"ymg{gslot}", "b2r_s"], writes=[f"ymg{gslot}"])
        wkeys = [f"ysrc_{l + 1}_{s0 + j}" for j in range(nsub)]
        for c8 in range(NCH):
            em.stage("sy", lambda e, s0=s0, nsub=nsub, ym=ym, c8=c8: e.dma_start(
                out=ydst[s0:s0 + nsub, c8].rearrange("s p t -> p s t"),
                in_=ym[:, c8, 0:nsub * 128].rearrange("p (s t) -> p s t", s=nsub)),
                reads=[f"ymg{gslot}"], writes=wkeys if c8 == 0 else [],
                accw=wkeys if c8 > 0 else [], dma=True)

    emit_u2_loads(0)
    emit_stats_u2rg(0)
    for jg in range(ngroups):
        emit_w1(jg)
        if jg + 1 < ngroups:
            emit_u2_loads(jg + 1)
            emit_stats_u2rg(jg + 1)
        emit_w2(jg)


def _p_abranch(st, nlayers):
    em, T, ps, dram = st["em"], st["T"], st["ps"], st["dram"]
    em.barrier()
    A = T["arena"]
    WQ, WK, WV, WOA = 0, 8192, 16384, 45056
    xin = dram["xin"].ap()

    first = True
    for nm, off in (("Wq", WQ), ("Wk", WK), ("Wv", WV)):
        for c in range(NCH):
            em.stage("pool", lambda e, nm=nm, off=off, c=c: e.dma_start(
                out=A[:, off + c * 1024: off + (c + 1) * 1024],
                in_=dram[nm].ap()[ts(c), :]),
                writes=["slotW1"] if first else [],
                accw=[] if first else ["slotW1"], dma=True)
            first = False
    for c in range(NCH):
        em.stage("pool", lambda e, c=c: e.dma_start(
            out=A[:, WOA + c * 1024: WOA + (c + 1) * 1024],
            in_=dram["WoA"].ap()[ts(c), :]),
            writes=["slotW2"] if c == 0 else [], accw=["slotW2"] if c > 0 else [],
            dma=True)

    for dst, srcn in (("g1t_s", "g1t"), ("b1t_s", "b1t")):
        em.stage("sy", lambda e, d=dst, s=srcn: e.dma_start(
            out=T[d][:], in_=dram[s].ap()), writes=[dst], dma=True)

    def fz(e):
        e.memset(T["Sacc"][:], 0.0)
        e.memset(T["zacc"][:], 0.0)
        e.memset(T["SbD"][:], 0.0)
        return e.memset(T["zbD"][:], 0.0)
    em.stage("dve", fz, writes=["Sacc", "SbD"])

    for j in range(NPS + 32):
        _p_asub(st, j, j >= NPS)

    _p_affn(st, nlayers)


def _p_asub(st, j, own):
    em, T, ps, dram = st["em"], st["T"], st["ps"], st["dram"]
    A = T["arena"]
    WQ, WK, WV, WOA = 0, 8192, 16384, 45056
    xin = dram["xin"].ap()
    slot = j % 2

    em.stage("sy", lambda e, j=j, slot=slot: e.dma_start(
        out=T["xld"][:, slot * D:(slot + 1) * D], in_=xin[ts(j), :]), writes=[f"xld{slot}"], dma=True)
    _p_ln(st, T["xld"][:, slot * D:(slot + 1) * D][:, :], T["xnt"][:, :], f"xld{slot}", "xnt")
    _p_ln(st, T["xnt"][:, :], T["t1"][:, :], "xnt", "h1a",
          g=T["g1t_s"][:], b=T["b1t_s"][:])
    em.stage("dve", lambda e: e.tensor_copy(T["h1b"][:], T["t1"][:]),
             reads=["h1a"], writes=["h1b"])
    _p_transpose8(st, T["h1b"][:], "h1b", "trE")
    em.stage("dve", lambda e: e.tensor_copy(T["h1fm"][:], ps["trE"][:]),
             reads=["trE"], writes=["h1fm"])
    h1f = T["h1fm"][:].rearrange("p (c t) -> p c t", c=NCH)

    if own:
        _p_transpose8(st, T["xnt"][:], "xnt", "P1", dt="f")
        em.stage("dve", lambda e: e.tensor_copy(T["xnfm"][:], ps["P1"][:, 0:1024]),
                 reads=["P1"], writes=["xnfm"])

    def fkv(e):
        for half in range(2):
            for c in range(NCH):
                e.matmul(ps["P1"][:, half * 512:(half + 1) * 512], h1f[:, c],
                         A[:, WK + c * 1024 + half * 512: WK + c * 1024 + (half + 1) * 512],
                         start=(c == 0), stop=(c == NCH - 1))
        for half in range(2):
            for c in range(NCH):
                i = e.matmul(ps["P2"][:, half * 512:(half + 1) * 512], h1f[:, c],
                             A[:, WV + c * 1024 + half * 512: WV + c * 1024 + (half + 1) * 512],
                             start=(c == 0), stop=(c == NCH - 1))
        return i
    em.stage("pe", fkv, reads=["h1fm", "slotW1"], writes=["P1", "P2"])

    def fp1(e):
        e.tensor_scalar_min(T["t1"][:], ps["P1"][:, 0:1024], 0.0)
        return e.tensor_scalar_add(T["t2"][:], ps["P1"][:, 0:1024], 1.0)
    em.stage("dve", fp1, reads=["P1"], writes=["t1", "t2"])
    em.stage("act", lambda e: e.activation(T["t1"][:], T["t1"][:], AF.Exp),
             reads=["t1"], writes=["t1"])

    def fp2(e, bump, j=j, own=own):
        i = e.tensor_max(T["t3"][:], T["t1"][:], T["t2"][:])
        bump(i)
        if not own:
            return e.tensor_scalar_mul(T["Kkm"][:], T["t3"][:], T["pval_s"][:, j:j + 1])
        return e.tensor_copy(T["Kkm"][:], T["t3"][:])
    em.stage("dve", fp2, reads=["t1", "t2", "pval_s"], writes=["Kkm", "t3"], steps=2)
    em.stage("act", lambda e: e.copy(T["Vkm2"][:], ps["P2"][:, 0:1024]),
             reads=["P2"], writes=["Vkm2"])

    if own:
        def fq(e):
            for pr in range(NCH):
                for c in range(NCH):
                    i = e.matmul(ps["P1"][:, ts(pr)],
                                 A[:, WQ + c * 1024 + pr * 128: WQ + c * 1024 + (pr + 1) * 128],
                                 h1f[:, c], start=(c == 0), stop=(c == NCH - 1))
            return i
        em.stage("pe", fq, reads=["h1fm", "slotW1"], writes=["P1"])

        def fq1(e):
            e.tensor_scalar_min(T["t1"][:], ps["P1"][:, 0:1024], 0.0)
            return e.tensor_scalar_add(T["t2"][:], ps["P1"][:, 0:1024], 1.0)
        em.stage("dve", fq1, reads=["P1"], writes=["t1", "t2"])
        em.stage("act", lambda e: e.activation(T["t1"][:], T["t1"][:], AF.Exp),
                 reads=["t1"], writes=["t1"])
        em.stage("dve", lambda e: e.tensor_max(T["Qfm"][:], T["t1"][:], T["t2"][:]),
                 reads=["t1", "t2"], writes=["Qfm"])

        _p_transpose8(st, T["Kkm"][:], "Kkm", "trF")
        em.stage("act", lambda e: e.copy(T["Kfm"][:], ps["trF"][:]),
                 reads=["trF"], writes=["Kfm"])

        for h in range(H):
            pb = (h % 2) * 64
            pc = (h // 2) * 128
            asl = h % 2

            em.stage("pe", lambda e, pb=pb, pc=pc, asl=asl: e.matmul(
                ps["P2"][:, 1024 + asl * 128: 1152 + asl * 128],
                T["Kfm"][pb:pb + 64, pc:pc + 128], T["Qfm"][pb:pb + 64, pc:pc + 128],
                start=True, stop=True),
                reads=["Kfm", "Qfm"], writes=[f"ATp{asl}"])

            em.stage("dve", lambda e, asl=asl: e.tensor_tensor(
                T["ATs"][:, asl * 128:(asl + 1) * 128],
                ps["P2"][:, 1024 + asl * 128: 1152 + asl * 128], T["triA_s"][:],
                ALU.mult),
                reads=[f"ATp{asl}", "triA_s"], writes=[f"ATs{asl}"])

            def fnd(e, h=h, pb=pb, pc=pc, asl=asl):
                vh = T["Vkm2"][:].rearrange("p (h d) -> p h d", h=H)[:, h]
                e.matmul(ps["P1"][:, h * 64:(h + 1) * 64],
                         T["ATs"][:, asl * 128:(asl + 1) * 128], vh,
                         start=True, stop=False)
                e.matmul(ps["P1"][:, h * 64:(h + 1) * 64],
                         T["Qfm"][pb:pb + 64, pc:pc + 128],
                         T["SbD"][pb:pb + 64, h * 64:(h + 1) * 64],
                         start=False, stop=True)
                e.matmul(ps["P1"][:, 1024 + h:1024 + h + 1],
                         T["ATs"][:, asl * 128:(asl + 1) * 128], T["ones_c"][:],
                         start=True, stop=False)
                return e.matmul(ps["P1"][:, 1024 + h:1024 + h + 1],
                                T["Qfm"][pb:pb + 64, pc:pc + 128],
                                T["zbD"][pb:pb + 64, h:h + 1],
                                start=False, stop=True)
            em.stage("pe", fnd, reads=[f"ATs{asl}", "Vkm2", "SbD", "zbD", "ones_c",
                                       "Qfm"], writes=["P1"])

        def fdv(e, bump):
            i = e.tensor_scalar_max(T["den_t"][:], ps["P1"][:, 1024:1024 + H], 1e-6)
            bump(i)
            i = e.reciprocal(T["rA"][:], T["den_t"][:])
            bump(i)
            return e.tensor_tensor(
                T["aAkm"][:].rearrange("p (h d) -> p h d", h=H),
                ps["P1"][:, 0:1024].rearrange("p (h d) -> p h d", h=H),
                T["rA"][:].unsqueeze(2).broadcast_to([128, H, DH]), ALU.mult)
        em.stage("dve", fdv, reads=["P1"], writes=["aAkm", "rA", "den_t"], steps=3)

    def fsu(e):
        kk = T["Kkm"][:].rearrange("p (h d) -> p h d", h=H)
        vv = T["Vkm2"][:].rearrange("p (h d) -> p h d", h=H)
        for h in range(H):
            e.matmul(ps["P2"][0:64, h * 64:(h + 1) * 64], kk[:, h], vv[:, h],
                     start=True, stop=True)
            i = e.matmul(ps["P2"][0:64, 1024 + h:1024 + h + 1], kk[:, h],
                         T["ones_c"][:], start=True, stop=True)
        return i
    em.stage("pe", fsu, reads=["Kkm", "Vkm2", "ones_c"], writes=["P2"])

    def fsa(e, bump):
        e.tensor_add(T["Sacc"][:], T["Sacc"][:], ps["P2"][0:64, 0:1024])
        i = e.tensor_add(T["zacc"][:], T["zacc"][:], ps["P2"][0:64, 1024:1024 + H])
        bump(i)
        e.tensor_copy(T["SaccB"][:], T["Sacc"][:])
        return e.tensor_copy(T["zaccB"][:], T["zacc"][:])
    em.stage("dve", fsa, reads=["P2", "Sacc"], writes=["Sacc", "SaccB"], steps=2)

    if j + 1 >= NPS:
        def fdp(e):
            e.matmul(ps["P2"][:, 0:512], T["dupI_s"][:], T["SaccB"][:, 0:512],
                     start=True, stop=True)
            e.matmul(ps["P2"][:, 512:1024], T["dupI_s"][:], T["SaccB"][:, 512:1024],
                     start=True, stop=True)
            return e.matmul(ps["P2"][:, 1024:1024 + H], T["dupI_s"][:], T["zaccB"][:],
                            start=True, stop=True)
        em.stage("pe", fdp, reads=["SaccB", "dupI_s"], writes=["P2"])

        def fdc(e):
            e.tensor_copy(T["SbD"][:], ps["P2"][:, 0:1024])
            return e.tensor_copy(T["zbD"][:], ps["P2"][:, 1024:1024 + H])
        em.stage("dve", fdc, reads=["P2"], writes=["SbD", "zbD"])

    if own:
        if st.get("dbg_suba") == j - NPS:
            for nm, key in (("daA", "aAkm"), ("dQf", "Qfm"), ("dKf", "Kfm"),
                            ("dVk", "Vkm2"), ("dSb", "SbD"), ("dzb", "zbD")):
                em.stage("pool", lambda e, nm=nm, key=key: e.dma_start(
                    out=dram[nm].ap(), in_=T[key][:]), reads=[key], dma=True)
        oj = j - NPS
        gi = oj % 4
        _p_transpose8(st, T["aAkm"][:], "aAkm", "trF")
        em.stage("dve", lambda e, gi=gi: e.tensor_copy(
            T["aAfm"][:].rearrange("p (c g t) -> p c g t", c=NCH, g=4)[:, :, gi],
            ps["trF"][:].rearrange("p (c t) -> p c t", c=NCH)),
            reads=["trF"], writes=["aAfm"])
        em.stage("dve", lambda e, gi=gi: e.tensor_copy(
            T["xag"][:].rearrange("p (c g t) -> p c g t", c=NCH, g=4)[:, :, gi],
            T["xnfm"][:].rearrange("p (c t) -> p c t", c=NCH)),
            reads=["xnfm"], writes=["xag"])

        if gi == 3:
            af = T["aAfm"][:].rearrange("p (c t) -> p c t", c=NCH)
            xg = T["xag"][:].rearrange("p (c t) -> p c t", c=NCH)
            for op in range(4):
                pk = "P2" if op % 2 == 0 else "P1"

                def fwo(e, op=op, pk=pk, af=af):
                    for oc in (2 * op, 2 * op + 1):
                        half = oc % 2
                        for ic in range(NCH):
                            i = e.matmul(ps[pk][:, half * 512:(half + 1) * 512],
                                         A[:, WOA + ic * 1024 + oc * 128:
                                            WOA + ic * 1024 + (oc + 1) * 128],
                                         af[:, ic], start=(ic == 0), stop=(ic == NCH - 1))
                    return i
                em.stage("pe", fwo, reads=["aAfm", "slotW2"], writes=[pk])

                def fad(e, op=op, pk=pk, xg=xg):
                    for oc in (2 * op, 2 * op + 1):
                        half = oc % 2
                        i = e.tensor_add(xg[:, oc], ps[pk][:, half * 512:(half + 1) * 512],
                                         xg[:, oc])
                    return i
                em.stage("dve", fad, reads=[pk, "xag"], writes=["xag"])

            for c8 in range(NCH):
                em.stage("sy", lambda e, oj=oj, xg=xg, c8=c8: e.dma_start(
                    out=dram["xasc"].ap()[oj - 3:oj + 1, c8].rearrange("s p t -> p s t"),
                    in_=xg[:, c8].rearrange("p (s t) -> p s t", s=4)),
                    reads=["xag"], writes=[f"xasc{oj // 4}"] if c8 == 0 else [],
                    accw=[f"xasc{oj // 4}"] if c8 > 0 else [], dma=True)


def _p_affn(st, nlayers):
    em, T, ps, dram = st["em"], st["T"], st["ps"], st["dram"]
    em.barrier()
    A = T["arena"]
    WF1, WF2 = 0, NCH * DF
    yfin = dram["ysc0" if nlayers % 2 == 0 else "ysc1"].ap()
    xin = dram["xin"].ap()

    for c in range(NCH):
        em.stage("pool", lambda e, c=c: e.dma_start(
            out=A[:, WF1 + c * DF: WF1 + (c + 1) * DF], in_=dram["Wf1"].ap()[ts(c), :]),
            writes=["slotW1"] if c == 0 else [], accw=["slotW1"] if c > 0 else [],
            dma=True)
    for c in range(NOC):
        em.stage("pool", lambda e, c=c: e.dma_start(
            out=A[:, WF2 + c * 1024: WF2 + (c + 1) * 1024],
            in_=dram["Wf2"].ap()[ts(c), :]),
            writes=["slotW2"] if c == 0 else [], accw=["slotW2"] if c > 0 else [],
            dma=True)

    for seg in range(8):
        pk = "P1" if seg % 2 == 0 else "P2"

        def fseg(e, seg=seg, pk=pk):
            for c in range(NCH):
                i = e.matmul(ps[pk][0:1, 0:512], T["b2a_s"][:, c:c + 1],
                             A[:, WF1 + c * DF + seg * 512: WF1 + c * DF + (seg + 1) * 512],
                             start=(c == 0), stop=(c == NCH - 1))
            return i
        em.stage("pe", fseg, reads=["b2a_s", "slotW1"], writes=[pk])
        pk2 = "P2" if pk == "P1" else "P1"
        em.stage("act", lambda e, pk=pk: e.copy(T["mrow"][0:1, 0:512],
                                                ps[pk][0:1, 0:512]),
                 reads=[pk], writes=["mrow"])

        def fsp(e, pk2=pk2):
            for u in range(4):
                i = e.matmul(ps[pk2][:, u:u + 1],
                             T["mrow"][0:1, u * 128:(u + 1) * 128],
                             T["ones_c"][0:1, 0:1], start=True, stop=True)
            return i
        em.stage("pe", fsp, reads=["mrow", "ones_c"], writes=[pk2])
        em.stage("dve", lambda e, seg=seg, pk2=pk2: e.tensor_copy(
            T["cbA_s"][:, seg * 4:seg * 4 + 4], ps[pk2][:, 0:4]),
            reads=[pk2], writes=["cbA_s"] if seg == 7 else [])

    fva = T["fA"][:].rearrange("p (k t) -> p k t", k=NOC, t=128)
    TC = 128
    for s0 in range(32):
        xg2 = T["xag2"][:].rearrange("p (c t) -> p c t", c=NCH, t=128)
        ygv = T["yg"][:].rearrange("p (c t) -> p c t", c=NCH, t=128)

        em.stage("sy", lambda e, s0=s0, xg2=xg2: e.dma_start(
            out=xg2, in_=dram["xasc"].ap()[s0].rearrange("c p t -> p c t")),
            reads=[f"xasc{s0 // 4}"], writes=["xag2"], dma=True)

        em.stage("pool", lambda e, xg2=xg2: e.tensor_copy(
            T["xab"][:], xg2.rearrange("p c t -> p (c t)")),
            reads=["xag2"], writes=["xab"])
        xabv = T["xab"][:].rearrange("p (c t) -> p c t", c=NCH, t=128)

        def fst1(e, xabv=xabv):
            for c in range(NCH):
                i = e.matmul(ps["P2"][0:1, 0:TC], T["ones_c"][:], xabv[:, c, 0:TC],
                             start=(c == 0), stop=(c == NCH - 1))
            return i
        em.stage("pe", fst1, reads=["xab", "ones_c"], writes=["P2"])
        for c in range(NCH):
            sq = T["sqA"][:, (c % 2) * 128:((c % 2) + 1) * 128]
            em.stage("pool", lambda e, c=c, sq=sq, xabv=xabv: e.tensor_mul(
                sq[:, 0:TC], xabv[:, c, 0:TC], xabv[:, c, 0:TC]),
                reads=["xab"], writes=[f"sqt{c % 2}"])
            em.stage("pe", lambda e, c=c, sq=sq: e.matmul(
                ps["P2"][0:1, 512:512 + TC], T["ones_c"][:], sq[:, 0:TC],
                start=(c == 0), stop=(c == NCH - 1)),
                reads=[f"sqt{c % 2}", "ones_c"], writes=["P2"])

        def fst2(e, bump):
            e.tensor_scalar_mul(T["trow"][0:1, 0:TC], ps["P2"][0:1, 0:TC], 1.0 / 1024.0)
            i = e.tensor_scalar_mul(T["trow"][0:1, 512:512 + TC],
                                    ps["P2"][0:1, 512:512 + TC], 1.0 / 1024.0)
            bump(i)
            i = e.tensor_mul(T["trow2"][0:1, 0:TC], T["trow"][0:1, 0:TC],
                             T["trow"][0:1, 0:TC])
            bump(i)
            e.tensor_sub(T["trow2"][0:1, 512:512 + TC], T["trow"][0:1, 512:512 + TC],
                         T["trow2"][0:1, 0:TC])
            return e.tensor_copy(T["mrow"][0:1, 0:TC], T["trow"][0:1, 0:TC])
        em.stage("dve", fst2, reads=["P2"], writes=["trow", "mrow"], steps=3)

        def fst3(e, bump):
            i = e.activation(T["trow2"][0:1, 0:TC], T["trow2"][0:1, 512:512 + TC],
                             AF.Ln, bias=T["ceps"][0:1, :])
            bump(i)
            return e.activation(T["rrow"][0:1, 0:TC], T["trow2"][0:1, 0:TC], AF.Exp,
                                scale=-0.5)
        em.stage("act", fst3, reads=["trow", "ceps"], writes=["rrow"], steps=2)

        def fbc(e):
            e.matmul(ps["P2"][:, 0:TC], T["ones_r"][:], T["mrow"][0:1, 0:TC],
                     start=True, stop=True)
            return e.matmul(ps["P2"][:, 512:512 + TC], T["ones_r"][:],
                            T["rrow"][0:1, 0:TC], start=True, stop=True)
        em.stage("pe", fbc, reads=["mrow", "rrow", "ones_r"], writes=["P2"])

        def fxc(e, bump, xabv=xabv):
            xcv = T["xac"][:].rearrange("p (c t) -> p c t", c=NCH, t=128)
            for c in range(NCH):
                i = e.tensor_sub(T["t1"][:, 0:TC], xabv[:, c, 0:TC], ps["P2"][:, 0:TC])
                bump(i)
                i = e.scalar_tensor_tensor(xcv[:, c, 0:TC], T["t1"][:, 0:TC],
                                           T["g2r_s"][:, c:c + 1],
                                           ps["P2"][:, 512:512 + TC], ALU.mult, ALU.mult)
                if c < NCH - 1:
                    bump(i)
            return i
        em.stage("dve", fxc, reads=["P2", "xab", "g2r_s"], writes=["xac", "t1"],
                 steps=16)
        xcv = T["xac"][:].rearrange("p (c t) -> p c t", c=NCH, t=128)

        for oc in range(NOC):
            pk = "P1" if oc % 2 == 0 else "P2"

            def fh(e, oc=oc, pk=pk, xcv=xcv):
                for c in range(NCH):
                    i = e.matmul(ps[pk][:, 0:TC],
                                 A[:, WF1 + c * DF + oc * 128: WF1 + c * DF + (oc + 1) * 128],
                                 xcv[:, c, 0:TC], start=(c == 0), stop=(c == NCH - 1))
                return i
            em.stage("pe", fh, reads=["xac", "slotW1"], writes=[pk])
            em.stage("act", lambda e, oc=oc, pk=pk: e.activation(
                fva[:, oc, 0:TC], ps[pk][:, 0:TC], AF.Gelu,
                bias=T["cbA_s"][:, oc:oc + 1]),
                reads=[pk, "cbA_s"], writes=["fA"])

        for oc in range(NCH):
            pk = "P1" if oc % 2 == 0 else "P2"

            def f2(e, oc=oc, pk=pk):
                for k in range(NOC):
                    i = e.matmul(ps[pk][:, 0:TC],
                                 A[:, WF2 + k * 1024 + oc * 128: WF2 + k * 1024 + (oc + 1) * 128],
                                 fva[:, k, 0:TC], start=(k == 0), stop=(k == NOC - 1))
                return i
            em.stage("pe", f2, reads=["fA", "slotW2"], writes=[pk])
            em.stage("dve", lambda e, oc=oc, pk=pk, xg2=xg2: e.tensor_add(
                xg2[:, oc, 0:TC], ps[pk][:, 0:TC], xg2[:, oc, 0:TC]),
                reads=[pk, "xag2"], writes=["xag2"])

        em.stage("sy", lambda e, s0=s0, ygv=ygv: e.dma_start(
            out=ygv, in_=yfin[OWN0 + s0].rearrange("c p t -> p c t")),
            reads=[f"ysrc_{nlayers + 1}_{OWN0 + s0}"], writes=["yg"], dma=True)

        em.stage("act", lambda e: e.activation(
            T["yg"][:], T["yg"][:], AF.Silu),
            reads=["yg"], writes=["yg"])

        em.stage("pool", lambda e: e.tensor_mul(
            T["prodb"][:], T["xag2"][:], T["yg"][:]),
            reads=["xag2", "yg"], writes=["prodb"])

        pv = T["prodb"][:].rearrange("p (c t) -> p c t", c=NCH, t=128)

        def ftp(e, pv=pv):
            for c in range(NCH):
                i = e.transpose(ps["trE"][:, ts(c)], pv[:, c], T["identb_s"][:])
            return i
        em.stage("pe", ftp, reads=["prodb", "identb_s"], writes=["trE"])
        em.stage("dve", lambda e: e.tensor_copy(T["prkm"][:], ps["trE"][:]),
                 reads=["trE"], writes=["prkm"])

        em.stage("sy", lambda e, s0=s0: e.dma_start(
            out=T["xkm"][:], in_=xin[PREF + s0 * 128: PREF + (s0 + 1) * 128, :]),
            writes=["xkm"], dma=True)

        em.stage("dve", lambda e: e.tensor_add(
            T["xkm"][:], T["prkm"][:], T["xkm"][:]),
            reads=["prkm", "xkm"], writes=["xkm"])

        em.stage("sy", lambda e, s0=s0: e.dma_start(
            out=dram["out"].ap()[s0 * 128:(s0 + 1) * 128, :], in_=T["xkm"][:]),
            reads=["xkm"], dma=True)



_CACHE = {}


def _host_prep(inputs):
    x = np.ascontiguousarray(inputs["L0_attn_x"])
    f32 = np.float32

    ii = np.arange(128)
    mA = np.where(ii[:, None] > ii[None, :], -BIG, 0).astype(f32)
    mC = np.where(ii[:, None] < ii[None, :], -BIG, 0).astype(f32)
    allm = np.full((128, 128), -BIG, f32)
    zer = np.zeros((128, 128), f32)
    triA = (ii[:, None] <= ii[None, :]).astype(f32)
    identf = np.eye(128, dtype=f32)
    dupI = np.concatenate([np.eye(64, dtype=f32), np.eye(64, dtype=f32)], axis=1)

    half = np.arange(0, DH, 2, dtype=f32)
    inv_freq = (f32(1.0) / f32(10000.0) ** (half / f32(DH))).astype(f32)
    tg = np.arange(16384, dtype=f32)
    freqs = np.concatenate([tg[:, None] * inv_freq[None, :]] * 2, -1).astype(f32)
    base = ((half + f32(0.4 * DH)) / f32(1.4 * DH)).astype(f32)
    power = ((tg - 8192.0) / f32(128.0)).astype(f32)
    scale = np.concatenate([base[None, :] ** power[:, None]] * 2, -1).astype(f32)
    cos, sin = np.cos(freqs).astype(f32), np.sin(freqs).astype(f32)
    sgn = np.concatenate([-np.ones(32, f32), np.ones(32, f32)])
    cs8 = (cos * scale * 8).astype(f32)
    ss8s = (sin * scale * 8 * sgn).astype(f32)
    ci = (cos / scale).astype(f32)
    sis = (sin / scale * sgn).astype(f32)

    W1 = inputs["L0_ffn_W1"]; W2 = inputs["L0_ffn_W2"]; b1 = inputs["L0_ffn_b1"]
    W1p = np.zeros((6, D, 2 * DI), W1.dtype)
    b1p = np.zeros((6, 2 * DI), b1.dtype)
    W2p = np.zeros((6, DI, D), W2.dtype)
    for c in range(NIC):
        lo, hi = c * 128, min((c + 1) * 128, 2730)
        n = hi - lo
        if n <= 0:
            continue
        W1p[:, :, 2 * c * 128: 2 * c * 128 + n] = W1[:, :, lo:hi]
        W1p[:, :, (2 * c + 1) * 128: (2 * c + 1) * 128 + n] = W1[:, :, 2730 + lo: 2730 + hi]
        b1p[:, 2 * c * 128: 2 * c * 128 + n] = b1[:, lo:hi]
        b1p[:, (2 * c + 1) * 128: (2 * c + 1) * 128 + n] = b1[:, 2730 + lo: 2730 + hi]
        W2p[:, c * 128: c * 128 + n, :] = W2[:, lo:hi, :]
    b1pr = b1p.reshape(6, 2 * NIC, 128).transpose(0, 2, 1).copy()

    shared = {
        "rot_tab": (cs8, ss8s, ci, sis),
        "maskA_t": mA.T.astype(bf16).copy(), "maskC_t": mC.T.astype(bf16).copy(),
        "allm_t": allm.astype(bf16), "zer_t": zer.astype(bf16),
        "triA": triA.astype(bf16), "identf": identf,
        "identb": identf.astype(bf16), "dupI": dupI.astype(bf16),
        "Wqkv": inputs["L0_ffn_Wqkv"], "Wo6": inputs["L0_ffn_Wo"],
        "W1p": W1p, "W2p": W2p, "b1pr": b1pr,
        "lnbr": inputs["L0_ffn_ln_b"].reshape(6, NCH, 128).transpose(0, 2, 1).copy(),
        "lngr": inputs["L0_ffn_ln_g"].reshape(6, NCH, 128).transpose(0, 2, 1).copy(),
        "b2r": inputs["L0_ffn_b2"].reshape(6, NCH, 128).transpose(0, 2, 1).copy(),
        "qsrep": np.broadcast_to(inputs["L0_ffn_q_scale"][:, None, :],
                                 (6, 128, DH)).copy(),
        "ksrep": np.broadcast_to(inputs["L0_ffn_k_scale"][:, None, :],
                                 (6, 128, DH)).copy(),
        "Wq": inputs["L0_attn_Wq"], "Wk": inputs["L0_attn_Wk"],
        "Wv": inputs["L0_attn_Wv"], "WoA": inputs["L0_attn_Wo"],
        "Wf1": inputs["L0_attn_Wf1"], "Wf2": inputs["L0_attn_Wf2"],
        "g1t": np.broadcast_to(inputs["L0_attn_ln1_g"][None, :], (128, D)).copy(),
        "b1t": np.broadcast_to(inputs["L0_attn_ln1_b"][None, :], (128, D)).copy(),
        "g2r": inputs["L0_attn_ln2_g"].reshape(NCH, 128).T.copy(),
        "b2ar": inputs["L0_attn_ln2_b"].reshape(NCH, 128).T.copy(),
    }

    maps = []
    for c in range(8):
        b, q = c // 4, c % 4
        s = q * 4096
        xfull = np.zeros((PREF + OWN, D), f32)
        lo = s - PREF
        xfull[max(0, -lo):, :] = x[b, max(0, lo): s + OWN]
        xb = xfull[PREF - HALO:, :].copy()
        gidx = np.clip(np.arange(-HALO, OWN) + s, 0, 16383)
        cs8c, ss8c, cic, sic = (t[gidx] for t in shared["rot_tab"])
        rotc = np.concatenate([cs8c, ss8c, cic, sic], axis=1).reshape(NS, 128, 256)
        pv = np.zeros((128, NPS), f32)
        pv[:, NPS - 32 * q:] = 1.0
        hv = np.ones((128, NS), f32)
        if q == 0:
            hv[:, OWN0] = 0.0
        m0 = shared["allm_t"].T if q == 0 else shared["maskA_t"].T
        m1 = shared["allm_t"].T if q == 0 else shared["zer_t"].T
        m2 = shared["allm_t"].T if q == 0 else shared["maskA_t"].T
        mp = {k: v for k, v in shared.items() if k not in ("rot_tab", "allm_t", "zer_t")}
        mp.update({
            "xin": xfull, "xbuf": xb, "rotcat": np.ascontiguousarray(rotc),
            "pvalidT": pv, "hvalrow": hv,
            "m0_t": np.ascontiguousarray(m0.T), "m1_t": np.ascontiguousarray(m1.T),
            "m2_t": np.ascontiguousarray(m2.T),
        })
        maps.append(mp)
    return maps


def kernel(**inputs):
    maps = _host_prep(inputs)
    if "nc" not in _CACHE:
        _CACHE["nc"] = build_program()
    nc = _CACHE["nc"]
    res = run_bass_kernel_spmd(nc, maps, core_ids=list(range(8)))
    out = np.empty((2, 16384, D), np.float32)
    for c in range(8):
        b, q = c // 4, c % 4
        out[b, q * 4096:(q + 1) * 4096] = res.results[c]["out"]
    return out


# version 04
# speedup vs baseline: 1.0320x; 1.0320x over previous
import numpy as np
import ml_dtypes
import concourse.bass as bass
import concourse.mybir as mybir
from concourse.bass_utils import run_bass_kernel_spmd

F32 = mybir.dt.float32
BF16 = mybir.dt.bfloat16
AF = mybir.ActivationFunctionType
ALU = mybir.AluOpType
AX = mybir.AxisListType
bf16 = ml_dtypes.bfloat16

D = 1024
NCH = 8
H = 16
DH = 64
OWN = 4096
HALO = 2432
BUF = HALO + OWN
NS = BUF // 128
OWN0 = HALO // 128
PREF = 12288
NPS = PREF // 128
DI = 2816
NIC = DI // 128
DF = 4096
NOC = DF // 128
BIG = np.float32(3.0e38)


def ts(i, n=128):
    return slice(i * n, (i + 1) * n)


class EM:

    ENGS = ("pe", "dve", "act", "pool", "sy")
    NSYD = 10
    NGPD = 8
    TAGS = (("pe", "dve", "act", "pool", "sy")
            + tuple(f"syd{i}" for i in range(10))
            + tuple(f"gpd{i}" for i in range(8)))

    def __init__(self):
        self.stages = {e: [] for e in self.ENGS}
        self.cnt = {t: 0 for t in self.TAGS}
        self.seen = {e: {} for e in self.ENGS}
        self.buf = {}
        self.dma_rr = {"syd": 0, "gpd": 0}

    def _b(self, k):
        if k not in self.buf:
            self.buf[k] = {"w": [], "r": {}}
        return self.buf[k]

    def stage(self, e, fn, reads=(), writes=(), accw=(), dma=False, steps=1):
        waits = {}
        seen = self.seen[e]

        def need(tag, val):
            if seen.get(tag, 0) < val:
                if waits.get(tag, 0) < val:
                    waits[tag] = val

        rdkeys = list(reads)
        for k in reads:
            b = self._b(k)
            for wt, wv in b["w"]:
                need(wt, wv)
                if wt.startswith(("syd", "gpd")):
                    rdkeys.append("SEM_" + wt)
        for k in writes:
            b = self._b(k)
            for wt, wv in b["w"]:
                need(wt, wv)
            for t, v in b["r"].items():
                need(t, v)
        if dma:
            q = "syd" if e == "sy" else "gpd"
            n = self.NSYD if q == "syd" else self.NGPD
            tag = f"{q}{self.dma_rr[q] % n}"
            self.dma_rr[q] += 1
            inc = 16
            assert steps == 1
            b = self._b("SEM_" + tag)
            for wt, wv in b["w"]:
                need(wt, wv)
            for t, v in b["r"].items():
                need(t, v)
        else:
            tag = e
            inc = steps
        for t, v in waits.items():
            seen[t] = v
        self.cnt[tag] += inc
        v = self.cnt[tag]
        if dma:
            b = self._b("SEM_" + tag)
            b["w"] = [(tag, v)]
            b["r"] = {}
        for k in writes:
            b = self._b(k)
            b["w"] = [(tag, v)]
            b["r"] = {}
        for k in accw:
            b = self._b(k)
            b["w"].append((tag, v))
        for k in rdkeys:
            self._b(k)["r"][tag] = v
        self.stages[e].append((sorted(waits.items()), fn, tag, inc, steps))

    def alias(self, dst, src):
        self.buf[dst] = self._b(src)

    def barrier(self):
        snap = dict(self.cnt)
        for e in self.ENGS:
            waits = sorted((t, v) for t, v in snap.items() if v > 0 and t != e)
            seen = self.seen[e]
            for t, v in waits:
                if seen.get(t, 0) < v:
                    seen[t] = v
            waits = [(t, v) for t, v in waits]
            self.cnt[e] += 1
            self.stages[e].append((waits, lambda eng: eng.nop(), e, 1, 1))

    def replay(self, nc):
        sems = {}
        ctxs = []
        for t in self.TAGS:
            h = nc.semaphore(f"s_{t}")
            sems[t] = h.__enter__()
            ctxs.append(h)
        block = nc.Block().__enter__()
        engmap = {"pe": "tensor", "dve": "vector", "act": "scalar",
                  "pool": "gpsimd", "sy": "sync"}
        rcnt = {t: 0 for t in self.TAGS}
        for e in self.ENGS:
            def body(eng, e=e):
                for waits, fn, tag, inc, steps in self.stages[e]:
                    for t, v in waits:
                        eng.wait_ge(sems[t], v)
                    if steps == 1:
                        inst = fn(eng)
                        inst.then_inc(sems[tag], inc)
                    else:
                        base = rcnt[tag]
                        nb = [0]

                        def bump(pinst):
                            nb[0] += 1
                            pinst.then_inc(sems[tag], 1)
                            eng.wait_ge(sems[tag], base + nb[0])
                        inst = fn(eng, bump)
                        assert nb[0] == steps - 1, (e, nb[0], steps)
                        inst.then_inc(sems[tag], 1)
                    rcnt[tag] += inc
                if e == "sy":
                    for t in self.TAGS:
                        if t.startswith(("syd", "gpd")) and self.cnt[t] > 0:
                            eng.wait_ge(sems[t], self.cnt[t])
            getattr(block, engmap[e])(body)
        block.__exit__(None, None, None)
        for h in ctxs:
            h.__exit__(None, None, None)


def build_program(nlayers=6, do_a=True, dumps=(), bffn=True, dbg_sub=None, dbg_suba=None):
    nc = bass.Bass("TRN2", target_bir_lowering=False)
    dram = {}

    def din(name, shape, dt=F32):
        dram[name] = nc.dram_tensor(name, list(shape), dt, kind="ExternalInput")

    din("xin", [PREF + OWN, D])
    din("xbuf", [BUF, D])
    din("rotcat", [NS, 128, 256])
    din("pvalidT", [128, NPS])
    din("hvalrow", [128, NS])
    din("maskA_t", [128, 128], BF16)
    din("maskC_t", [128, 128], BF16)
    din("m0_t", [128, 128], BF16)
    din("m1_t", [128, 128], BF16)
    din("m2_t", [128, 128], BF16)
    din("triA", [128, 128], BF16)
    din("identf", [128, 128])
    din("identb", [128, 128], BF16)
    din("dupI", [64, 128], BF16)
    din("Wqkv", [6, D, 3 * D]); din("Wo6", [6, D, D])
    din("W1p", [6, D, 2 * DI]); din("W2p", [6, DI, D])
    din("b1pr", [6, 128, 2 * NIC])
    din("lnbr", [6, 128, NCH])
    din("lngr", [6, 128, NCH]); din("b2r", [6, 128, NCH])
    din("qsrep", [6, 128, DH]); din("ksrep", [6, 128, DH])
    din("Wq", [D, D]); din("Wk", [D, D]); din("Wv", [D, D]); din("WoA", [D, D])
    din("Wf1", [D, DF]); din("Wf2", [DF, D])
    din("g1t", [128, D]); din("b1t", [128, D])
    din("g2r", [128, NCH]); din("b2ar", [128, NCH])
    dram["out"] = nc.dram_tensor("out", [OWN, D], F32, kind="ExternalOutput")
    dram["ysc0"] = nc.dram_tensor("ysc0", [NS, NCH, 128, 128], F32)
    dram["ysc1"] = nc.dram_tensor("ysc1", [NS, NCH, 128, 128], F32)
    dram["xasc"] = nc.dram_tensor("xasc", [32, NCH, 128, 128], F32)
    dram["cvec"] = nc.dram_tensor("cvec", [DF], F32)
    for d in dumps:
        nm, src = d[0], d[1]
        shp = list(dram[src].shape)
        dram[nm] = nc.dram_tensor(nm, shp, F32, kind="ExternalOutput")

    sbuf = {}

    def S(name, shape, dt=F32):
        sbuf[name] = nc.alloc_sbuf_tensor(name, list(shape), dt).ap()

    S("identf_s", [128, 128]); S("identb_s", [128, 128], BF16)
    S("maskA_s", [128, 128], BF16); S("maskC_s", [128, 128], BF16)
    S("m0_s", [128, 128], BF16); S("m1_s", [128, 128], BF16); S("m2_s", [128, 128], BF16)
    S("triA_s", [128, 128], BF16); S("dupI_s", [64, 128], BF16)
    S("ones_c", [128, 1], BF16); S("ones_r", [1, 128], BF16)
    S("pval_s", [128, NPS]); S("hval_s", [128, NS])
    S("rot", [128, 2 * 256])
    S("qs_s", [128, DH]); S("ks_s", [128, DH])
    S("lng_s", [128, NCH]); S("lnb_s", [128, NCH], BF16); S("b2r_s", [128, NCH])
    S("b1p_s", [128, 2 * NIC]); S("cb1_s", [128, 2 * NIC])
    S("g2r_s", [128, NCH]); S("b2a_s", [128, NCH], BF16); S("cbA_s", [128, NOC])
    S("trow", [1, 1024]); S("trow2", [1, 1024])
    S("mrow", [1, 512], BF16); S("rrow", [1, 512], BF16)
    S("ssum", [128, H]); S("rn", [128, H]); S("tt", [128, H])
    S("nmx", [128, H]); S("rsb", [128, H])
    S("den_t", [128, H]); S("rA", [128, H])
    S("ycar", [128, 4])
    S("ceps", [128, 1])
    AR = nc.alloc_sbuf_tensor("arena", [128, 67584], BF16).ap()
    WK = nc.alloc_sbuf_tensor("warena", [128, 29696], BF16).ap()
    sbuf["arena"] = AR

    def view(base, byte_off, dt, ncols, npart=128):
        assert byte_off % 2 == 0
        c0 = byte_off // 2
        if dt == BF16:
            ap = base[0:npart, c0:c0 + ncols]
        else:
            ap = base[0:npart, c0:c0 + 2 * ncols].bitcast(F32)
        return ap

    sbuf["ytile"] = view(AR, 49152, F32, 8 * NCH * 128)
    sbuf["afm"] = view(AR, 81920, BF16, NCH * 512)
    sbuf["kfm"] = view(AR, 106496, BF16, 4 * D)
    sbuf["vkm"] = view(AR, 114688, BF16, 4 * D)
    sbuf["Pt"] = view(AR, 122880, BF16, 2 * 4 * 384)
    sbuf["ptsb"] = view(AR, 129024, BF16, 2 * 12 * 128)
    sbuf["xag"] = view(AR, 49152, F32, NCH * 512)
    sbuf["aAfm"] = view(AR, 65536, BF16, NCH * 512)
    sbuf["xld"] = view(AR, 73728, F32, 2 * D)
    sbuf["xnfm"] = view(AR, 81920, F32, NCH * 128)
    sbuf["xnt"] = view(AR, 86016, F32, D)
    sbuf["Qfm"] = view(AR, 106496, BF16, D)
    sbuf["Kfm"] = view(AR, 108544, BF16, D)
    sbuf["Kkm"] = view(AR, 110592, BF16, D)
    sbuf["Vkm2"] = view(AR, 112640, BF16, D)
    sbuf["h1b"] = view(AR, 114688, BF16, D)
    sbuf["h1fm"] = view(AR, 116736, BF16, NCH * 128)
    sbuf["aAkm"] = view(AR, 118784, BF16, D)
    sbuf["ATs"] = view(AR, 120832, BF16, 2 * 128)
    sbuf["SbD"] = view(AR, 121344, BF16, H * 64)
    sbuf["zbD"] = view(AR, 123392, BF16, H)
    sbuf["Sacc"] = view(AR, 123456, F32, H * 64, npart=64)
    sbuf["zacc"] = view(AR, 127552, F32, H, npart=64)
    sbuf["SaccB"] = view(AR, 127616, BF16, H * 64, npart=64)
    sbuf["zaccB"] = view(AR, 129664, BF16, H, npart=64)
    sbuf["prodb"] = view(AR, 131072, BF16, D)
    sbuf["qA"] = view(WK, 0, F32, 2 * D)
    sbuf["kA"] = view(WK, 8192, F32, 2 * D)
    sbuf["t1"] = view(WK, 16384, F32, D)
    sbuf["t2"] = view(WK, 20480, F32, D)
    sbuf["t3"] = view(WK, 24576, F32, D)
    sbuf["qD"] = view(WK, 28672, BF16, D)
    sbuf["kD"] = view(WK, 30720, BF16, D)
    sbuf["qfm"] = view(WK, 32768, BF16, D)
    sbuf["ubf"] = view(WK, 34816, BF16, 2 * NCH * 128)
    sbuf["akm"] = view(WK, 38912, BF16, D)
    sbuf["ymg"] = view(WK, 0, F32, 2 * NCH * 384)
    sbuf["u2g"] = view(WK, 24576, BF16, NCH * 384)
    sbuf["fsb"] = view(WK, 30720, BF16, NIC * 384)
    sbuf["sqt"] = view(WK, 47616, BF16, 2 * 384)
    sbuf["tf1"] = view(WK, 49152, F32, D)
    sbuf["tf2"] = view(WK, 53248, F32, D)
    sbuf["g1t_s"] = view(WK, 40960, F32, D)
    sbuf["b1t_s"] = view(WK, 45056, F32, D)
    sbuf["xag2"] = view(WK, 28672, F32, NCH * 128)
    sbuf["yg"] = view(WK, 32768, F32, NCH * 128)
    sbuf["xkm"] = view(WK, 36864, F32, D)
    sbuf["fA"] = view(WK, 40960, BF16, NOC * 128)
    sbuf["xab"] = view(WK, 49152, BF16, NCH * 128)
    sbuf["xac"] = view(WK, 51200, BF16, NCH * 128)
    sbuf["prkm"] = view(WK, 53248, BF16, D)
    sbuf["sqA"] = view(WK, 55296, BF16, 2 * 128)
    sbuf["ssb"] = sbuf["xag2"]
    sbuf["gsb"] = sbuf["yg"]
    sbuf["outk"] = sbuf["xkm"]

    ps = {}
    ps["P1"] = nc.alloc_psum_tensor("psP1", [128, 1536], F32)
    ps["P2"] = nc.alloc_psum_tensor("psP2", [128, 1536], F32)
    ps["trE"] = nc.alloc_psum_tensor("psE", [128, 1024], BF16)
    ps["trF"] = nc.alloc_psum_tensor("psF", [128, 1024], BF16)

    em = EM()
    st = dict(em=em, T=sbuf, ps=ps, dram=dram)
    st["bffn"] = bffn
    st["dbg_sub"] = dbg_sub
    st["dbg_suba"] = dbg_suba
    if dbg_suba is not None:
        for nm, shp in (("daA", [128, D]), ("dQf", [128, D]), ("dKf", [128, D]),
                        ("dVk", [128, D]), ("dSb", [128, H * 64]), ("dzb", [128, H])):
            dram[nm] = nc.dram_tensor(nm, shp, F32, kind="ExternalOutput")
    if dbg_sub is not None:
        for nm, cols in (("dakm", D), ("dqfm", D), ("dkfm", 3 * D), ("dvkm", 3 * D)):
            dram[nm] = nc.dram_tensor(nm, [128, cols], F32, kind="ExternalOutput")
    _p_consts(st)
    _p_layer0(st)
    for l in range(1, nlayers + 1):
        _p_blayer(st, l)
    if do_a:
        _p_abranch(st, nlayers)
    for d in dumps:
        _p_dump(st, d[0], d[1], d[2] if len(d) > 2 else ())
    em.replay(nc)
    return nc


def _p_consts(st):
    em, T, dram = st["em"], st["T"], st["dram"]
    for dst, src in (("identf_s", "identf"), ("identb_s", "identb"),
                     ("maskA_s", "maskA_t"), ("maskC_s", "maskC_t"),
                     ("m0_s", "m0_t"), ("m1_s", "m1_t"), ("m2_s", "m2_t"),
                     ("triA_s", "triA"), ("pval_s", "pvalidT"), ("hval_s", "hvalrow"),
                     ("g2r_s", "g2r")):
        em.stage("sy", lambda e, d=dst, s=src: e.dma_start(out=T[d][:], in_=dram[s].ap()),
                 writes=[dst], dma=True)
    em.stage("pool", lambda e: e.dma_start(out=T["dupI_s"][:], in_=dram["dupI"].ap()),
             writes=["dupI_s"], dma=True)
    em.stage("pool", lambda e: e.dma_start(out=T["b2a_s"][:], in_=dram["b2ar"].ap()),
             writes=["b2a_s"], dma=True)
    em.stage("pool", lambda e: e.memset(T["ones_c"][:], 1.0), writes=["ones_c"])
    em.stage("pool", lambda e: e.memset(T["ones_r"][:], 1.0), writes=["ones_r"])
    em.stage("pool", lambda e: e.memset(T["ceps"][:], 1e-5), writes=["ceps"])


def _p_dump(st, nm, src, keys=()):
    em, dram = st["em"], st["dram"]
    em.stage("sy", lambda e: e.dma_start(out=dram[nm].ap().flatten(),
                                         in_=dram[src].ap().flatten()),
             reads=list(keys), dma=True)


def _p_ln(st, src_t, dst_t, ksrc, kdst, g=None, b=None):
    em, T = st["em"], st["T"]
    su, tt, rn = T["ssum"], T["tt"], T["rn"]

    em.stage("pool", lambda e: e.tensor_mul(T["t3"][:], src_t, src_t),
             reads=[ksrc], writes=["t3"])

    def f1(e, bump):
        e.reduce_sum(su[:, 0:1].unsqueeze(1), src_t.unsqueeze(1), AX.X)
        i = e.reduce_sum(su[:, 1:2].unsqueeze(1), T["t3"][:].unsqueeze(1), AX.X)
        bump(i)
        e.tensor_scalar_mul(su[:, 2:3], su[:, 0:1], 1.0 / 1024.0)
        i = e.tensor_scalar_mul(su[:, 3:4], su[:, 1:2], 1.0 / 1024.0)
        bump(i)
        i = e.tensor_mul(tt[:, 0:1], su[:, 2:3], su[:, 2:3])
        bump(i)
        return e.tensor_sub(tt[:, 1:2], su[:, 3:4], tt[:, 0:1])
    em.stage("dve", f1, reads=[ksrc, "t3"], writes=["lnstat"], steps=4)

    def f2(e, bump):
        i = e.activation(tt[:, 2:3], tt[:, 1:2], AF.Ln, bias=T["ceps"][:])
        bump(i)
        return e.activation(rn[:, 0:1], tt[:, 2:3], AF.Exp, scale=-0.5)
    em.stage("act", f2, reads=["lnstat", "ceps"], writes=["lnr"], steps=2)

    def f3(e, bump):
        i = e.tensor_scalar_mul(tt[:, 3:4], su[:, 2:3], -1.0)
        bump(i)
        return e.tensor_scalar(dst_t, src_t, tt[:, 3:4], rn[:, 0:1], ALU.add, ALU.mult)
    em.stage("dve", f3, reads=["lnstat", "lnr", ksrc], writes=[kdst], steps=2)

    if g is not None:
        def f4(e, bump):
            i = e.tensor_mul(T["t3"][:], dst_t, g)
            bump(i)
            return e.tensor_add(dst_t, T["t3"][:], b)
        em.stage("pool", f4, reads=[kdst, "g1t_s", "b1t_s"], writes=[kdst, "t3"],
                 steps=2)


def _p_transpose8(st, src_t, kin, pkey, dt="b"):
    em, T, ps = st["em"], st["T"], st["ps"]
    ident = T["identb_s"] if dt == "b" else T["identf_s"]
    ikey = "identb_s" if dt == "b" else "identf_s"

    def f(e):
        for c in range(NCH):
            i = e.transpose(ps[pkey][:, ts(c)], src_t[:, ts(c)], ident[:])
        return i
    em.stage("pe", f, reads=[kin, ikey], writes=[pkey])


def _p_layer0(st):
    em, T, ps, dram = st["em"], st["T"], st["ps"], st["dram"]
    xbuf = dram["xbuf"].ap()
    y0, y1 = dram["ysc0"].ap(), dram["ysc1"].ap()
    for m in range(NS):
        slot = m % 2
        xl = T["xld"][:, slot * D:(slot + 1) * D]
        em.stage("sy", lambda e, m=m, xl=xl: e.dma_start(out=xl, in_=xbuf[ts(m), :]),
                 writes=[f"xld{slot}"], dma=True)
        _p_ln(st, xl[:, :], T["xnt"][:, :], f"xld{slot}", "xnt")
        _p_transpose8(st, T["xnt"][:], "xnt", "P1", dt="f")
        em.stage("dve", lambda e: e.tensor_copy(T["xnfm"][:], ps["P1"][:, 0:1024]),
                 reads=["P1"], writes=["xnfm"])
        em.stage("sy", lambda e, m=m: e.dma_start(
            out=y0[m].rearrange("c p t -> p c t"),
            in_=T["xnfm"][:].rearrange("p (c t) -> p c t", c=NCH)),
            reads=["xnfm"], writes=[f"ysrc_1_{m}"], dma=True)
        if m < OWN0:
            em.stage("sy", lambda e, m=m: e.dma_start(
                out=y1[m].rearrange("c p t -> p c t"),
                in_=T["xnfm"][:].rearrange("p (c t) -> p c t", c=NCH)),
                reads=["xnfm"], writes=[f"y1init_{m}"], dma=True)


def _p_blayer(st, l):
    em, T, ps, dram = st["em"], st["T"], st["ps"], st["dram"]
    em.barrier()
    G = 3 * l + 1
    kv0 = G - 3
    q0 = G - 1
    ysrc = dram["ysc0" if l % 2 == 1 else "ysc1"].ap()
    ydst = dram["ysc1" if l % 2 == 1 else "ysc0"].ap()
    A = T["arena"]
    WQKV, WO = 0, 45056
    yt = T["ytile"][:].rearrange("p (s c t) -> p s c t", s=8, c=NCH)

    for dst, src in (("qs_s", "qsrep"), ("ks_s", "ksrep"), ("lng_s", "lngr"),
                     ("b2r_s", "b2r"), ("b1p_s", "b1pr")):
        em.stage("sy", lambda e, d=dst, s=src: e.dma_start(
            out=T[d][:], in_=dram[s].ap()[l - 1]), writes=[dst], dma=True)
    em.stage("pool", lambda e: e.dma_start(
        out=T["lnb_s"][:], in_=dram["lnbr"].ap()[l - 1]), writes=["lnb_s"], dma=True)

    wq, wo = dram["Wqkv"].ap()[l - 1], dram["Wo6"].ap()[l - 1]
    for c in range(NCH):
        em.stage("pool", lambda e, c=c: e.dma_start(
            out=A[:, WQKV + c * 3072: WQKV + (c + 1) * 3072], in_=wq[ts(c), :]),
            writes=["slotW1"] if c == 0 else [], accw=["slotW1"] if c > 0 else [],
            dma=True)
    for c in range(NCH):
        em.stage("pool", lambda e, c=c: e.dma_start(
            out=A[:, WO + c * 1024: WO + (c + 1) * 1024], in_=wo[ts(c), :]),
            writes=["slotW2"] if c == 0 else [], accw=["slotW2"] if c > 0 else [],
            dma=True)

    def load_y(mm):
        if mm >= NS:
            return
        sl = (mm - G) % 8
        if mm == kv0 - 1 and l > 1:
            key = f"ymid_{l - 1}_{mm}"
        else:
            key = f"ysrc_{l}_{mm}"
        em.stage("sy", lambda e, mm=mm, sl=sl: e.dma_start(
            out=yt[:, sl],
            in_=ysrc[mm].rearrange("c p t -> p c t")),
            reads=[key], writes=[f"yt{sl}"], dma=True)

    def build_u(mm):
        if mm >= NS:
            return
        ysl = (mm - G) % 8
        psl = (mm - 1 - G) % 8
        usl = mm % 2
        ub = T["ubf"][:, usl * 1024:(usl + 1) * 1024].rearrange("p (c t) -> p c t", c=NCH)

        def fu(e, ub=ub, ysl=ysl, psl=psl, mm=mm):
            e.tensor_copy(ub[:, 0:4], yt[:, ysl, 0:4])
            e.tensor_copy(ub[:, 4:8, 1:128], yt[:, ysl, 4:8, 0:127])
            return e.tensor_scalar_mul(ub[:, 4:8, 0:1], yt[:, psl, 4:8, 127:128],
                                       T["hval_s"][:, mm:mm + 1])
        em.stage("pool", fu, reads=[f"yt{ysl}", f"yt{psl}", "hval_s"],
                 writes=[f"ubf{usl}"])

    def emit_rot(mm):
        if mm >= NS:
            return
        us = mm % 2
        em.stage("pool", lambda e, mm=mm, us=us: e.dma_start(
            out=T["rot"][:, us * 256:(us + 1) * 256], in_=dram["rotcat"].ap()[mm]),
            writes=[f"rot{us}"], dma=True)

    def emit_qkv_part(mm, part):
        if mm >= NS:
            return
        us = mm % 2
        ubv = T["ubf"][:, us * 1024:(us + 1) * 1024].rearrange("p (c t) -> p c t",
                                                              c=NCH)
        off = 0 if part == "q" else 1536

        def fmm(e, off=off, ubv=ubv):
            for c in range(NCH):
                for sl in range(3):
                    i = e.matmul(ps["P1"][:, sl * 512:(sl + 1) * 512],
                                 ubv[:, c],
                                 A[:, WQKV + c * 3072 + off + sl * 512:
                                    WQKV + c * 3072 + off + (sl + 1) * 512],
                                 start=(c == 0), stop=(c == NCH - 1))
            return i
        em.stage("pe", fmm, reads=[f"ubf{us}", "slotW1"], writes=["P1"])
        if part == "q":
            em.stage("dve", lambda e, us=us: e.tensor_copy(
                T["qA"][:, us * D:(us + 1) * D], ps["P1"][:, 0:1024]),
                reads=["P1"], writes=[f"qA{us}"])
            em.stage("act", lambda e, us=us: e.copy(
                T["kA"][:, us * D:us * D + 512], ps["P1"][:, 1024:1536]),
                reads=["P1"], writes=[f"kAlo{us}"])
        else:
            em.stage("act", lambda e, us=us: e.copy(
                T["kA"][:, us * D + 512:(us + 1) * D], ps["P1"][:, 0:512]),
                reads=["P1"], writes=[f"kA{us}"])
            vkmr = T["vkm"][:, (mm % 4) * D:((mm % 4) + 1) * D]
            em.stage("dve", lambda e, v=vkmr, mm=mm: e.tensor_copy(
                v[:], ps["P1"][:, 512:1536]), reads=["P1"], writes=[f"vkm{mm % 4}"])

    def emit_chain(m, which):
        us = m % 2
        rot = T["rot"][:, us * 256:(us + 1) * 256]
        sk = "qA" if which == "q" else "kA"
        skkey = f"{sk}{us}"
        skeys = [skkey] + ([f"kAlo{us}"] if which == "k" else [])
        dst = "qD" if which == "q" else "kD"
        sct = "qs_s" if which == "q" else "ks_s"
        coff = 0 if which == "q" else 128
        soff = 64 if which == "q" else 192
        src_t = T[sk][:, us * D:(us + 1) * D]
        seg = src_t.rearrange("p (h d) -> p h d", h=H)

        em.stage("pool", lambda e, s=src_t: e.tensor_mul(T["t3"][:], s, s),
                 reads=skeys, writes=["t3"])

        def fr(e, bump):
            i = e.reduce_sum(T["ssum"][:].unsqueeze(2),
                             T["t3"][:].rearrange("p (h d) -> p h d", h=H), AX.X)
            bump(i)
            return e.tensor_scalar_max(T["ssum"][:], T["ssum"][:], 1e-24)
        em.stage("dve", fr, reads=["t3"], writes=["ssum"], steps=2)

        def fe(e, bump):
            i = e.activation(T["tt"][:], T["ssum"][:], AF.Ln)
            bump(i)
            return e.activation(T["rn"][:], T["tt"][:], AF.Exp, scale=-0.5)
        em.stage("act", fe, reads=["ssum"], writes=["rn"], steps=2)

        def fm1(e, seg=seg):
            return e.tensor_tensor(
                T["t1"][:].rearrange("p (h d) -> p h d", h=H), seg,
                T["rn"][:].unsqueeze(2).broadcast_to([128, H, DH]), ALU.mult)
        em.stage("dve", fm1, reads=skeys + ["rn"], writes=["t1"])

        def fm2(e, bump, sct=sct, coff=coff, rot=rot):
            t1s = T["t1"][:].rearrange("p (h d) -> p h d", h=H)
            i = e.tensor_tensor(t1s, t1s,
                                T[sct][:].unsqueeze(1).broadcast_to([128, H, DH]),
                                ALU.mult)
            bump(i)
            return e.tensor_tensor(
                T["t2"][:].rearrange("p (h d) -> p h d", h=H), t1s,
                rot[:, coff:coff + 64].unsqueeze(1).broadcast_to([128, H, DH]),
                ALU.mult)
        em.stage("pool", fm2, reads=["t1", sct, f"rot{us}"],
                 writes=["t1", "t2"], steps=2)

        def fm3(e, bump, dst=dst, soff=soff, rot=rot):
            sg = T[dst][:].rearrange("p (h d) -> p h d", h=H)
            t1s = T["t1"][:].rearrange("p (h d) -> p h d", h=H)
            t2s = T["t2"][:].rearrange("p (h d) -> p h d", h=H)
            t3s = T["t3"][:].rearrange("p (h d) -> p h d", h=H)
            e.tensor_tensor(t3s[:, :, 0:32], t1s[:, :, 32:64],
                            rot[:, soff:soff + 32].unsqueeze(1).broadcast_to([128, H, 32]),
                            ALU.mult)
            i = e.tensor_tensor(t3s[:, :, 32:64], t1s[:, :, 0:32],
                                rot[:, soff + 32:soff + 64].unsqueeze(1).broadcast_to([128, H, 32]),
                                ALU.mult)
            bump(i)
            return e.tensor_add(sg, t2s, t3s)
        em.stage("dve", fm3, reads=["t1", "t2", f"rot{us}"],
                 writes=[dst, "t3"], steps=2)

    for m in range(kv0, NS):
        yslot = (m - G) % 8
        if m == kv0:
            for mm in (m - 1, m, m + 1, m + 2):
                load_y(mm)
            build_u(m)
            emit_rot(m)
            emit_qkv_part(m, "q")
            emit_qkv_part(m, "kv")
        else:
            load_y(m + 2)
        build_u(m + 1)
        emit_rot(m + 1)

        uslot = m % 2
        kfmr = T["kfm"][:, (m % 4) * D:((m % 4) + 1) * D]

        emit_qkv_part(m + 1, "q")
        emit_chain(m, "q")
        emit_qkv_part(m + 1, "kv")
        emit_chain(m, "k")

        _p_transpose8(st, T["qD"][:], "qD", "trE")
        em.stage("dve", lambda e: e.tensor_copy(T["qfm"][:], ps["trE"][:]),
                 reads=["trE"], writes=["qfm"])
        _p_transpose8(st, T["kD"][:], "kD", "trF")
        em.stage("act", lambda e, k=kfmr: e.copy(k[:], ps["trF"][:]),
                 reads=["trF"], writes=[f"kfm{m % 4}"])

        if m < q0:
            continue

        def wv_parts(w):
            pkey = "P1" if w % 2 == 0 else "P2"
            pw = ps[pkey]
            wslot = w % 2
            return pkey, pw, wslot

        def wv_sim(w, m=m):
            pkey, pw, wslot = wv_parts(w)
            kreads = ["qfm"] + [f"kfm{(m - 2 + kc) % 4}" for kc in range(3)]

            def fsim(e, w=w, pw=pw, m=m):
                km0 = (T["m0_s"] if m == OWN0 else
                       (T["m2_s"] if m == OWN0 + 1 else T["maskA_s"]))
                km2 = T["maskC_s"]
                for hi in range(4):
                    h = 4 * w + hi
                    pb = (h % 2) * 64
                    pc = (h // 2) * 128

                    def kv(kc):
                        return T["kfm"][:, ((m - 2 + kc) % 4) * D:
                                        (((m - 2 + kc) % 4) + 1) * D][pb:pb + 64,
                                                                     pc:pc + 128]
                    qv = T["qfm"][pb:pb + 64, pc:pc + 128]
                    e.matmul(pw[:, hi * 384: hi * 384 + 128], qv, kv(0),
                             start=True, stop=False)
                    e.matmul(pw[:, hi * 384: hi * 384 + 128], km0[:], T["identb_s"][:],
                             start=False, stop=True)
                    if m == OWN0:
                        e.matmul(pw[:, hi * 384 + 128: hi * 384 + 256], qv, kv(1),
                                 start=True, stop=False)
                        e.matmul(pw[:, hi * 384 + 128: hi * 384 + 256], T["m1_s"][:],
                                 T["identb_s"][:], start=False, stop=True)
                    else:
                        e.matmul(pw[:, hi * 384 + 128: hi * 384 + 256], qv, kv(1),
                                 start=True, stop=True)
                    e.matmul(pw[:, hi * 384 + 256: hi * 384 + 384], qv, kv(2),
                             start=True, stop=False)
                    i = e.matmul(pw[:, hi * 384 + 256: hi * 384 + 384], km2[:],
                                 T["identb_s"][:], start=False, stop=True)
                return i
            em.stage("pe", fsim, reads=kreads + ["maskA_s", "maskC_s", "m0_s", "m1_s",
                                                 "m2_s", "identb_s"], writes=[pkey])

        def wv_maxexp(w):
            pkey, pw, wslot = wv_parts(w)
            em.stage("dve", lambda e, w=w, pw=pw: e.reduce_max(
                T["nmx"][:, 4 * w:4 * w + 4].unsqueeze(2),
                pw[:, 0:1536].rearrange("p (h x) -> p h x", h=4), AX.X, negate=True),
                reads=[pkey], writes=["nmx"])

            def fexp(e, w=w, pw=pw, wslot=wslot):
                for hi in range(4):
                    i = e.activation(T["Pt"][:, wslot * 1536:(wslot + 1) * 1536][:, hi * 384:(hi + 1) * 384],
                                     pw[:, hi * 384:(hi + 1) * 384], AF.Exp,
                                     bias=T["nmx"][:, 4 * w + hi:4 * w + hi + 1])
                return i
            em.stage("act", fexp, reads=[pkey, "nmx"], writes=[f"Pt{wslot}"])

        def wv_ptav(w, m=m):
            pkey, pw, wslot = wv_parts(w)
            for half in range(2):
                tk = "trE" if half == 0 else "trF"

                def ftr(e, half=half, wslot=wslot, tk=tk):
                    for j in range(6):
                        i = e.transpose(ps[tk][:, j * 128:(j + 1) * 128],
                                        T["Pt"][:, wslot * 1536:(wslot + 1) * 1536][:, half * 768 + j * 128:
                                                       half * 768 + (j + 1) * 128],
                                        T["identb_s"][:])
                    return i
                em.stage("pe", ftr, reads=[f"Pt{wslot}", "identb_s"], writes=[tk])
                em.stage("act", lambda e, half=half, wslot=wslot, tk=tk: e.copy(
                    T["ptsb"][:, wslot * 1536:(wslot + 1) * 1536][:, half * 768:(half + 1) * 768], ps[tk][:, 0:768]),
                    reads=[tk], writes=[f"ptsb{wslot}"])

            vreads = [f"ptsb{wslot}"] + [f"vkm{(m - 2 + kc) % 4}" for kc in range(3)]

            def fav(e, w=w, pw=pw, wslot=wslot, m=m):
                for hi in range(4):
                    for kc in range(3):
                        pt = T["ptsb"][:, wslot * 1536:(wslot + 1) * 1536][:, (hi * 3 + kc) * 128:(hi * 3 + kc + 1) * 128]
                        vk = T["vkm"][:, ((m - 2 + kc) % 4) * D:(((m - 2 + kc) % 4) + 1) * D][:].rearrange(
                            "p (h d) -> p h d", h=H)[:, 4 * w + hi]
                        e.matmul(pw[:, hi * 64:(hi + 1) * 64], pt, vk,
                                 start=(kc == 0), stop=(kc == 2))
                    for kc in range(3):
                        pt = T["ptsb"][:, wslot * 1536:(wslot + 1) * 1536][:, (hi * 3 + kc) * 128:(hi * 3 + kc + 1) * 128]
                        i = e.matmul(pw[:, 256 + hi:256 + hi + 1], pt, T["ones_c"][:],
                                     start=(kc == 0), stop=(kc == 2))
                return i
            em.stage("pe", fav, reads=vreads + ["ones_c"], writes=[pkey])

            def fdiv(e, bump, w=w, pw=pw):
                i = e.reciprocal(T["rsb"][:, 4 * w:4 * w + 4], pw[:, 256:260])
                bump(i)
                return e.tensor_tensor(
                    T["akm"][:].rearrange("p (h d) -> p h d", h=H)[:, 4 * w:4 * w + 4],
                    pw[:, 0:256].rearrange("p (h d) -> p h d", h=4),
                    T["rsb"][:, 4 * w:4 * w + 4].unsqueeze(2).broadcast_to([128, 4, DH]),
                    ALU.mult)
            em.stage("dve", fdiv, reads=[pkey], writes=["akm", "rsb"], steps=2)

        wv_sim(0)
        wv_maxexp(0)
        for w in range(4):
            if w + 1 < 4:
                wv_sim(w + 1)
                wv_maxexp(w + 1)
            wv_ptav(w)

        gi = (m - G) % 4
        _p_transpose8(st, T["akm"][:], "akm", "trF")
        em.stage("dve", lambda e, gi=gi: e.tensor_copy(
            T["afm"][:].rearrange("p (c g t) -> p c g t", c=NCH, g=4)[:, :, gi],
            ps["trF"][:].rearrange("p (c t) -> p c t", c=NCH)),
            reads=["trF"], writes=["afm"])

        if m == q0 or (m - G) % 4 == 3 or m == NS - 1:
            ng = 1 if m == q0 else (m - G) % 4 + 1
            gi0 = (m - ng + 1 - G) % 4
            af = T["afm"][:].rearrange("p (c t) -> p c t", c=NCH)
            g0s = (m - ng + 1 - G) % 8
            yrd = [f"yt{(g0s + j) % 8}" for j in range(ng)]
            for op in range(4):
                pk = "P1" if op % 2 == 0 else "P2"

                def fwo(e, op=op, pk=pk, ng=ng, af=af, gi0=gi0):
                    for oc in (2 * op, 2 * op + 1):
                        half = oc % 2
                        for ic in range(NCH):
                            i = e.matmul(ps[pk][:, half * 512: half * 512 + ng * 128],
                                         A[:, WO + ic * 1024 + oc * 128:
                                            WO + ic * 1024 + (oc + 1) * 128],
                                         af[:, ic, gi0 * 128:(gi0 + ng) * 128],
                                         start=(ic == 0), stop=(ic == NCH - 1))
                    return i
                em.stage("pe", fwo, reads=["afm", "slotW2"], writes=[pk])

                def fadd(e, op=op, pk=pk, ng=ng, g0s=g0s):
                    for oc in (2 * op, 2 * op + 1):
                        half = oc % 2
                        i = e.tensor_tensor(
                            yt[:, g0s:g0s + ng, oc],
                            ps[pk][:, half * 512: half * 512 + ng * 128]
                            .rearrange("p (s t) -> p s t", s=ng),
                            yt[:, g0s:g0s + ng, oc],
                            ALU.add)
                    return i
                em.stage("dve", fadd, reads=[pk] + yrd, writes=yrd)

            for j in range(ng):
                mm = m - ng + 1 + j
                sl = (g0s + j) % 8
                em.stage("sy", lambda e, mm=mm, sl=sl: e.dma_start(
                    out=ydst[mm].rearrange("c p t -> p c t"), in_=yt[:, sl]),
                    reads=yrd, writes=[f"ymid_{l}_{mm}"], dma=True)

    if st.get("bffn", True):
        _p_bffn(st, l, ydst)


def _p_bffn(st, l, ydst):
    em, T, ps, dram = st["em"], st["T"], st["ps"], st["dram"]
    em.barrier()
    G = 3 * l + 1
    A = T["arena"]
    W1, W2 = 0, 45056
    w1, w2 = dram["W1p"].ap()[l - 1], dram["W2p"].ap()[l - 1]

    for c in range(NCH):
        em.stage("pool", lambda e, c=c: e.dma_start(
            out=A[:, W1 + c * 5632: W1 + (c + 1) * 5632], in_=w1[ts(c), :]),
            writes=["slotW1"] if c == 0 else [], accw=["slotW1"] if c > 0 else [],
            dma=True)
    for c in range(NIC):
        em.stage("pool", lambda e, c=c: e.dma_start(
            out=A[:, W2 + c * 1024: W2 + (c + 1) * 1024], in_=w2[ts(c), :]),
            writes=["slotW2"] if c == 0 else [], accw=["slotW2"] if c > 0 else [],
            dma=True)

    for seg in range(11):
        pk, pk2 = ("P1", "P2") if seg % 2 == 0 else ("P2", "P1")

        def fseg(e, seg=seg, pk=pk):
            for c in range(NCH):
                i = e.matmul(ps[pk][0:1, 0:512], T["lnb_s"][:, c:c + 1],
                             A[:, W1 + c * 5632 + seg * 512: W1 + c * 5632 + (seg + 1) * 512],
                             start=(c == 0), stop=(c == NCH - 1))
            return i
        em.stage("pe", fseg, reads=["lnb_s", "slotW1"], writes=[pk])
        em.stage("act", lambda e, pk=pk: e.copy(T["mrow"][0:1, 0:512],
                                                ps[pk][0:1, 0:512]),
                 reads=[pk], writes=["mrow"])

        def fsp2(e, pk2=pk2):
            for u in range(4):
                i = e.matmul(ps[pk2][:, u:u + 1],
                             T["mrow"][0:1, u * 128:(u + 1) * 128],
                             T["ones_c"][0:1, 0:1], start=True, stop=True)
            return i
        em.stage("pe", fsp2, reads=["mrow", "ones_c"], writes=[pk2])
        em.stage("dve", lambda e, seg=seg, pk2=pk2: e.tensor_copy(
            T["cb1_s"][:, seg * 4:(seg * 4) + 4], ps[pk2][:, 0:4]),
            reads=[pk2], writes=["cb1_s"] if seg == 10 else [])
    em.stage("dve", lambda e: e.tensor_add(T["cb1_s"][:], T["cb1_s"][:], T["b1p_s"][:]),
             reads=["cb1_s", "b1p_s"], writes=["cb1_s"])

    TG = 3
    ngroups = (NS - G + TG - 1) // TG

    def grp(jg):
        s0 = G + jg * TG
        nsub = min(TG, NS - s0)
        return s0, nsub, nsub * 128, jg % 2

    def emit_u2_loads(jg):
        s0, nsub, TC, gslot = grp(jg)
        u2 = T["u2g"][:].rearrange("p (c t) -> p c t", c=NCH, t=384)
        ym = T["ymg"][:, gslot * (NCH * 384):(gslot + 1) * (NCH * 384)].rearrange(
            "p (c t) -> p c t", c=NCH, t=384)
        ymk = [f"ymid_{l}_{s0 + j}" for j in range(nsub)]
        for c4 in range(4):
            em.stage("pool", lambda e, s0=s0, nsub=nsub, c4=c4, u2=u2: e.dma_start(
                out=u2[:, c4, 0:nsub * 128].rearrange("p (s t) -> p s t", s=nsub),
                in_=ydst[s0:s0 + nsub, c4].rearrange("s p t -> p s t")),
                reads=ymk, writes=["u2g"] if c4 == 0 else [],
                accw=["u2g"] if c4 > 0 else [], dma=True)
        for c4 in range(4, 8):
            em.stage("pool", lambda e, s0=s0, nsub=nsub, c4=c4, u2=u2: e.dma_start(
                out=u2[:, c4, 0:nsub * 128].rearrange("p (s t) -> p s t", s=nsub)[:, :, 1:128],
                in_=ydst[s0:s0 + nsub, c4].rearrange("s p t -> p s t")[:, :, 0:127]),
                reads=ymk, accw=["u2g"], dma=True)
        if nsub > 1:
            for c4 in range(4, 8):
                em.stage("pool", lambda e, s0=s0, nsub=nsub, c4=c4, u2=u2: e.dma_start(
                    out=u2[:, c4, 128:nsub * 128].rearrange(
                        "p (s t) -> p s t", s=nsub - 1)[:, :, 0:1],
                    in_=ydst[s0:s0 + nsub - 1, c4].rearrange(
                        "s p t -> p s t")[:, :, 127:128],
                    allow_slow_non_contiguous=True),
                    reads=ymk, accw=["u2g"], dma=True)
        if jg == 0:
            def fu2d(e, s0=s0, u2=u2):
                return e.dma_start(
                    out=u2[:, 4:8, 0:1],
                    in_=ydst[s0 - 1, 4:8].rearrange("c p t -> p c t")[:, :, 127:128],
                    allow_slow_non_contiguous=True)
            em.stage("pool", fu2d, reads=[f"ymid_{l}_{s0 - 1}"], accw=["u2g"],
                     dma=True)
        else:
            em.stage("pool", lambda e, u2=u2: e.tensor_copy(
                u2[:, 4:8, 0:1], T["ycar"][:].unsqueeze(2)),
                reads=["ycar"], accw=["u2g"])
        if s0 <= OWN0 < s0 + nsub:
            o = (OWN0 - s0) * 128
            em.stage("pool", lambda e, o=o, u2=u2: e.tensor_scalar_mul(
                u2[:, 4:8, o:o + 1], u2[:, 4:8, o:o + 1],
                T["hval_s"][:, OWN0:OWN0 + 1]),
                reads=["u2g", "hval_s"], writes=["u2g"])
        for c8 in range(NCH):
            em.stage("sy", lambda e, s0=s0, nsub=nsub, ym=ym, c8=c8: e.dma_start(
                out=ym[:, c8, 0:nsub * 128].rearrange("p (s t) -> p s t", s=nsub),
                in_=ydst[s0:s0 + nsub, c8].rearrange("s p t -> p s t")),
                reads=ymk, writes=[f"ymg{gslot}"] if c8 == 0 else [],
                accw=[f"ymg{gslot}"] if c8 > 0 else [], dma=True)
        em.stage("pool", lambda e, ym=ym, TC=TC: e.tensor_copy(
            T["ycar"][:].unsqueeze(2), ym[:, 4:8, TC - 1:TC]),
            reads=[f"ymg{gslot}"], writes=["ycar"])

    def emit_stats_u2rg(jg):
        s0, nsub, TC, gslot = grp(jg)
        u2 = T["u2g"][:].rearrange("p (c t) -> p c t", c=NCH, t=384)

        def fst1(e, TC=TC, u2=u2):
            for c in range(NCH):
                i = e.matmul(ps["P2"][0:1, 0:TC], T["ones_c"][:], u2[:, c, 0:TC],
                             start=(c == 0), stop=(c == NCH - 1))
            return i
        em.stage("pe", fst1, reads=["u2g", "ones_c"], writes=["P2"])
        for c in range(NCH):
            sq = T["sqt"][:, (c % 2) * 384:((c % 2) + 1) * 384]
            em.stage("pool", lambda e, c=c, sq=sq, TC=TC, u2=u2: e.tensor_mul(
                sq[:, 0:TC], u2[:, c, 0:TC], u2[:, c, 0:TC]),
                reads=["u2g"], writes=[f"sqt{c % 2}"])
            em.stage("pe", lambda e, c=c, sq=sq, TC=TC: e.matmul(
                ps["P2"][0:1, 512:512 + TC], T["ones_c"][:], sq[:, 0:TC],
                start=(c == 0), stop=(c == NCH - 1)),
                reads=[f"sqt{c % 2}", "ones_c"], writes=["P2"])

        def fst2(e, bump, TC=TC):
            e.tensor_scalar_mul(T["trow"][0:1, 0:TC], ps["P2"][0:1, 0:TC], 1.0 / 1024.0)
            i = e.tensor_scalar_mul(T["trow"][0:1, 512:512 + TC],
                                    ps["P2"][0:1, 512:512 + TC], 1.0 / 1024.0)
            bump(i)
            i = e.tensor_mul(T["trow2"][0:1, 0:TC], T["trow"][0:1, 0:TC],
                             T["trow"][0:1, 0:TC])
            bump(i)
            e.tensor_sub(T["trow2"][0:1, 512:512 + TC], T["trow"][0:1, 512:512 + TC],
                         T["trow2"][0:1, 0:TC])
            return e.tensor_copy(T["mrow"][0:1, 0:TC], T["trow"][0:1, 0:TC])
        em.stage("dve", fst2, reads=["P2"], writes=["trow", "mrow"], steps=3)

        def fst3(e, bump, TC=TC):
            i = e.activation(T["trow2"][0:1, 0:TC], T["trow2"][0:1, 512:512 + TC],
                             AF.Ln, bias=T["ceps"][0:1, :])
            bump(i)
            return e.activation(T["rrow"][0:1, 0:TC], T["trow2"][0:1, 0:TC], AF.Exp,
                                scale=-0.5)
        em.stage("act", fst3, reads=["trow", "ceps"], writes=["rrow"], steps=2)

        def fbc(e, TC=TC):
            e.matmul(ps["P2"][:, 0:TC], T["ones_r"][:], T["mrow"][0:1, 0:TC],
                     start=True, stop=True)
            return e.matmul(ps["P2"][:, 512:512 + TC], T["ones_r"][:],
                            T["rrow"][0:1, 0:TC], start=True, stop=True)
        em.stage("pe", fbc, reads=["mrow", "rrow", "ones_r"], writes=["P2"])

        def fu2rg(e, bump, TC=TC, u2=u2):
            for c in range(NCH):
                i = e.tensor_sub(T["tf1"][:, 0:TC], u2[:, c, 0:TC], ps["P2"][:, 0:TC])
                bump(i)
                i = e.scalar_tensor_tensor(u2[:, c, 0:TC], T["tf1"][:, 0:TC],
                                           T["lng_s"][:, c:c + 1],
                                           ps["P2"][:, 512:512 + TC], ALU.mult, ALU.mult)
                if c < NCH - 1:
                    bump(i)
            return i
        em.stage("dve", fu2rg, reads=["P2", "u2g", "lng_s"], writes=["u2g", "tf1"],
                 steps=16)

    def emit_w1(jg):
        s0, nsub, TC, gslot = grp(jg)
        u2 = T["u2g"][:].rearrange("p (c t) -> p c t", c=NCH, t=384)
        fv = T["fsb"][:].rearrange("p (k t) -> p k t", k=NIC, t=384)
        for pc in range(NIC):
            pk = "P1" if pc % 2 == 0 else "P2"
            pw = ps[pk]

            def fw1(e, pc=pc, pw=pw, TC=TC, u2=u2):
                for c in range(NCH):
                    e.matmul(pw[:, 0:TC],
                             A[:, W1 + c * 5632 + 2 * pc * 128: W1 + c * 5632 + (2 * pc + 1) * 128],
                             u2[:, c, 0:TC], start=(c == 0), stop=(c == NCH - 1))
                for c in range(NCH):
                    i = e.matmul(pw[:, 512:512 + TC],
                                 A[:, W1 + c * 5632 + (2 * pc + 1) * 128: W1 + c * 5632 + (2 * pc + 2) * 128],
                                 u2[:, c, 0:TC], start=(c == 0), stop=(c == NCH - 1))
                return i
            em.stage("pe", fw1, reads=["u2g", "slotW1"], writes=[pk])

            em.stage("act", lambda e, pc=pc, pw=pw, TC=TC: e.activation(
                T["tf2"][:, 0:TC], pw[:, 512:512 + TC], AF.Gelu,
                bias=T["cb1_s"][:, 2 * pc + 1:2 * pc + 2]),
                reads=[pk, "cb1_s"], writes=["tf2"])

            em.stage("dve", lambda e, pc=pc, pw=pw, TC=TC, fv=fv: e.scalar_tensor_tensor(
                fv[:, pc, 0:TC], pw[:, 0:TC], T["cb1_s"][:, 2 * pc:2 * pc + 1],
                T["tf2"][:, 0:TC], ALU.add, ALU.mult),
                reads=[pk, "tf2", "cb1_s"], writes=["fsb"])

    def emit_w2(jg):
        s0, nsub, TC, gslot = grp(jg)
        ym = T["ymg"][:, gslot * (NCH * 384):(gslot + 1) * (NCH * 384)].rearrange(
            "p (c t) -> p c t", c=NCH, t=384)
        fv = T["fsb"][:].rearrange("p (k t) -> p k t", k=NIC, t=384)
        for oc in range(NCH):
            def fw2(e, oc=oc, TC=TC, fv=fv):
                for k in range(NIC):
                    i = e.matmul(ps["P1"][:, 0:TC],
                                 A[:, W2 + k * 1024 + oc * 128: W2 + k * 1024 + (oc + 1) * 128],
                                 fv[:, k, 0:TC], start=(k == 0), stop=(k == NIC - 1))
                return i
            em.stage("pe", fw2, reads=["fsb", "slotW2"], writes=["P1"])
            em.stage("dve", lambda e, oc=oc, TC=TC, ym=ym: e.scalar_tensor_tensor(
                ym[:, oc, 0:TC], ps["P1"][:, 0:TC], T["b2r_s"][:, oc:oc + 1],
                ym[:, oc, 0:TC], ALU.add, ALU.add),
                reads=["P1", f"ymg{gslot}", "b2r_s"], writes=[f"ymg{gslot}"])
        wkeys = [f"ysrc_{l + 1}_{s0 + j}" for j in range(nsub)]
        for c8 in range(NCH):
            em.stage("sy", lambda e, s0=s0, nsub=nsub, ym=ym, c8=c8: e.dma_start(
                out=ydst[s0:s0 + nsub, c8].rearrange("s p t -> p s t"),
                in_=ym[:, c8, 0:nsub * 128].rearrange("p (s t) -> p s t", s=nsub)),
                reads=[f"ymg{gslot}"], writes=wkeys if c8 == 0 else [],
                accw=wkeys if c8 > 0 else [], dma=True)

    emit_u2_loads(0)
    emit_stats_u2rg(0)
    for jg in range(ngroups):
        emit_w1(jg)
        if jg + 1 < ngroups:
            emit_u2_loads(jg + 1)
            emit_stats_u2rg(jg + 1)
        emit_w2(jg)


def _p_abranch(st, nlayers):
    em, T, ps, dram = st["em"], st["T"], st["ps"], st["dram"]
    em.barrier()
    A = T["arena"]
    WQ, WK, WV, WOA = 0, 8192, 16384, 45056
    xin = dram["xin"].ap()

    first = True
    for nm, off in (("Wq", WQ), ("Wk", WK), ("Wv", WV)):
        for c in range(NCH):
            em.stage("pool", lambda e, nm=nm, off=off, c=c: e.dma_start(
                out=A[:, off + c * 1024: off + (c + 1) * 1024],
                in_=dram[nm].ap()[ts(c), :]),
                writes=["slotW1"] if first else [],
                accw=[] if first else ["slotW1"], dma=True)
            first = False
    for c in range(NCH):
        em.stage("pool", lambda e, c=c: e.dma_start(
            out=A[:, WOA + c * 1024: WOA + (c + 1) * 1024],
            in_=dram["WoA"].ap()[ts(c), :]),
            writes=["slotW2"] if c == 0 else [], accw=["slotW2"] if c > 0 else [],
            dma=True)

    for dst, srcn in (("g1t_s", "g1t"), ("b1t_s", "b1t")):
        em.stage("sy", lambda e, d=dst, s=srcn: e.dma_start(
            out=T[d][:], in_=dram[s].ap()), writes=[dst], dma=True)

    def fz(e):
        e.memset(T["Sacc"][:], 0.0)
        e.memset(T["zacc"][:], 0.0)
        e.memset(T["SbD"][:], 0.0)
        return e.memset(T["zbD"][:], 0.0)
    em.stage("dve", fz, writes=["Sacc", "SbD"])

    for j in range(NPS + 32):
        _p_asub(st, j, j >= NPS)

    _p_affn(st, nlayers)


def _p_asub(st, j, own):
    em, T, ps, dram = st["em"], st["T"], st["ps"], st["dram"]
    A = T["arena"]
    WQ, WK, WV, WOA = 0, 8192, 16384, 45056
    xin = dram["xin"].ap()
    slot = j % 2

    em.stage("sy", lambda e, j=j, slot=slot: e.dma_start(
        out=T["xld"][:, slot * D:(slot + 1) * D], in_=xin[ts(j), :]), writes=[f"xld{slot}"], dma=True)
    _p_ln(st, T["xld"][:, slot * D:(slot + 1) * D][:, :], T["xnt"][:, :], f"xld{slot}", "xnt")
    _p_ln(st, T["xnt"][:, :], T["t1"][:, :], "xnt", "h1a",
          g=T["g1t_s"][:], b=T["b1t_s"][:])
    em.stage("dve", lambda e: e.tensor_copy(T["h1b"][:], T["t1"][:]),
             reads=["h1a"], writes=["h1b"])
    _p_transpose8(st, T["h1b"][:], "h1b", "trE")
    em.stage("dve", lambda e: e.tensor_copy(T["h1fm"][:], ps["trE"][:]),
             reads=["trE"], writes=["h1fm"])
    h1f = T["h1fm"][:].rearrange("p (c t) -> p c t", c=NCH)

    if own:
        _p_transpose8(st, T["xnt"][:], "xnt", "P1", dt="f")
        em.stage("dve", lambda e: e.tensor_copy(T["xnfm"][:], ps["P1"][:, 0:1024]),
                 reads=["P1"], writes=["xnfm"])

    def fkv(e):
        for half in range(2):
            for c in range(NCH):
                e.matmul(ps["P1"][:, half * 512:(half + 1) * 512], h1f[:, c],
                         A[:, WK + c * 1024 + half * 512: WK + c * 1024 + (half + 1) * 512],
                         start=(c == 0), stop=(c == NCH - 1))
        for half in range(2):
            for c in range(NCH):
                i = e.matmul(ps["P2"][:, half * 512:(half + 1) * 512], h1f[:, c],
                             A[:, WV + c * 1024 + half * 512: WV + c * 1024 + (half + 1) * 512],
                             start=(c == 0), stop=(c == NCH - 1))
        return i
    em.stage("pe", fkv, reads=["h1fm", "slotW1"], writes=["P1", "P2"])

    def fp1(e):
        e.tensor_scalar_min(T["t1"][:], ps["P1"][:, 0:1024], 0.0)
        return e.tensor_scalar_add(T["t2"][:], ps["P1"][:, 0:1024], 1.0)
    em.stage("dve", fp1, reads=["P1"], writes=["t1", "t2"])
    em.stage("act", lambda e: e.activation(T["t1"][:], T["t1"][:], AF.Exp),
             reads=["t1"], writes=["t1"])

    def fp2(e, bump, j=j, own=own):
        i = e.tensor_max(T["t3"][:], T["t1"][:], T["t2"][:])
        bump(i)
        if not own:
            return e.tensor_scalar_mul(T["Kkm"][:], T["t3"][:], T["pval_s"][:, j:j + 1])
        return e.tensor_copy(T["Kkm"][:], T["t3"][:])
    em.stage("dve", fp2, reads=["t1", "t2", "pval_s"], writes=["Kkm", "t3"], steps=2)
    em.stage("act", lambda e: e.copy(T["Vkm2"][:], ps["P2"][:, 0:1024]),
             reads=["P2"], writes=["Vkm2"])

    if own:
        def fq(e):
            for pr in range(NCH):
                for c in range(NCH):
                    i = e.matmul(ps["P1"][:, ts(pr)],
                                 A[:, WQ + c * 1024 + pr * 128: WQ + c * 1024 + (pr + 1) * 128],
                                 h1f[:, c], start=(c == 0), stop=(c == NCH - 1))
            return i
        em.stage("pe", fq, reads=["h1fm", "slotW1"], writes=["P1"])

        def fq1(e):
            e.tensor_scalar_min(T["t1"][:], ps["P1"][:, 0:1024], 0.0)
            return e.tensor_scalar_add(T["t2"][:], ps["P1"][:, 0:1024], 1.0)
        em.stage("dve", fq1, reads=["P1"], writes=["t1", "t2"])
        em.stage("act", lambda e: e.activation(T["t1"][:], T["t1"][:], AF.Exp),
                 reads=["t1"], writes=["t1"])
        em.stage("dve", lambda e: e.tensor_max(T["Qfm"][:], T["t1"][:], T["t2"][:]),
                 reads=["t1", "t2"], writes=["Qfm"])

        _p_transpose8(st, T["Kkm"][:], "Kkm", "trF")
        em.stage("act", lambda e: e.copy(T["Kfm"][:], ps["trF"][:]),
                 reads=["trF"], writes=["Kfm"])

        for h in range(H):
            pb = (h % 2) * 64
            pc = (h // 2) * 128
            asl = h % 2

            em.stage("pe", lambda e, pb=pb, pc=pc, asl=asl: e.matmul(
                ps["P2"][:, 1024 + asl * 128: 1152 + asl * 128],
                T["Kfm"][pb:pb + 64, pc:pc + 128], T["Qfm"][pb:pb + 64, pc:pc + 128],
                start=True, stop=True),
                reads=["Kfm", "Qfm"], writes=[f"ATp{asl}"])

            em.stage("dve", lambda e, asl=asl: e.tensor_tensor(
                T["ATs"][:, asl * 128:(asl + 1) * 128],
                ps["P2"][:, 1024 + asl * 128: 1152 + asl * 128], T["triA_s"][:],
                ALU.mult),
                reads=[f"ATp{asl}", "triA_s"], writes=[f"ATs{asl}"])

            def fnd(e, h=h, pb=pb, pc=pc, asl=asl):
                vh = T["Vkm2"][:].rearrange("p (h d) -> p h d", h=H)[:, h]
                e.matmul(ps["P1"][:, h * 64:(h + 1) * 64],
                         T["ATs"][:, asl * 128:(asl + 1) * 128], vh,
                         start=True, stop=False)
                e.matmul(ps["P1"][:, h * 64:(h + 1) * 64],
                         T["Qfm"][pb:pb + 64, pc:pc + 128],
                         T["SbD"][pb:pb + 64, h * 64:(h + 1) * 64],
                         start=False, stop=True)
                e.matmul(ps["P1"][:, 1024 + h:1024 + h + 1],
                         T["ATs"][:, asl * 128:(asl + 1) * 128], T["ones_c"][:],
                         start=True, stop=False)
                return e.matmul(ps["P1"][:, 1024 + h:1024 + h + 1],
                                T["Qfm"][pb:pb + 64, pc:pc + 128],
                                T["zbD"][pb:pb + 64, h:h + 1],
                                start=False, stop=True)
            em.stage("pe", fnd, reads=[f"ATs{asl}", "Vkm2", "SbD", "zbD", "ones_c",
                                       "Qfm"], writes=["P1"])

        def fdv(e, bump):
            i = e.tensor_scalar_max(T["den_t"][:], ps["P1"][:, 1024:1024 + H], 1e-6)
            bump(i)
            i = e.reciprocal(T["rA"][:], T["den_t"][:])
            bump(i)
            return e.tensor_tensor(
                T["aAkm"][:].rearrange("p (h d) -> p h d", h=H),
                ps["P1"][:, 0:1024].rearrange("p (h d) -> p h d", h=H),
                T["rA"][:].unsqueeze(2).broadcast_to([128, H, DH]), ALU.mult)
        em.stage("dve", fdv, reads=["P1"], writes=["aAkm", "rA", "den_t"], steps=3)

    def fsu(e):
        kk = T["Kkm"][:].rearrange("p (h d) -> p h d", h=H)
        vv = T["Vkm2"][:].rearrange("p (h d) -> p h d", h=H)
        for h in range(H):
            e.matmul(ps["P2"][0:64, h * 64:(h + 1) * 64], kk[:, h], vv[:, h],
                     start=True, stop=True)
            i = e.matmul(ps["P2"][0:64, 1024 + h:1024 + h + 1], kk[:, h],
                         T["ones_c"][:], start=True, stop=True)
        return i
    em.stage("pe", fsu, reads=["Kkm", "Vkm2", "ones_c"], writes=["P2"])

    def fsa(e, bump):
        e.tensor_add(T["Sacc"][:], T["Sacc"][:], ps["P2"][0:64, 0:1024])
        i = e.tensor_add(T["zacc"][:], T["zacc"][:], ps["P2"][0:64, 1024:1024 + H])
        bump(i)
        e.tensor_copy(T["SaccB"][:], T["Sacc"][:])
        return e.tensor_copy(T["zaccB"][:], T["zacc"][:])
    em.stage("dve", fsa, reads=["P2", "Sacc"], writes=["Sacc", "SaccB"], steps=2)

    if j + 1 >= NPS:
        def fdp(e):
            e.matmul(ps["P2"][:, 0:512], T["dupI_s"][:], T["SaccB"][:, 0:512],
                     start=True, stop=True)
            e.matmul(ps["P2"][:, 512:1024], T["dupI_s"][:], T["SaccB"][:, 512:1024],
                     start=True, stop=True)
            return e.matmul(ps["P2"][:, 1024:1024 + H], T["dupI_s"][:], T["zaccB"][:],
                            start=True, stop=True)
        em.stage("pe", fdp, reads=["SaccB", "dupI_s"], writes=["P2"])

        def fdc(e):
            e.tensor_copy(T["SbD"][:], ps["P2"][:, 0:1024])
            return e.tensor_copy(T["zbD"][:], ps["P2"][:, 1024:1024 + H])
        em.stage("dve", fdc, reads=["P2"], writes=["SbD", "zbD"])

    if own:
        if st.get("dbg_suba") == j - NPS:
            for nm, key in (("daA", "aAkm"), ("dQf", "Qfm"), ("dKf", "Kfm"),
                            ("dVk", "Vkm2"), ("dSb", "SbD"), ("dzb", "zbD")):
                em.stage("pool", lambda e, nm=nm, key=key: e.dma_start(
                    out=dram[nm].ap(), in_=T[key][:]), reads=[key], dma=True)
        oj = j - NPS
        gi = oj % 4
        _p_transpose8(st, T["aAkm"][:], "aAkm", "trF")
        em.stage("dve", lambda e, gi=gi: e.tensor_copy(
            T["aAfm"][:].rearrange("p (c g t) -> p c g t", c=NCH, g=4)[:, :, gi],
            ps["trF"][:].rearrange("p (c t) -> p c t", c=NCH)),
            reads=["trF"], writes=["aAfm"])
        em.stage("dve", lambda e, gi=gi: e.tensor_copy(
            T["xag"][:].rearrange("p (c g t) -> p c g t", c=NCH, g=4)[:, :, gi],
            T["xnfm"][:].rearrange("p (c t) -> p c t", c=NCH)),
            reads=["xnfm"], writes=["xag"])

        if gi == 3:
            af = T["aAfm"][:].rearrange("p (c t) -> p c t", c=NCH)
            xg = T["xag"][:].rearrange("p (c t) -> p c t", c=NCH)
            for op in range(4):
                pk = "P2" if op % 2 == 0 else "P1"

                def fwo(e, op=op, pk=pk, af=af):
                    for oc in (2 * op, 2 * op + 1):
                        half = oc % 2
                        for ic in range(NCH):
                            i = e.matmul(ps[pk][:, half * 512:(half + 1) * 512],
                                         A[:, WOA + ic * 1024 + oc * 128:
                                            WOA + ic * 1024 + (oc + 1) * 128],
                                         af[:, ic], start=(ic == 0), stop=(ic == NCH - 1))
                    return i
                em.stage("pe", fwo, reads=["aAfm", "slotW2"], writes=[pk])

                def fad(e, op=op, pk=pk, xg=xg):
                    for oc in (2 * op, 2 * op + 1):
                        half = oc % 2
                        i = e.tensor_add(xg[:, oc], ps[pk][:, half * 512:(half + 1) * 512],
                                         xg[:, oc])
                    return i
                em.stage("dve", fad, reads=[pk, "xag"], writes=["xag"])

            for c8 in range(NCH):
                em.stage("sy", lambda e, oj=oj, xg=xg, c8=c8: e.dma_start(
                    out=dram["xasc"].ap()[oj - 3:oj + 1, c8].rearrange("s p t -> p s t"),
                    in_=xg[:, c8].rearrange("p (s t) -> p s t", s=4)),
                    reads=["xag"], writes=[f"xasc{oj // 4}"] if c8 == 0 else [],
                    accw=[f"xasc{oj // 4}"] if c8 > 0 else [], dma=True)


def _p_affn(st, nlayers):
    em, T, ps, dram = st["em"], st["T"], st["ps"], st["dram"]
    em.barrier()
    A = T["arena"]
    WF1, WF2 = 0, NCH * DF
    yfin = dram["ysc0" if nlayers % 2 == 0 else "ysc1"].ap()
    xin = dram["xin"].ap()

    for c in range(NCH):
        em.stage("pool", lambda e, c=c: e.dma_start(
            out=A[:, WF1 + c * DF: WF1 + (c + 1) * DF], in_=dram["Wf1"].ap()[ts(c), :]),
            writes=["slotW1"] if c == 0 else [], accw=["slotW1"] if c > 0 else [],
            dma=True)
    for c in range(NOC):
        em.stage("pool", lambda e, c=c: e.dma_start(
            out=A[:, WF2 + c * 1024: WF2 + (c + 1) * 1024],
            in_=dram["Wf2"].ap()[ts(c), :]),
            writes=["slotW2"] if c == 0 else [], accw=["slotW2"] if c > 0 else [],
            dma=True)

    for seg in range(8):
        pk = "P1" if seg % 2 == 0 else "P2"

        def fseg(e, seg=seg, pk=pk):
            for c in range(NCH):
                i = e.matmul(ps[pk][0:1, 0:512], T["b2a_s"][:, c:c + 1],
                             A[:, WF1 + c * DF + seg * 512: WF1 + c * DF + (seg + 1) * 512],
                             start=(c == 0), stop=(c == NCH - 1))
            return i
        em.stage("pe", fseg, reads=["b2a_s", "slotW1"], writes=[pk])
        pk2 = "P2" if pk == "P1" else "P1"
        em.stage("act", lambda e, pk=pk: e.copy(T["mrow"][0:1, 0:512],
                                                ps[pk][0:1, 0:512]),
                 reads=[pk], writes=["mrow"])

        def fsp(e, pk2=pk2):
            for u in range(4):
                i = e.matmul(ps[pk2][:, u:u + 1],
                             T["mrow"][0:1, u * 128:(u + 1) * 128],
                             T["ones_c"][0:1, 0:1], start=True, stop=True)
            return i
        em.stage("pe", fsp, reads=["mrow", "ones_c"], writes=[pk2])
        em.stage("dve", lambda e, seg=seg, pk2=pk2: e.tensor_copy(
            T["cbA_s"][:, seg * 4:seg * 4 + 4], ps[pk2][:, 0:4]),
            reads=[pk2], writes=["cbA_s"] if seg == 7 else [])

    fva = T["fA"][:].rearrange("p (k t) -> p k t", k=NOC, t=128)
    TC = 128
    for s0 in range(32):
        xg2 = T["xag2"][:].rearrange("p (c t) -> p c t", c=NCH, t=128)
        ygv = T["yg"][:].rearrange("p (c t) -> p c t", c=NCH, t=128)

        em.stage("sy", lambda e, s0=s0, xg2=xg2: e.dma_start(
            out=xg2, in_=dram["xasc"].ap()[s0].rearrange("c p t -> p c t")),
            reads=[f"xasc{s0 // 4}"], writes=["xag2"], dma=True)

        em.stage("pool", lambda e, xg2=xg2: e.tensor_copy(
            T["xab"][:], xg2.rearrange("p c t -> p (c t)")),
            reads=["xag2"], writes=["xab"])
        xabv = T["xab"][:].rearrange("p (c t) -> p c t", c=NCH, t=128)

        def fst1(e, xabv=xabv):
            for c in range(NCH):
                i = e.matmul(ps["P2"][0:1, 0:TC], T["ones_c"][:], xabv[:, c, 0:TC],
                             start=(c == 0), stop=(c == NCH - 1))
            return i
        em.stage("pe", fst1, reads=["xab", "ones_c"], writes=["P2"])
        for c in range(NCH):
            sq = T["sqA"][:, (c % 2) * 128:((c % 2) + 1) * 128]
            em.stage("pool", lambda e, c=c, sq=sq, xabv=xabv: e.tensor_mul(
                sq[:, 0:TC], xabv[:, c, 0:TC], xabv[:, c, 0:TC]),
                reads=["xab"], writes=[f"sqt{c % 2}"])
            em.stage("pe", lambda e, c=c, sq=sq: e.matmul(
                ps["P2"][0:1, 512:512 + TC], T["ones_c"][:], sq[:, 0:TC],
                start=(c == 0), stop=(c == NCH - 1)),
                reads=[f"sqt{c % 2}", "ones_c"], writes=["P2"])

        def fst2(e, bump):
            e.tensor_scalar_mul(T["trow"][0:1, 0:TC], ps["P2"][0:1, 0:TC], 1.0 / 1024.0)
            i = e.tensor_scalar_mul(T["trow"][0:1, 512:512 + TC],
                                    ps["P2"][0:1, 512:512 + TC], 1.0 / 1024.0)
            bump(i)
            i = e.tensor_mul(T["trow2"][0:1, 0:TC], T["trow"][0:1, 0:TC],
                             T["trow"][0:1, 0:TC])
            bump(i)
            e.tensor_sub(T["trow2"][0:1, 512:512 + TC], T["trow"][0:1, 512:512 + TC],
                         T["trow2"][0:1, 0:TC])
            return e.tensor_copy(T["mrow"][0:1, 0:TC], T["trow"][0:1, 0:TC])
        em.stage("dve", fst2, reads=["P2"], writes=["trow", "mrow"], steps=3)

        def fst3(e, bump):
            i = e.activation(T["trow2"][0:1, 0:TC], T["trow2"][0:1, 512:512 + TC],
                             AF.Ln, bias=T["ceps"][0:1, :])
            bump(i)
            return e.activation(T["rrow"][0:1, 0:TC], T["trow2"][0:1, 0:TC], AF.Exp,
                                scale=-0.5)
        em.stage("act", fst3, reads=["trow", "ceps"], writes=["rrow"], steps=2)

        def fbc(e):
            e.matmul(ps["P2"][:, 0:TC], T["ones_r"][:], T["mrow"][0:1, 0:TC],
                     start=True, stop=True)
            return e.matmul(ps["P2"][:, 512:512 + TC], T["ones_r"][:],
                            T["rrow"][0:1, 0:TC], start=True, stop=True)
        em.stage("pe", fbc, reads=["mrow", "rrow", "ones_r"], writes=["P2"])

        def fxc(e, bump, xabv=xabv):
            xcv = T["xac"][:].rearrange("p (c t) -> p c t", c=NCH, t=128)
            for c in range(NCH):
                i = e.tensor_sub(T["t1"][:, 0:TC], xabv[:, c, 0:TC], ps["P2"][:, 0:TC])
                bump(i)
                i = e.scalar_tensor_tensor(xcv[:, c, 0:TC], T["t1"][:, 0:TC],
                                           T["g2r_s"][:, c:c + 1],
                                           ps["P2"][:, 512:512 + TC], ALU.mult, ALU.mult)
                if c < NCH - 1:
                    bump(i)
            return i
        em.stage("dve", fxc, reads=["P2", "xab", "g2r_s"], writes=["xac", "t1"],
                 steps=16)
        xcv = T["xac"][:].rearrange("p (c t) -> p c t", c=NCH, t=128)

        for oc in range(NOC):
            pk = "P1" if oc % 2 == 0 else "P2"

            def fh(e, oc=oc, pk=pk, xcv=xcv):
                for c in range(NCH):
                    i = e.matmul(ps[pk][:, 0:TC],
                                 A[:, WF1 + c * DF + oc * 128: WF1 + c * DF + (oc + 1) * 128],
                                 xcv[:, c, 0:TC], start=(c == 0), stop=(c == NCH - 1))
                return i
            em.stage("pe", fh, reads=["xac", "slotW1"], writes=[pk])
            em.stage("act", lambda e, oc=oc, pk=pk: e.activation(
                fva[:, oc, 0:TC], ps[pk][:, 0:TC], AF.Gelu,
                bias=T["cbA_s"][:, oc:oc + 1]),
                reads=[pk, "cbA_s"], writes=["fA"])

        for oc in range(NCH):
            pk = "P1" if oc % 2 == 0 else "P2"

            def f2(e, oc=oc, pk=pk):
                for k in range(NOC):
                    i = e.matmul(ps[pk][:, 0:TC],
                                 A[:, WF2 + k * 1024 + oc * 128: WF2 + k * 1024 + (oc + 1) * 128],
                                 fva[:, k, 0:TC], start=(k == 0), stop=(k == NOC - 1))
                return i
            em.stage("pe", f2, reads=["fA", "slotW2"], writes=[pk])
            em.stage("dve", lambda e, oc=oc, pk=pk, xg2=xg2: e.tensor_add(
                xg2[:, oc, 0:TC], ps[pk][:, 0:TC], xg2[:, oc, 0:TC]),
                reads=[pk, "xag2"], writes=["xag2"])

        em.stage("sy", lambda e, s0=s0, ygv=ygv: e.dma_start(
            out=ygv, in_=yfin[OWN0 + s0].rearrange("c p t -> p c t")),
            reads=[f"ysrc_{nlayers + 1}_{OWN0 + s0}"], writes=["yg"], dma=True)

        em.stage("act", lambda e: e.activation(
            T["yg"][:], T["yg"][:], AF.Silu),
            reads=["yg"], writes=["yg"])

        em.stage("pool", lambda e: e.tensor_mul(
            T["prodb"][:], T["xag2"][:], T["yg"][:]),
            reads=["xag2", "yg"], writes=["prodb"])

        pv = T["prodb"][:].rearrange("p (c t) -> p c t", c=NCH, t=128)

        def ftp(e, pv=pv):
            for c in range(NCH):
                i = e.transpose(ps["trE"][:, ts(c)], pv[:, c], T["identb_s"][:])
            return i
        em.stage("pe", ftp, reads=["prodb", "identb_s"], writes=["trE"])
        em.stage("dve", lambda e: e.tensor_copy(T["prkm"][:], ps["trE"][:]),
                 reads=["trE"], writes=["prkm"])

        em.stage("sy", lambda e, s0=s0: e.dma_start(
            out=T["xkm"][:], in_=xin[PREF + s0 * 128: PREF + (s0 + 1) * 128, :]),
            writes=["xkm"], dma=True)

        em.stage("dve", lambda e: e.tensor_add(
            T["xkm"][:], T["prkm"][:], T["xkm"][:]),
            reads=["prkm", "xkm"], writes=["xkm"])

        em.stage("sy", lambda e, s0=s0: e.dma_start(
            out=dram["out"].ap()[s0 * 128:(s0 + 1) * 128, :], in_=T["xkm"][:]),
            reads=["xkm"], dma=True)



_CACHE = {}


def _host_prep(inputs):
    x = np.ascontiguousarray(inputs["L0_attn_x"])
    f32 = np.float32

    ii = np.arange(128)
    mA = np.where(ii[:, None] > ii[None, :], -BIG, 0).astype(f32)
    mC = np.where(ii[:, None] < ii[None, :], -BIG, 0).astype(f32)
    allm = np.full((128, 128), -BIG, f32)
    zer = np.zeros((128, 128), f32)
    triA = (ii[:, None] <= ii[None, :]).astype(f32)
    identf = np.eye(128, dtype=f32)
    dupI = np.concatenate([np.eye(64, dtype=f32), np.eye(64, dtype=f32)], axis=1)

    half = np.arange(0, DH, 2, dtype=f32)
    inv_freq = (f32(1.0) / f32(10000.0) ** (half / f32(DH))).astype(f32)
    tg = np.arange(16384, dtype=f32)
    freqs = np.concatenate([tg[:, None] * inv_freq[None, :]] * 2, -1).astype(f32)
    base = ((half + f32(0.4 * DH)) / f32(1.4 * DH)).astype(f32)
    power = ((tg - 8192.0) / f32(128.0)).astype(f32)
    scale = np.concatenate([base[None, :] ** power[:, None]] * 2, -1).astype(f32)
    cos, sin = np.cos(freqs).astype(f32), np.sin(freqs).astype(f32)
    sgn = np.concatenate([-np.ones(32, f32), np.ones(32, f32)])
    cs8 = (cos * scale * 8).astype(f32)
    ss8s = (sin * scale * 8 * sgn).astype(f32)
    ci = (cos / scale).astype(f32)
    sis = (sin / scale * sgn).astype(f32)

    W1 = inputs["L0_ffn_W1"]; W2 = inputs["L0_ffn_W2"]; b1 = inputs["L0_ffn_b1"]
    W1p = np.zeros((6, D, 2 * DI), W1.dtype)
    b1p = np.zeros((6, 2 * DI), b1.dtype)
    W2p = np.zeros((6, DI, D), W2.dtype)
    for c in range(NIC):
        lo, hi = c * 128, min((c + 1) * 128, 2730)
        n = hi - lo
        if n <= 0:
            continue
        W1p[:, :, 2 * c * 128: 2 * c * 128 + n] = W1[:, :, lo:hi]
        W1p[:, :, (2 * c + 1) * 128: (2 * c + 1) * 128 + n] = W1[:, :, 2730 + lo: 2730 + hi]
        b1p[:, 2 * c * 128: 2 * c * 128 + n] = b1[:, lo:hi]
        b1p[:, (2 * c + 1) * 128: (2 * c + 1) * 128 + n] = b1[:, 2730 + lo: 2730 + hi]
        W2p[:, c * 128: c * 128 + n, :] = W2[:, lo:hi, :]
    b1pr = b1p.reshape(6, 2 * NIC, 128).transpose(0, 2, 1).copy()

    shared = {
        "rot_tab": (cs8, ss8s, ci, sis),
        "maskA_t": mA.T.astype(bf16).copy(), "maskC_t": mC.T.astype(bf16).copy(),
        "allm_t": allm.astype(bf16), "zer_t": zer.astype(bf16),
        "triA": triA.astype(bf16), "identf": identf,
        "identb": identf.astype(bf16), "dupI": dupI.astype(bf16),
        "Wqkv": inputs["L0_ffn_Wqkv"], "Wo6": inputs["L0_ffn_Wo"],
        "W1p": W1p, "W2p": W2p, "b1pr": b1pr,
        "lnbr": inputs["L0_ffn_ln_b"].reshape(6, NCH, 128).transpose(0, 2, 1).copy(),
        "lngr": inputs["L0_ffn_ln_g"].reshape(6, NCH, 128).transpose(0, 2, 1).copy(),
        "b2r": inputs["L0_ffn_b2"].reshape(6, NCH, 128).transpose(0, 2, 1).copy(),
        "qsrep": np.broadcast_to(inputs["L0_ffn_q_scale"][:, None, :],
                                 (6, 128, DH)).copy(),
        "ksrep": np.broadcast_to(inputs["L0_ffn_k_scale"][:, None, :],
                                 (6, 128, DH)).copy(),
        "Wq": inputs["L0_attn_Wq"], "Wk": inputs["L0_attn_Wk"],
        "Wv": inputs["L0_attn_Wv"], "WoA": inputs["L0_attn_Wo"],
        "Wf1": inputs["L0_attn_Wf1"], "Wf2": inputs["L0_attn_Wf2"],
        "g1t": np.broadcast_to(inputs["L0_attn_ln1_g"][None, :], (128, D)).copy(),
        "b1t": np.broadcast_to(inputs["L0_attn_ln1_b"][None, :], (128, D)).copy(),
        "g2r": inputs["L0_attn_ln2_g"].reshape(NCH, 128).T.copy(),
        "b2ar": inputs["L0_attn_ln2_b"].reshape(NCH, 128).T.copy(),
    }

    maps = []
    for c in range(8):
        b, q = c // 4, c % 4
        s = q * 4096
        xfull = np.zeros((PREF + OWN, D), f32)
        lo = s - PREF
        xfull[max(0, -lo):, :] = x[b, max(0, lo): s + OWN]
        xb = xfull[PREF - HALO:, :].copy()
        gidx = np.clip(np.arange(-HALO, OWN) + s, 0, 16383)
        cs8c, ss8c, cic, sic = (t[gidx] for t in shared["rot_tab"])
        rotc = np.concatenate([cs8c, ss8c, cic, sic], axis=1).reshape(NS, 128, 256)
        pv = np.zeros((128, NPS), f32)
        pv[:, NPS - 32 * q:] = 1.0
        hv = np.ones((128, NS), f32)
        if q == 0:
            hv[:, OWN0] = 0.0
        m0 = shared["allm_t"].T if q == 0 else shared["maskA_t"].T
        m1 = shared["allm_t"].T if q == 0 else shared["zer_t"].T
        m2 = shared["allm_t"].T if q == 0 else shared["maskA_t"].T
        mp = {k: v for k, v in shared.items() if k not in ("rot_tab", "allm_t", "zer_t")}
        mp.update({
            "xin": xfull, "xbuf": xb, "rotcat": np.ascontiguousarray(rotc),
            "pvalidT": pv, "hvalrow": hv,
            "m0_t": np.ascontiguousarray(m0.T), "m1_t": np.ascontiguousarray(m1.T),
            "m2_t": np.ascontiguousarray(m2.T),
        })
        maps.append(mp)
    return maps


def kernel(**inputs):
    maps = _host_prep(inputs)
    if "nc" not in _CACHE:
        _CACHE["nc"] = build_program()
    nc = _CACHE["nc"]
    res = run_bass_kernel_spmd(nc, maps, core_ids=list(range(8)))
    out = np.empty((2, 16384, D), np.float32)
    for c in range(8):
        b, q = c // 4, c % 4
        out[b, q * 4096:(q + 1) * 4096] = res.results[c]["out"]
    return out


# version 08
# speedup vs baseline: 1.1762x; 1.1398x over previous
import numpy as np
import ml_dtypes
import concourse.bass as bass
import concourse.mybir as mybir
from concourse.bass_utils import run_bass_kernel_spmd

F32 = mybir.dt.float32
BF16 = mybir.dt.bfloat16
AF = mybir.ActivationFunctionType
ALU = mybir.AluOpType
AX = mybir.AxisListType
bf16 = ml_dtypes.bfloat16

D = 1024
NCH = 8
H = 16
DH = 64
OWN = 4096
HALO = 2432
BUF = HALO + OWN
NS = BUF // 128
OWN0 = HALO // 128
PREF = 12288
NPS = PREF // 128
DI = 2816
NIC = DI // 128
DF = 4096
NOC = DF // 128
BIG = np.float32(3.0e38)


def ts(i, n=128):
    return slice(i * n, (i + 1) * n)


class EM:

    ENGS = ("pe", "dve", "act", "pool", "sy")
    NSYD = 10
    NGPD = 8
    TAGS = (("pe", "dve", "act", "pool", "sy")
            + tuple(f"syd{i}" for i in range(10))
            + tuple(f"gpd{i}" for i in range(8)))

    def __init__(self):
        self.stages = {e: [] for e in self.ENGS}
        self.cnt = {t: 0 for t in self.TAGS}
        self.seen = {e: {} for e in self.ENGS}
        self.buf = {}
        self.dma_rr = {"syd": 0, "gpd": 0}

    def _b(self, k):
        if k not in self.buf:
            self.buf[k] = {"w": [], "r": {}}
        return self.buf[k]

    def stage(self, e, fn, reads=(), writes=(), accw=(), dma=False, steps=1):
        waits = {}
        seen = self.seen[e]

        def need(tag, val):
            if seen.get(tag, 0) < val:
                if waits.get(tag, 0) < val:
                    waits[tag] = val

        rdkeys = list(reads)
        for k in reads:
            b = self._b(k)
            for wt, wv in b["w"]:
                need(wt, wv)
                if wt.startswith(("syd", "gpd")):
                    rdkeys.append("SEM_" + wt)
        for k in writes:
            b = self._b(k)
            for wt, wv in b["w"]:
                need(wt, wv)
            for t, v in b["r"].items():
                need(t, v)
        if dma:
            q = "syd" if e == "sy" else "gpd"
            n = self.NSYD if q == "syd" else self.NGPD
            tag = f"{q}{self.dma_rr[q] % n}"
            self.dma_rr[q] += 1
            inc = 16
            assert steps == 1
            b = self._b("SEM_" + tag)
            for wt, wv in b["w"]:
                need(wt, wv)
            for t, v in b["r"].items():
                need(t, v)
        else:
            tag = e
            inc = steps
        for t, v in waits.items():
            seen[t] = v
        self.cnt[tag] += inc
        v = self.cnt[tag]
        if dma:
            b = self._b("SEM_" + tag)
            b["w"] = [(tag, v)]
            b["r"] = {}
        for k in writes:
            b = self._b(k)
            b["w"] = [(tag, v)]
            b["r"] = {}
        for k in accw:
            b = self._b(k)
            b["w"].append((tag, v))
        for k in rdkeys:
            self._b(k)["r"][tag] = v
        self.stages[e].append((sorted(waits.items()), fn, tag, inc, steps))

    def alias(self, dst, src):
        self.buf[dst] = self._b(src)

    def barrier(self):
        snap = dict(self.cnt)
        for e in self.ENGS:
            waits = sorted((t, v) for t, v in snap.items() if v > 0 and t != e)
            seen = self.seen[e]
            for t, v in waits:
                if seen.get(t, 0) < v:
                    seen[t] = v
            waits = [(t, v) for t, v in waits]
            self.cnt[e] += 1
            self.stages[e].append((waits, lambda eng: eng.nop(), e, 1, 1))

    def replay(self, nc):
        sems = {}
        ctxs = []
        for t in self.TAGS:
            h = nc.semaphore(f"s_{t}")
            sems[t] = h.__enter__()
            ctxs.append(h)
        block = nc.Block().__enter__()
        engmap = {"pe": "tensor", "dve": "vector", "act": "scalar",
                  "pool": "gpsimd", "sy": "sync"}
        rcnt = {t: 0 for t in self.TAGS}
        for e in self.ENGS:
            def body(eng, e=e):
                for waits, fn, tag, inc, steps in self.stages[e]:
                    for t, v in waits:
                        eng.wait_ge(sems[t], v)
                    if steps == 1:
                        inst = fn(eng)
                        inst.then_inc(sems[tag], inc)
                    else:
                        base = rcnt[tag]
                        nb = [0]

                        def bump(pinst):
                            nb[0] += 1
                            pinst.then_inc(sems[tag], 1)
                            eng.wait_ge(sems[tag], base + nb[0])
                        inst = fn(eng, bump)
                        assert nb[0] == steps - 1, (e, nb[0], steps)
                        inst.then_inc(sems[tag], 1)
                    rcnt[tag] += inc
                if e == "sy":
                    for t in self.TAGS:
                        if t.startswith(("syd", "gpd")) and self.cnt[t] > 0:
                            eng.wait_ge(sems[t], self.cnt[t])
            getattr(block, engmap[e])(body)
        block.__exit__(None, None, None)
        for h in ctxs:
            h.__exit__(None, None, None)


def build_program(nlayers=6, do_a=True, dumps=(), bffn=True, dbg_sub=None, dbg_suba=None):
    nc = bass.Bass("TRN2", target_bir_lowering=False)
    dram = {}

    def din(name, shape, dt=F32):
        dram[name] = nc.dram_tensor(name, list(shape), dt, kind="ExternalInput")

    din("xin", [PREF + OWN, D])
    din("xbuf", [BUF, D])
    din("rotcat", [NS, 128, 256])
    din("pvalidT", [128, NPS])
    din("hvalrow", [128, NS])
    din("mh_t", [128, 384], BF16)
    din("mh0_t", [128, 384], BF16)
    din("mh1_t", [128, 384], BF16)
    din("triA", [128, 128], BF16)
    din("identf", [128, 128])
    din("identb", [128, 128], BF16)
    din("dupI", [64, 128], BF16)
    din("Wqkv", [6, D, 3 * D]); din("Wo6", [6, D, D])
    din("W1p", [6, D, 2 * DI]); din("W2p", [6, DI, D])
    din("b1pr", [6, 128, 2 * NIC])
    din("lnbr", [6, 128, NCH])
    din("lngr", [6, 128, NCH]); din("b2r", [6, 128, NCH])
    din("qsrep", [6, 128, D]); din("ksrep", [6, 128, D])
    din("Wq", [D, D]); din("Wk", [D, D]); din("Wv", [D, D]); din("WoA", [D, D])
    din("Wf1", [D, DF]); din("Wf2", [DF, D])
    din("g1t", [128, D]); din("b1t", [128, D])
    din("g2r", [128, NCH]); din("b2ar", [128, NCH])
    dram["out"] = nc.dram_tensor("out", [OWN, D], F32, kind="ExternalOutput")
    dram["ysc0"] = nc.dram_tensor("ysc0", [NS, NCH, 128, 128], F32)
    dram["ysc1"] = nc.dram_tensor("ysc1", [NS, NCH, 128, 128], F32)
    dram["xasc"] = nc.dram_tensor("xasc", [32, NCH, 128, 128], F32)
    dram["cvec"] = nc.dram_tensor("cvec", [DF], F32)
    for d in dumps:
        nm, src = d[0], d[1]
        shp = list(dram[src].shape)
        dram[nm] = nc.dram_tensor(nm, shp, F32, kind="ExternalOutput")

    sbuf = {}

    def S(name, shape, dt=F32):
        sbuf[name] = nc.alloc_sbuf_tensor(name, list(shape), dt).ap()

    S("identf_s", [128, 128]); S("identb_s", [128, 128], BF16)
    S("mh_s", [128, 384], BF16); S("mh0_s", [128, 384], BF16)
    S("mh1_s", [128, 384], BF16)
    S("triA_s", [128, 128], BF16); S("dupI_s", [64, 128], BF16)
    S("ones_c", [128, 1], BF16); S("ones_r", [1, 128], BF16)
    S("pval_s", [128, NPS]); S("hval_s", [128, NS])
    S("rot", [128, 2 * 256])
    S("qs_s", [128, D], BF16); S("ks_s", [128, D], BF16)
    S("lng_s", [128, NCH]); S("lnb_s", [128, NCH], BF16); S("b2r_s", [128, NCH])
    S("b1p_s", [128, 2 * NIC]); S("cb1_s", [128, 2 * NIC])
    S("g2r_s", [128, NCH]); S("b2a_s", [128, NCH], BF16); S("cbA_s", [128, NOC])
    S("trow", [1, 896]); S("trow2", [1, 896])
    S("mrow", [1, 512], BF16); S("rrow", [1, 512], BF16)
    S("ssum", [128, H]); S("rn", [128, H]); S("tt", [128, H])
    S("nmx", [128, H]); S("rsb", [128, H])
    S("den_t", [128, H]); S("rA", [128, H])
    S("ycar", [128, 4])
    S("ceps", [128, 1])
    AR = nc.alloc_sbuf_tensor("arena", [128, 67584], BF16).ap()
    WK = nc.alloc_sbuf_tensor("warena", [128, 26624], BF16).ap()
    sbuf["arena"] = AR

    def view(base, byte_off, dt, ncols, npart=128):
        assert byte_off % 2 == 0
        c0 = byte_off // 2
        if dt == BF16:
            ap = base[0:npart, c0:c0 + ncols]
        else:
            ap = base[0:npart, c0:c0 + 2 * ncols].bitcast(F32)
        return ap

    sbuf["ytile"] = view(AR, 49152, F32, 8 * NCH * 128)
    sbuf["afm"] = view(AR, 81920, BF16, NCH * 512)
    sbuf["kfm"] = view(AR, 106496, BF16, 4 * D)
    sbuf["vkm"] = view(AR, 114688, BF16, 4 * D)
    sbuf["Pt"] = view(AR, 122880, BF16, 2 * 4 * 384)
    sbuf["ptsb"] = view(AR, 129024, BF16, 2 * 12 * 128)
    sbuf["xag"] = view(AR, 49152, F32, NCH * 512)
    sbuf["aAfm"] = view(AR, 65536, BF16, NCH * 512)
    sbuf["xld"] = view(AR, 73728, F32, 2 * D)
    sbuf["xnfm"] = view(AR, 81920, F32, NCH * 128)
    sbuf["xnt"] = view(AR, 86016, F32, D)
    sbuf["Qfm"] = view(AR, 106496, BF16, D)
    sbuf["Kfm"] = view(AR, 108544, BF16, D)
    sbuf["Kkm"] = view(AR, 110592, BF16, D)
    sbuf["Vkm2"] = view(AR, 112640, BF16, D)
    sbuf["h1b"] = view(AR, 114688, BF16, D)
    sbuf["h1fm"] = view(AR, 116736, BF16, NCH * 128)
    sbuf["aAkm"] = view(AR, 118784, BF16, D)
    sbuf["ATs"] = view(AR, 120832, BF16, 2 * 128)
    sbuf["SbD"] = view(AR, 121344, BF16, H * 64)
    sbuf["zbD"] = view(AR, 123392, BF16, H)
    sbuf["Sacc"] = view(AR, 123456, F32, H * 64, npart=64)
    sbuf["zacc"] = view(AR, 127552, F32, H, npart=64)
    sbuf["SaccB"] = view(AR, 127616, BF16, H * 64, npart=64)
    sbuf["zaccB"] = view(AR, 129664, BF16, H, npart=64)
    sbuf["prodb"] = view(AR, 131072, BF16, D)
    sbuf["qA"] = view(WK, 0, F32, 2 * D)
    sbuf["kA"] = view(WK, 8192, F32, 2 * D)
    sbuf["t1"] = view(WK, 16384, F32, D)
    sbuf["t2"] = view(WK, 20480, F32, D)
    sbuf["t3"] = view(WK, 24576, F32, D)
    sbuf["qD"] = view(WK, 28672, BF16, D)
    sbuf["kD"] = view(WK, 30720, BF16, D)
    sbuf["qfm"] = view(WK, 32768, BF16, D)
    sbuf["ubf"] = view(WK, 34816, BF16, 2 * NCH * 128)
    sbuf["akm"] = view(WK, 38912, BF16, D)
    sbuf["ymg"] = view(WK, 0, F32, 2 * NCH * 384)
    sbuf["u2g"] = view(WK, 24576, BF16, NCH * 384)
    sbuf["fsb"] = view(WK, 30720, BF16, NIC * 384)
    sbuf["sqt"] = view(WK, 47616, BF16, 2 * 384)
    sbuf["tf1"] = view(WK, 49152, F32, 384)
    sbuf["tf2"] = view(WK, 50688, F32, 384)
    sbuf["g1t_s"] = view(WK, 40960, F32, D)
    sbuf["b1t_s"] = view(WK, 45056, F32, D)
    sbuf["xag2"] = view(WK, 28672, F32, NCH * 128)
    sbuf["yg"] = view(WK, 32768, F32, NCH * 128)
    sbuf["xkm"] = view(WK, 36864, F32, D)
    sbuf["fA"] = view(WK, 40960, BF16, NOC * 128)
    sbuf["xab"] = view(WK, 49152, BF16, NCH * 128)
    sbuf["xac"] = view(WK, 51200, BF16, NCH * 128)
    sbuf["prkm"] = view(WK, 20480, BF16, D)
    sbuf["sqA"] = view(WK, 22528, BF16, 2 * 128)
    sbuf["ssb"] = sbuf["xag2"]
    sbuf["gsb"] = sbuf["yg"]
    sbuf["outk"] = sbuf["xkm"]

    ps = {}
    ps["P1"] = nc.alloc_psum_tensor("psP1", [128, 1536], F32)
    ps["P2"] = nc.alloc_psum_tensor("psP2", [128, 1536], F32)
    ps["trE"] = nc.alloc_psum_tensor("psE", [128, 1024], BF16)
    ps["trF"] = nc.alloc_psum_tensor("psF", [128, 1024], BF16)

    em = EM()
    st = dict(em=em, T=sbuf, ps=ps, dram=dram)
    st["bffn"] = bffn
    st["dbg_sub"] = dbg_sub
    st["dbg_suba"] = dbg_suba
    if dbg_suba is not None:
        for nm, shp in (("daA", [128, D]), ("dQf", [128, D]), ("dKf", [128, D]),
                        ("dVk", [128, D]), ("dSb", [128, H * 64]), ("dzb", [128, H])):
            dram[nm] = nc.dram_tensor(nm, shp, F32, kind="ExternalOutput")
    if dbg_sub is not None:
        for nm, cols in (("dakm", D), ("dqfm", D), ("dkfm", 3 * D), ("dvkm", 3 * D)):
            dram[nm] = nc.dram_tensor(nm, [128, cols], F32, kind="ExternalOutput")
    _p_consts(st)
    _p_layer0(st)
    for l in range(1, nlayers + 1):
        _p_blayer(st, l)
    if do_a:
        _p_abranch(st, nlayers)
    for d in dumps:
        _p_dump(st, d[0], d[1], d[2] if len(d) > 2 else ())
    em.replay(nc)
    return nc


def _p_consts(st):
    em, T, dram = st["em"], st["T"], st["dram"]
    for dst, src in (("identf_s", "identf"), ("identb_s", "identb"),
                     ("mh_s", "mh_t"), ("mh0_s", "mh0_t"), ("mh1_s", "mh1_t"),
                     ("triA_s", "triA"), ("pval_s", "pvalidT"), ("hval_s", "hvalrow"),
                     ("g2r_s", "g2r")):
        em.stage("sy", lambda e, d=dst, s=src: e.dma_start(out=T[d][:], in_=dram[s].ap()),
                 writes=[dst], dma=True)
    em.stage("pool", lambda e: e.dma_start(out=T["dupI_s"][:], in_=dram["dupI"].ap()),
             writes=["dupI_s"], dma=True)
    em.stage("pool", lambda e: e.dma_start(out=T["b2a_s"][:], in_=dram["b2ar"].ap()),
             writes=["b2a_s"], dma=True)
    em.stage("pool", lambda e: e.memset(T["ones_c"][:], 1.0), writes=["ones_c"])
    em.stage("pool", lambda e: e.memset(T["ones_r"][:], 1.0), writes=["ones_r"])
    em.stage("pool", lambda e: e.memset(T["ceps"][:], 1e-5), writes=["ceps"])


def _p_dump(st, nm, src, keys=()):
    em, dram = st["em"], st["dram"]
    em.stage("sy", lambda e: e.dma_start(out=dram[nm].ap().flatten(),
                                         in_=dram[src].ap().flatten()),
             reads=list(keys), dma=True)


def _p_ln(st, src_t, dst_t, ksrc, kdst, g=None, b=None):
    em, T = st["em"], st["T"]
    su, tt, rn = T["ssum"], T["tt"], T["rn"]

    em.stage("pool", lambda e: e.tensor_mul(T["t3"][:], src_t, src_t),
             reads=[ksrc], writes=["t3"])

    def f1(e, bump):
        e.reduce_sum(su[:, 0:1].unsqueeze(1), src_t.unsqueeze(1), AX.X)
        i = e.reduce_sum(su[:, 1:2].unsqueeze(1), T["t3"][:].unsqueeze(1), AX.X)
        bump(i)
        e.tensor_scalar_mul(su[:, 2:3], su[:, 0:1], 1.0 / 1024.0)
        i = e.tensor_scalar_mul(su[:, 3:4], su[:, 1:2], 1.0 / 1024.0)
        bump(i)
        i = e.tensor_mul(tt[:, 0:1], su[:, 2:3], su[:, 2:3])
        bump(i)
        return e.tensor_sub(tt[:, 1:2], su[:, 3:4], tt[:, 0:1])
    em.stage("dve", f1, reads=[ksrc, "t3"], writes=["lnstat"], steps=4)

    def f2(e, bump):
        i = e.activation(tt[:, 2:3], tt[:, 1:2], AF.Ln, bias=T["ceps"][:])
        bump(i)
        return e.activation(rn[:, 0:1], tt[:, 2:3], AF.Exp, scale=-0.5)
    em.stage("act", f2, reads=["lnstat", "ceps"], writes=["lnr"], steps=2)

    def f3(e, bump):
        i = e.tensor_scalar_mul(tt[:, 3:4], su[:, 2:3], -1.0)
        bump(i)
        return e.tensor_scalar(dst_t, src_t, tt[:, 3:4], rn[:, 0:1], ALU.add, ALU.mult)
    em.stage("dve", f3, reads=["lnstat", "lnr", ksrc], writes=[kdst], steps=2)

    if g is not None:
        def f4(e, bump):
            i = e.tensor_mul(T["t3"][:], dst_t, g)
            bump(i)
            return e.tensor_add(dst_t, T["t3"][:], b)
        em.stage("pool", f4, reads=[kdst, "g1t_s", "b1t_s"], writes=[kdst, "t3"],
                 steps=2)


def _p_transpose8(st, src_t, kin, pkey, dt="b"):
    em, T, ps = st["em"], st["T"], st["ps"]
    ident = T["identb_s"] if dt == "b" else T["identf_s"]
    ikey = "identb_s" if dt == "b" else "identf_s"

    def f(e):
        for c in range(NCH):
            i = e.transpose(ps[pkey][:, ts(c)], src_t[:, ts(c)], ident[:])
        return i
    em.stage("pe", f, reads=[kin, ikey], writes=[pkey])


def _p_layer0(st):
    em, T, ps, dram = st["em"], st["T"], st["ps"], st["dram"]
    xbuf = dram["xbuf"].ap()
    y0, y1 = dram["ysc0"].ap(), dram["ysc1"].ap()
    for m in range(NS):
        slot = m % 2
        xl = T["xld"][:, slot * D:(slot + 1) * D]
        em.stage("sy", lambda e, m=m, xl=xl: e.dma_start(out=xl, in_=xbuf[ts(m), :]),
                 writes=[f"xld{slot}"], dma=True)
        _p_ln(st, xl[:, :], T["xnt"][:, :], f"xld{slot}", "xnt")
        _p_transpose8(st, T["xnt"][:], "xnt", "P1", dt="f")
        em.stage("dve", lambda e: e.tensor_copy(T["xnfm"][:], ps["P1"][:, 0:1024]),
                 reads=["P1"], writes=["xnfm"])
        em.stage("sy", lambda e, m=m: e.dma_start(
            out=y0[m].rearrange("c p t -> p c t"),
            in_=T["xnfm"][:].rearrange("p (c t) -> p c t", c=NCH)),
            reads=["xnfm"], writes=[f"ysrc_1_{m}"], dma=True)
        if m < OWN0:
            em.stage("sy", lambda e, m=m: e.dma_start(
                out=y1[m].rearrange("c p t -> p c t"),
                in_=T["xnfm"][:].rearrange("p (c t) -> p c t", c=NCH)),
                reads=["xnfm"], writes=[f"y1init_{m}"], dma=True)


def _p_blayer(st, l):
    em, T, ps, dram = st["em"], st["T"], st["ps"], st["dram"]
    em.barrier()
    G = 3 * l + 1
    kv0 = G - 3
    q0 = G - 1
    ysrc = dram["ysc0" if l % 2 == 1 else "ysc1"].ap()
    ydst = dram["ysc1" if l % 2 == 1 else "ysc0"].ap()
    A = T["arena"]
    WQKV, WO = 0, 45056
    yt = T["ytile"][:].rearrange("p (s c t) -> p s c t", s=8, c=NCH)

    for dst, src in (("lng_s", "lngr"), ("b2r_s", "b2r"), ("b1p_s", "b1pr")):
        em.stage("sy", lambda e, d=dst, s=src: e.dma_start(
            out=T[d][:], in_=dram[s].ap()[l - 1]), writes=[dst], dma=True)
    for dst, src in (("qs_s", "qsrep"), ("ks_s", "ksrep")):
        em.stage("pool", lambda e, d=dst, s=src: e.dma_start(
            out=T[d][:], in_=dram[s].ap()[l - 1]), writes=[dst], dma=True)
    em.stage("pool", lambda e: e.dma_start(
        out=T["lnb_s"][:], in_=dram["lnbr"].ap()[l - 1]), writes=["lnb_s"], dma=True)

    wq, wo = dram["Wqkv"].ap()[l - 1], dram["Wo6"].ap()[l - 1]
    for c in range(NCH):
        em.stage("pool", lambda e, c=c: e.dma_start(
            out=A[:, WQKV + c * 3072: WQKV + (c + 1) * 3072], in_=wq[ts(c), :]),
            writes=["slotW1"] if c == 0 else [], accw=["slotW1"] if c > 0 else [],
            dma=True)
    for c in range(NCH):
        em.stage("pool", lambda e, c=c: e.dma_start(
            out=A[:, WO + c * 1024: WO + (c + 1) * 1024], in_=wo[ts(c), :]),
            writes=["slotW2"] if c == 0 else [], accw=["slotW2"] if c > 0 else [],
            dma=True)

    def load_y(mm):
        if mm >= NS:
            return
        sl = (mm - G) % 8
        if mm == kv0 - 1 and l > 1:
            key = f"ymid_{l - 1}_{mm}"
        else:
            key = f"ysrc_{l}_{mm}"
        em.stage("sy", lambda e, mm=mm, sl=sl: e.dma_start(
            out=yt[:, sl],
            in_=ysrc[mm].rearrange("c p t -> p c t")),
            reads=[key], writes=[f"yt{sl}"], dma=True)

    def build_u(mm):
        if mm >= NS:
            return
        ysl = (mm - G) % 8
        psl = (mm - 1 - G) % 8
        usl = mm % 2
        ub = T["ubf"][:, usl * 1024:(usl + 1) * 1024].rearrange("p (c t) -> p c t", c=NCH)

        def fu(e, ub=ub, ysl=ysl, psl=psl, mm=mm):
            e.tensor_copy(ub[:, 0:4], yt[:, ysl, 0:4])
            e.tensor_copy(ub[:, 4:8, 1:128], yt[:, ysl, 4:8, 0:127])
            return e.tensor_scalar_mul(ub[:, 4:8, 0:1], yt[:, psl, 4:8, 127:128],
                                       T["hval_s"][:, mm:mm + 1])
        em.stage("pool", fu, reads=[f"yt{ysl}", f"yt{psl}", "hval_s"],
                 writes=[f"ubf{usl}"])

    def emit_rot(mm):
        if mm >= NS:
            return
        us = mm % 2
        em.stage("pool", lambda e, mm=mm, us=us: e.dma_start(
            out=T["rot"][:, us * 256:(us + 1) * 256], in_=dram["rotcat"].ap()[mm]),
            writes=[f"rot{us}"], dma=True)

    def emit_qkv_part(mm, part):
        if mm >= NS:
            return
        us = mm % 2
        ubv = T["ubf"][:, us * 1024:(us + 1) * 1024].rearrange("p (c t) -> p c t",
                                                              c=NCH)
        off = 0 if part == "q" else 1536

        def fmm(e, off=off, ubv=ubv):
            for c in range(NCH):
                for sl in range(3):
                    i = e.matmul(ps["P1"][:, sl * 512:(sl + 1) * 512],
                                 ubv[:, c],
                                 A[:, WQKV + c * 3072 + off + sl * 512:
                                    WQKV + c * 3072 + off + (sl + 1) * 512],
                                 start=(c == 0), stop=(c == NCH - 1))
            return i
        em.stage("pe", fmm, reads=[f"ubf{us}", "slotW1"], writes=["P1"])
        if part == "q":
            em.stage("dve", lambda e, us=us: e.tensor_copy(
                T["qA"][:, us * D:(us + 1) * D], ps["P1"][:, 0:1024]),
                reads=["P1"], writes=[f"qA{us}"])
            em.stage("act", lambda e, us=us: e.copy(
                T["kA"][:, us * D:us * D + 512], ps["P1"][:, 1024:1536]),
                reads=["P1"], writes=[f"kAlo{us}"])
        else:
            em.stage("act", lambda e, us=us: e.copy(
                T["kA"][:, us * D + 512:(us + 1) * D], ps["P1"][:, 0:512]),
                reads=["P1"], writes=[f"kA{us}"])
            vkmr = T["vkm"][:, (mm % 4) * D:((mm % 4) + 1) * D]
            em.stage("dve", lambda e, v=vkmr, mm=mm: e.tensor_copy(
                v[:], ps["P1"][:, 512:1536]), reads=["P1"], writes=[f"vkm{mm % 4}"])

    def emit_chain(m, which):
        us = m % 2
        rot = T["rot"][:, us * 256:(us + 1) * 256]
        sk = "qA" if which == "q" else "kA"
        skkey = f"{sk}{us}"
        skeys = [skkey] + ([f"kAlo{us}"] if which == "k" else [])
        dst = "qD" if which == "q" else "kD"
        sct = "qs_s" if which == "q" else "ks_s"
        coff = 0 if which == "q" else 128
        soff = 64 if which == "q" else 192
        src_t = T[sk][:, us * D:(us + 1) * D]
        seg = src_t.rearrange("p (h d) -> p h d", h=H)

        em.stage("pool", lambda e, s=src_t: e.tensor_mul(T["t3"][:], s, s),
                 reads=skeys, writes=["t3"])

        def fr(e, bump):
            i = e.reduce_sum(T["ssum"][:].unsqueeze(2),
                             T["t3"][:].rearrange("p (h d) -> p h d", h=H), AX.X)
            bump(i)
            return e.tensor_scalar_max(T["ssum"][:], T["ssum"][:], 1e-24)
        em.stage("dve", fr, reads=["t3"], writes=["ssum"], steps=2)

        def fe(e, bump):
            i = e.activation(T["tt"][:], T["ssum"][:], AF.Ln)
            bump(i)
            return e.activation(T["rn"][:], T["tt"][:], AF.Exp, scale=-0.5)
        em.stage("act", fe, reads=["ssum"], writes=["rn"], steps=2)

        def fm1(e, seg=seg):
            return e.tensor_tensor(
                T["t1"][:].rearrange("p (h d) -> p h d", h=H), seg,
                T["rn"][:].unsqueeze(2).broadcast_to([128, H, DH]), ALU.mult)
        em.stage("dve", fm1, reads=skeys + ["rn"], writes=["t1"])

        def fm2(e, bump, sct=sct, coff=coff, rot=rot):
            t1s = T["t1"][:].rearrange("p (h d) -> p h d", h=H)
            i = e.tensor_tensor(T["t1"][:], T["t1"][:], T[sct][:], ALU.mult)
            bump(i)
            return e.tensor_tensor(
                T["t2"][:].rearrange("p (h d) -> p h d", h=H), t1s,
                rot[:, coff:coff + 64].unsqueeze(1).broadcast_to([128, H, DH]),
                ALU.mult)
        em.stage("pool", fm2, reads=["t1", sct, f"rot{us}"],
                 writes=["t1", "t2"], steps=2)

        def fm3(e, bump, dst=dst, soff=soff, rot=rot):
            sg = T[dst][:].rearrange("p (h d) -> p h d", h=H)
            t1s = T["t1"][:].rearrange("p (h d) -> p h d", h=H)
            t2s = T["t2"][:].rearrange("p (h d) -> p h d", h=H)
            t3s = T["t3"][:].rearrange("p (h d) -> p h d", h=H)
            e.tensor_tensor(t3s[:, :, 0:32], t1s[:, :, 32:64],
                            rot[:, soff:soff + 32].unsqueeze(1).broadcast_to([128, H, 32]),
                            ALU.mult)
            i = e.tensor_tensor(t3s[:, :, 32:64], t1s[:, :, 0:32],
                                rot[:, soff + 32:soff + 64].unsqueeze(1).broadcast_to([128, H, 32]),
                                ALU.mult)
            bump(i)
            return e.tensor_add(sg, t2s, t3s)
        em.stage("dve", fm3, reads=["t1", "t2", f"rot{us}"],
                 writes=[dst, "t3"], steps=2)

    def collect(emitfn):
        rec = []
        real = em.stage
        em.stage = lambda *aa, **kk: rec.append((aa, kk))
        try:
            emitfn()
        finally:
            em.stage = real
        return rec

    def flush(rec):
        for aa, kk in rec:
            em.stage(*aa, **kk)

    def zip_flush(wv, ch, ratio=2):
        wi = ci = 0
        while wi < len(wv) or ci < len(ch):
            for _ in range(ratio):
                if wi < len(wv):
                    aa, kk = wv[wi]; wi += 1
                    em.stage(*aa, **kk)
            if ci < len(ch):
                aa, kk = ch[ci]; ci += 1
                em.stage(*aa, **kk)

    for m in range(kv0, NS):
        yslot = (m - G) % 8
        if m == kv0:
            for mm in (m - 1, m, m + 1, m + 2):
                load_y(mm)
            build_u(m)
            emit_rot(m)
            emit_qkv_part(m, "q")
            emit_qkv_part(m, "kv")
            emit_chain(m, "q")
            emit_chain(m, "k")
        else:
            load_y(m + 2)
        build_u(m + 1)
        emit_rot(m + 1)

        uslot = m % 2
        kfmr = T["kfm"][:, (m % 4) * D:((m % 4) + 1) * D]

        emit_qkv_part(m + 1, "q")
        emit_qkv_part(m + 1, "kv")

        _p_transpose8(st, T["qD"][:], "qD", "trE")
        em.stage("dve", lambda e: e.tensor_copy(T["qfm"][:], ps["trE"][:]),
                 reads=["trE"], writes=["qfm"])
        _p_transpose8(st, T["kD"][:], "kD", "trF")
        em.stage("act", lambda e, k=kfmr: e.copy(k[:], ps["trF"][:]),
                 reads=["trF"], writes=[f"kfm{m % 4}"])

        chain_rec = collect(lambda: (emit_chain(m + 1, "q"),
                                     emit_chain(m + 1, "k"))) if m + 1 < NS else []

        if m < q0:
            flush(chain_rec)
            continue

        def wv_parts(w):
            pkey = "P1" if w % 2 == 0 else "P2"
            pw = ps[pkey]
            wslot = w % 2
            return pkey, pw, wslot

        def wv_sim(w, m=m):
            pkey, pw, wslot = wv_parts(w)
            kreads = ["qfm"] + [f"kfm{(m - 2 + kc) % 4}" for kc in range(3)]

            def fsim(e, w=w, pw=pw, m=m):
                for hi in range(4):
                    h = 4 * w + hi
                    pb = (h % 2) * 64
                    pc = (h // 2) * 128

                    def kv(kc):
                        return T["kfm"][:, ((m - 2 + kc) % 4) * D:
                                        (((m - 2 + kc) % 4) + 1) * D][pb:pb + 64,
                                                                     pc:pc + 128]
                    qv = T["qfm"][pb:pb + 64, pc:pc + 128]
                    if hi > 0:
                        e.ldweights(T["identb_s"][:])
                    for kc in range(3):
                        i = e.matmul(pw[:, hi * 384 + kc * 128: hi * 384 + (kc + 1) * 128],
                                     qv, kv(kc), start=True, stop=True)
                return i
            em.stage("pe", fsim, reads=kreads, writes=[pkey])

        def wv_maxexp(w):
            pkey, pw, wslot = wv_parts(w)
            em.stage("dve", lambda e, w=w, pw=pw: e.reduce_max(
                T["nmx"][:, 4 * w:4 * w + 4].unsqueeze(2),
                pw[:, 0:1536].rearrange("p (h x) -> p h x", h=4), AX.X, negate=True),
                reads=[pkey], writes=["nmx"])

            def fexp(e, w=w, pw=pw, wslot=wslot):
                for hi in range(4):
                    i = e.activation(T["Pt"][:, wslot * 1536:(wslot + 1) * 1536][:, hi * 384:(hi + 1) * 384],
                                     pw[:, hi * 384:(hi + 1) * 384], AF.Exp,
                                     bias=T["nmx"][:, 4 * w + hi:4 * w + hi + 1])
                return i
            em.stage("act", fexp, reads=[pkey, "nmx"], writes=[f"Pt{wslot}"])

        def wv_ptav(w, m=m):
            pkey, pw, wslot = wv_parts(w)
            for half in range(2):
                tk = "trE" if half == 0 else "trF"

                def ftr(e, half=half, wslot=wslot, tk=tk):
                    for j in range(6):
                        i = e.transpose(ps[tk][:, j * 128:(j + 1) * 128],
                                        T["Pt"][:, wslot * 1536:(wslot + 1) * 1536][:, half * 768 + j * 128:
                                                       half * 768 + (j + 1) * 128],
                                        T["identb_s"][:])
                    return i
                em.stage("pe", ftr, reads=[f"Pt{wslot}", "identb_s"], writes=[tk])
                mh = (T["mh0_s"] if m == OWN0 else
                      (T["mh1_s"] if m == OWN0 + 1 else T["mh_s"]))

                def fev(e, half=half, wslot=wslot, tk=tk, mh=mh):
                    pb = T["ptsb"][:, wslot * 1536:(wslot + 1) * 1536][:, half * 768:(half + 1) * 768]
                    for seg in range(2):
                        i = e.tensor_tensor(pb[:, seg * 384:(seg + 1) * 384],
                                            ps[tk][:, seg * 384:(seg + 1) * 384],
                                            mh[:], ALU.mult)
                    return i
                em.stage("dve", fev, reads=[tk, "mh_s", "mh0_s", "mh1_s"],
                         writes=[f"ptsb{wslot}"])

            vreads = [f"ptsb{wslot}"] + [f"vkm{(m - 2 + kc) % 4}" for kc in range(3)]

            def fav(e, w=w, pw=pw, wslot=wslot, m=m):
                for hi in range(4):
                    for kc in range(3):
                        pt = T["ptsb"][:, wslot * 1536:(wslot + 1) * 1536][:, (hi * 3 + kc) * 128:(hi * 3 + kc + 1) * 128]
                        vk = T["vkm"][:, ((m - 2 + kc) % 4) * D:(((m - 2 + kc) % 4) + 1) * D][:].rearrange(
                            "p (h d) -> p h d", h=H)[:, 4 * w + hi]
                        e.matmul(pw[:, hi * 64:(hi + 1) * 64], pt, vk,
                                 start=(kc == 0), stop=(kc == 2))
                    for kc in range(3):
                        pt = T["ptsb"][:, wslot * 1536:(wslot + 1) * 1536][:, (hi * 3 + kc) * 128:(hi * 3 + kc + 1) * 128]
                        i = e.matmul(pw[:, 256 + hi:256 + hi + 1], pt, T["ones_c"][:],
                                     start=(kc == 0), stop=(kc == 2))
                return i
            em.stage("pe", fav, reads=vreads + ["ones_c"], writes=[pkey])

            def fdiv(e, bump, w=w, pw=pw):
                i = e.reciprocal(T["rsb"][:, 4 * w:4 * w + 4], pw[:, 256:260])
                bump(i)
                return e.tensor_tensor(
                    T["akm"][:].rearrange("p (h d) -> p h d", h=H)[:, 4 * w:4 * w + 4],
                    pw[:, 0:256].rearrange("p (h d) -> p h d", h=4),
                    T["rsb"][:, 4 * w:4 * w + 4].unsqueeze(2).broadcast_to([128, 4, DH]),
                    ALU.mult)
            em.stage("dve", fdiv, reads=[pkey], writes=["akm", "rsb"], steps=2)

        def all_waves():
            wv_sim(0)
            wv_maxexp(0)
            for w in range(4):
                if w + 1 < 4:
                    wv_sim(w + 1)
                    wv_maxexp(w + 1)
                wv_ptav(w)
        wave_rec = collect(all_waves)
        zip_flush(wave_rec, chain_rec, ratio=2)

        gi = (m - G) % 4
        _p_transpose8(st, T["akm"][:], "akm", "trF")
        em.stage("dve", lambda e, gi=gi: e.tensor_copy(
            T["afm"][:].rearrange("p (c g t) -> p c g t", c=NCH, g=4)[:, :, gi],
            ps["trF"][:].rearrange("p (c t) -> p c t", c=NCH)),
            reads=["trF"], writes=["afm"])

        if m == q0 or (m - G) % 4 == 3 or m == NS - 1:
            ng = 1 if m == q0 else (m - G) % 4 + 1
            gi0 = (m - ng + 1 - G) % 4
            af = T["afm"][:].rearrange("p (c t) -> p c t", c=NCH)
            g0s = (m - ng + 1 - G) % 8
            yrd = [f"yt{(g0s + j) % 8}" for j in range(ng)]
            for op in range(4):
                pk = "P1" if op % 2 == 0 else "P2"

                def fwo(e, op=op, pk=pk, ng=ng, af=af, gi0=gi0):
                    for oc in (2 * op, 2 * op + 1):
                        half = oc % 2
                        for ic in range(NCH):
                            i = e.matmul(ps[pk][:, half * 512: half * 512 + ng * 128],
                                         A[:, WO + ic * 1024 + oc * 128:
                                            WO + ic * 1024 + (oc + 1) * 128],
                                         af[:, ic, gi0 * 128:(gi0 + ng) * 128],
                                         start=(ic == 0), stop=(ic == NCH - 1))
                    return i
                em.stage("pe", fwo, reads=["afm", "slotW2"], writes=[pk])

                def fadd(e, op=op, pk=pk, ng=ng, g0s=g0s):
                    for oc in (2 * op, 2 * op + 1):
                        half = oc % 2
                        i = e.tensor_tensor(
                            yt[:, g0s:g0s + ng, oc],
                            ps[pk][:, half * 512: half * 512 + ng * 128]
                            .rearrange("p (s t) -> p s t", s=ng),
                            yt[:, g0s:g0s + ng, oc],
                            ALU.add)
                    return i
                em.stage("dve", fadd, reads=[pk] + yrd, writes=yrd)

            for j in range(ng):
                mm = m - ng + 1 + j
                sl = (g0s + j) % 8
                em.stage("sy", lambda e, mm=mm, sl=sl: e.dma_start(
                    out=ydst[mm].rearrange("c p t -> p c t"), in_=yt[:, sl]),
                    reads=yrd, writes=[f"ymid_{l}_{mm}"], dma=True)

    if st.get("bffn", True):
        _p_bffn(st, l, ydst)


def _p_bffn(st, l, ydst):
    em, T, ps, dram = st["em"], st["T"], st["ps"], st["dram"]
    em.barrier()
    G = 3 * l + 1
    A = T["arena"]
    W1, W2 = 0, 45056
    w1, w2 = dram["W1p"].ap()[l - 1], dram["W2p"].ap()[l - 1]

    for c in range(NCH):
        em.stage("pool", lambda e, c=c: e.dma_start(
            out=A[:, W1 + c * 5632: W1 + (c + 1) * 5632], in_=w1[ts(c), :]),
            writes=["slotW1"] if c == 0 else [], accw=["slotW1"] if c > 0 else [],
            dma=True)
    for c in range(NIC):
        em.stage("pool", lambda e, c=c: e.dma_start(
            out=A[:, W2 + c * 1024: W2 + (c + 1) * 1024], in_=w2[ts(c), :]),
            writes=["slotW2"] if c == 0 else [], accw=["slotW2"] if c > 0 else [],
            dma=True)

    for seg in range(11):
        pk, pk2 = ("P1", "P2") if seg % 2 == 0 else ("P2", "P1")

        def fseg(e, seg=seg, pk=pk):
            for c in range(NCH):
                i = e.matmul(ps[pk][0:1, 0:512], T["lnb_s"][:, c:c + 1],
                             A[:, W1 + c * 5632 + seg * 512: W1 + c * 5632 + (seg + 1) * 512],
                             start=(c == 0), stop=(c == NCH - 1))
            return i
        em.stage("pe", fseg, reads=["lnb_s", "slotW1"], writes=[pk])
        em.stage("act", lambda e, pk=pk: e.copy(T["mrow"][0:1, 0:512],
                                                ps[pk][0:1, 0:512]),
                 reads=[pk], writes=["mrow"])

        def fsp2(e, pk2=pk2):
            for u in range(4):
                i = e.matmul(ps[pk2][:, u:u + 1],
                             T["mrow"][0:1, u * 128:(u + 1) * 128],
                             T["ones_c"][0:1, 0:1], start=True, stop=True)
            return i
        em.stage("pe", fsp2, reads=["mrow", "ones_c"], writes=[pk2])
        em.stage("dve", lambda e, seg=seg, pk2=pk2: e.tensor_copy(
            T["cb1_s"][:, seg * 4:(seg * 4) + 4], ps[pk2][:, 0:4]),
            reads=[pk2], writes=["cb1_s"] if seg == 10 else [])
    em.stage("dve", lambda e: e.tensor_add(T["cb1_s"][:], T["cb1_s"][:], T["b1p_s"][:]),
             reads=["cb1_s", "b1p_s"], writes=["cb1_s"])

    TG = 3
    ngroups = (NS - G + TG - 1) // TG

    def grp(jg):
        s0 = G + jg * TG
        nsub = min(TG, NS - s0)
        return s0, nsub, nsub * 128, jg % 2

    def emit_u2_loads(jg):
        s0, nsub, TC, gslot = grp(jg)
        u2 = T["u2g"][:].rearrange("p (c t) -> p c t", c=NCH, t=384)
        ym = T["ymg"][:, gslot * (NCH * 384):(gslot + 1) * (NCH * 384)].rearrange(
            "p (c t) -> p c t", c=NCH, t=384)
        ymk = [f"ymid_{l}_{s0 + j}" for j in range(nsub)]
        for c4 in range(4):
            em.stage("pool", lambda e, s0=s0, nsub=nsub, c4=c4, u2=u2: e.dma_start(
                out=u2[:, c4, 0:nsub * 128].rearrange("p (s t) -> p s t", s=nsub),
                in_=ydst[s0:s0 + nsub, c4].rearrange("s p t -> p s t")),
                reads=ymk, writes=["u2g"] if c4 == 0 else [],
                accw=["u2g"] if c4 > 0 else [], dma=True)
        for c4 in range(4, 8):
            em.stage("pool", lambda e, s0=s0, nsub=nsub, c4=c4, u2=u2: e.dma_start(
                out=u2[:, c4, 0:nsub * 128].rearrange("p (s t) -> p s t", s=nsub)[:, :, 1:128],
                in_=ydst[s0:s0 + nsub, c4].rearrange("s p t -> p s t")[:, :, 0:127]),
                reads=ymk, accw=["u2g"], dma=True)
        if nsub > 1:
            for c4 in range(4, 8):
                em.stage("pool", lambda e, s0=s0, nsub=nsub, c4=c4, u2=u2: e.dma_start(
                    out=u2[:, c4, 128:nsub * 128].rearrange(
                        "p (s t) -> p s t", s=nsub - 1)[:, :, 0:1],
                    in_=ydst[s0:s0 + nsub - 1, c4].rearrange(
                        "s p t -> p s t")[:, :, 127:128],
                    allow_slow_non_contiguous=True),
                    reads=ymk, accw=["u2g"], dma=True)
        if jg == 0:
            def fu2d(e, s0=s0, u2=u2):
                return e.dma_start(
                    out=u2[:, 4:8, 0:1],
                    in_=ydst[s0 - 1, 4:8].rearrange("c p t -> p c t")[:, :, 127:128],
                    allow_slow_non_contiguous=True)
            em.stage("pool", fu2d, reads=[f"ymid_{l}_{s0 - 1}"], accw=["u2g"],
                     dma=True)
        else:
            em.stage("pool", lambda e, u2=u2: e.tensor_copy(
                u2[:, 4:8, 0:1], T["ycar"][:].unsqueeze(2)),
                reads=["ycar"], accw=["u2g"])
        if s0 <= OWN0 < s0 + nsub:
            o = (OWN0 - s0) * 128
            em.stage("pool", lambda e, o=o, u2=u2: e.tensor_scalar_mul(
                u2[:, 4:8, o:o + 1], u2[:, 4:8, o:o + 1],
                T["hval_s"][:, OWN0:OWN0 + 1]),
                reads=["u2g", "hval_s"], writes=["u2g"])
        for c8 in range(NCH):
            em.stage("sy", lambda e, s0=s0, nsub=nsub, ym=ym, c8=c8: e.dma_start(
                out=ym[:, c8, 0:nsub * 128].rearrange("p (s t) -> p s t", s=nsub),
                in_=ydst[s0:s0 + nsub, c8].rearrange("s p t -> p s t")),
                reads=ymk, writes=[f"ymg{gslot}"] if c8 == 0 else [],
                accw=[f"ymg{gslot}"] if c8 > 0 else [], dma=True)
        em.stage("pool", lambda e, ym=ym, TC=TC: e.tensor_copy(
            T["ycar"][:].unsqueeze(2), ym[:, 4:8, TC - 1:TC]),
            reads=[f"ymg{gslot}"], writes=["ycar"])

    def emit_stats_u2rg(jg):
        s0, nsub, TC, gslot = grp(jg)
        u2 = T["u2g"][:].rearrange("p (c t) -> p c t", c=NCH, t=384)

        def fst1(e, TC=TC, u2=u2):
            for c in range(NCH):
                i = e.matmul(ps["P2"][0:1, 0:TC], T["ones_c"][:], u2[:, c, 0:TC],
                             start=(c == 0), stop=(c == NCH - 1))
            return i
        em.stage("pe", fst1, reads=["u2g", "ones_c"], writes=["P2"])
        for c in range(NCH):
            sq = T["sqt"][:, (c % 2) * 384:((c % 2) + 1) * 384]
            em.stage("pool", lambda e, c=c, sq=sq, TC=TC, u2=u2: e.tensor_mul(
                sq[:, 0:TC], u2[:, c, 0:TC], u2[:, c, 0:TC]),
                reads=["u2g"], writes=[f"sqt{c % 2}"])
            em.stage("pe", lambda e, c=c, sq=sq, TC=TC: e.matmul(
                ps["P2"][0:1, 512:512 + TC], T["ones_c"][:], sq[:, 0:TC],
                start=(c == 0), stop=(c == NCH - 1)),
                reads=[f"sqt{c % 2}", "ones_c"], writes=["P2"])

        def fst2(e, bump, TC=TC):
            e.tensor_scalar_mul(T["trow"][0:1, 0:TC], ps["P2"][0:1, 0:TC], 1.0 / 1024.0)
            i = e.tensor_scalar_mul(T["trow"][0:1, 512:512 + TC],
                                    ps["P2"][0:1, 512:512 + TC], 1.0 / 1024.0)
            bump(i)
            i = e.tensor_mul(T["trow2"][0:1, 0:TC], T["trow"][0:1, 0:TC],
                             T["trow"][0:1, 0:TC])
            bump(i)
            e.tensor_sub(T["trow2"][0:1, 512:512 + TC], T["trow"][0:1, 512:512 + TC],
                         T["trow2"][0:1, 0:TC])
            return e.tensor_copy(T["mrow"][0:1, 0:TC], T["trow"][0:1, 0:TC])
        em.stage("dve", fst2, reads=["P2"], writes=["trow", "mrow"], steps=3)

        def fst3(e, bump, TC=TC):
            i = e.activation(T["trow2"][0:1, 0:TC], T["trow2"][0:1, 512:512 + TC],
                             AF.Ln, bias=T["ceps"][0:1, :])
            bump(i)
            return e.activation(T["rrow"][0:1, 0:TC], T["trow2"][0:1, 0:TC], AF.Exp,
                                scale=-0.5)
        em.stage("act", fst3, reads=["trow", "ceps"], writes=["rrow"], steps=2)

        def fbc(e, TC=TC):
            e.matmul(ps["P2"][:, 0:TC], T["ones_r"][:], T["mrow"][0:1, 0:TC],
                     start=True, stop=True)
            return e.matmul(ps["P2"][:, 512:512 + TC], T["ones_r"][:],
                            T["rrow"][0:1, 0:TC], start=True, stop=True)
        em.stage("pe", fbc, reads=["mrow", "rrow", "ones_r"], writes=["P2"])

        def fu2rg(e, bump, TC=TC, u2=u2):
            for c in range(NCH):
                i = e.tensor_sub(T["tf1"][:, 0:TC], u2[:, c, 0:TC], ps["P2"][:, 0:TC])
                bump(i)
                i = e.scalar_tensor_tensor(u2[:, c, 0:TC], T["tf1"][:, 0:TC],
                                           T["lng_s"][:, c:c + 1],
                                           ps["P2"][:, 512:512 + TC], ALU.mult, ALU.mult)
                if c < NCH - 1:
                    bump(i)
            return i
        em.stage("dve", fu2rg, reads=["P2", "u2g", "lng_s"], writes=["u2g", "tf1"],
                 steps=16)

    def emit_w1(jg):
        s0, nsub, TC, gslot = grp(jg)
        u2 = T["u2g"][:].rearrange("p (c t) -> p c t", c=NCH, t=384)
        fv = T["fsb"][:].rearrange("p (k t) -> p k t", k=NIC, t=384)
        for pc in range(NIC):
            pk = "P1" if pc % 2 == 0 else "P2"
            pw = ps[pk]

            def fw1(e, pc=pc, pw=pw, TC=TC, u2=u2):
                for c in range(NCH):
                    e.matmul(pw[:, 0:TC],
                             A[:, W1 + c * 5632 + 2 * pc * 128: W1 + c * 5632 + (2 * pc + 1) * 128],
                             u2[:, c, 0:TC], start=(c == 0), stop=(c == NCH - 1))
                for c in range(NCH):
                    i = e.matmul(pw[:, 512:512 + TC],
                                 A[:, W1 + c * 5632 + (2 * pc + 1) * 128: W1 + c * 5632 + (2 * pc + 2) * 128],
                                 u2[:, c, 0:TC], start=(c == 0), stop=(c == NCH - 1))
                return i
            em.stage("pe", fw1, reads=["u2g", "slotW1"], writes=[pk])

            em.stage("act", lambda e, pc=pc, pw=pw, TC=TC: e.activation(
                T["tf2"][:, 0:TC], pw[:, 512:512 + TC], AF.Gelu,
                bias=T["cb1_s"][:, 2 * pc + 1:2 * pc + 2]),
                reads=[pk, "cb1_s"], writes=["tf2"])

            em.stage("dve", lambda e, pc=pc, pw=pw, TC=TC, fv=fv: e.scalar_tensor_tensor(
                fv[:, pc, 0:TC], pw[:, 0:TC], T["cb1_s"][:, 2 * pc:2 * pc + 1],
                T["tf2"][:, 0:TC], ALU.add, ALU.mult),
                reads=[pk, "tf2", "cb1_s"], writes=["fsb"])

    def emit_w2(jg):
        s0, nsub, TC, gslot = grp(jg)
        ym = T["ymg"][:, gslot * (NCH * 384):(gslot + 1) * (NCH * 384)].rearrange(
            "p (c t) -> p c t", c=NCH, t=384)
        fv = T["fsb"][:].rearrange("p (k t) -> p k t", k=NIC, t=384)
        for oc in range(NCH):
            def fw2(e, oc=oc, TC=TC, fv=fv):
                for k in range(NIC):
                    i = e.matmul(ps["P1"][:, 0:TC],
                                 A[:, W2 + k * 1024 + oc * 128: W2 + k * 1024 + (oc + 1) * 128],
                                 fv[:, k, 0:TC], start=(k == 0), stop=(k == NIC - 1))
                return i
            em.stage("pe", fw2, reads=["fsb", "slotW2"], writes=["P1"])
            em.stage("dve", lambda e, oc=oc, TC=TC, ym=ym: e.scalar_tensor_tensor(
                ym[:, oc, 0:TC], ps["P1"][:, 0:TC], T["b2r_s"][:, oc:oc + 1],
                ym[:, oc, 0:TC], ALU.add, ALU.add),
                reads=["P1", f"ymg{gslot}", "b2r_s"], writes=[f"ymg{gslot}"])
        wkeys = [f"ysrc_{l + 1}_{s0 + j}" for j in range(nsub)]
        for c8 in range(NCH):
            em.stage("sy", lambda e, s0=s0, nsub=nsub, ym=ym, c8=c8: e.dma_start(
                out=ydst[s0:s0 + nsub, c8].rearrange("s p t -> p s t"),
                in_=ym[:, c8, 0:nsub * 128].rearrange("p (s t) -> p s t", s=nsub)),
                reads=[f"ymg{gslot}"], writes=wkeys if c8 == 0 else [],
                accw=wkeys if c8 > 0 else [], dma=True)

    emit_u2_loads(0)
    emit_stats_u2rg(0)
    for jg in range(ngroups):
        emit_w1(jg)
        if jg + 1 < ngroups:
            emit_u2_loads(jg + 1)
            emit_stats_u2rg(jg + 1)
        emit_w2(jg)


def _p_abranch(st, nlayers):
    em, T, ps, dram = st["em"], st["T"], st["ps"], st["dram"]
    em.barrier()
    A = T["arena"]
    WQ, WK, WV, WOA = 0, 8192, 16384, 45056
    xin = dram["xin"].ap()

    first = True
    for nm, off in (("Wq", WQ), ("Wk", WK), ("Wv", WV)):
        for c in range(NCH):
            em.stage("pool", lambda e, nm=nm, off=off, c=c: e.dma_start(
                out=A[:, off + c * 1024: off + (c + 1) * 1024],
                in_=dram[nm].ap()[ts(c), :]),
                writes=["slotW1"] if first else [],
                accw=[] if first else ["slotW1"], dma=True)
            first = False
    for c in range(NCH):
        em.stage("pool", lambda e, c=c: e.dma_start(
            out=A[:, WOA + c * 1024: WOA + (c + 1) * 1024],
            in_=dram["WoA"].ap()[ts(c), :]),
            writes=["slotW2"] if c == 0 else [], accw=["slotW2"] if c > 0 else [],
            dma=True)

    for dst, srcn in (("g1t_s", "g1t"), ("b1t_s", "b1t")):
        em.stage("sy", lambda e, d=dst, s=srcn: e.dma_start(
            out=T[d][:], in_=dram[s].ap()), writes=[dst], dma=True)

    def fz(e):
        e.memset(T["Sacc"][:], 0.0)
        e.memset(T["zacc"][:], 0.0)
        e.memset(T["SbD"][:], 0.0)
        return e.memset(T["zbD"][:], 0.0)
    em.stage("dve", fz, writes=["Sacc", "SbD"])

    for j in range(NPS + 32):
        _p_asub(st, j, j >= NPS)

    _p_affn(st, nlayers)


def _p_asub(st, j, own):
    em, T, ps, dram = st["em"], st["T"], st["ps"], st["dram"]
    A = T["arena"]
    WQ, WK, WV, WOA = 0, 8192, 16384, 45056
    xin = dram["xin"].ap()
    slot = j % 2

    em.stage("sy", lambda e, j=j, slot=slot: e.dma_start(
        out=T["xld"][:, slot * D:(slot + 1) * D], in_=xin[ts(j), :]), writes=[f"xld{slot}"], dma=True)
    _p_ln(st, T["xld"][:, slot * D:(slot + 1) * D][:, :], T["xnt"][:, :], f"xld{slot}", "xnt")
    _p_ln(st, T["xnt"][:, :], T["t1"][:, :], "xnt", "h1a",
          g=T["g1t_s"][:], b=T["b1t_s"][:])
    em.stage("dve", lambda e: e.tensor_copy(T["h1b"][:], T["t1"][:]),
             reads=["h1a"], writes=["h1b"])
    _p_transpose8(st, T["h1b"][:], "h1b", "trE")
    em.stage("dve", lambda e: e.tensor_copy(T["h1fm"][:], ps["trE"][:]),
             reads=["trE"], writes=["h1fm"])
    h1f = T["h1fm"][:].rearrange("p (c t) -> p c t", c=NCH)

    if own:
        _p_transpose8(st, T["xnt"][:], "xnt", "P1", dt="f")
        em.stage("dve", lambda e: e.tensor_copy(T["xnfm"][:], ps["P1"][:, 0:1024]),
                 reads=["P1"], writes=["xnfm"])

    def fkv(e):
        for half in range(2):
            for c in range(NCH):
                e.matmul(ps["P1"][:, half * 512:(half + 1) * 512], h1f[:, c],
                         A[:, WK + c * 1024 + half * 512: WK + c * 1024 + (half + 1) * 512],
                         start=(c == 0), stop=(c == NCH - 1))
        for half in range(2):
            for c in range(NCH):
                i = e.matmul(ps["P2"][:, half * 512:(half + 1) * 512], h1f[:, c],
                             A[:, WV + c * 1024 + half * 512: WV + c * 1024 + (half + 1) * 512],
                             start=(c == 0), stop=(c == NCH - 1))
        return i
    em.stage("pe", fkv, reads=["h1fm", "slotW1"], writes=["P1", "P2"])

    def fp1(e):
        e.tensor_scalar_min(T["t1"][:], ps["P1"][:, 0:1024], 0.0)
        return e.tensor_scalar_add(T["t2"][:], ps["P1"][:, 0:1024], 1.0)
    em.stage("dve", fp1, reads=["P1"], writes=["t1", "t2"])
    em.stage("act", lambda e: e.activation(T["t1"][:], T["t1"][:], AF.Exp),
             reads=["t1"], writes=["t1"])

    def fp2(e, bump, j=j, own=own):
        i = e.tensor_max(T["t3"][:], T["t1"][:], T["t2"][:])
        bump(i)
        if not own:
            return e.tensor_scalar_mul(T["Kkm"][:], T["t3"][:], T["pval_s"][:, j:j + 1])
        return e.tensor_copy(T["Kkm"][:], T["t3"][:])
    em.stage("dve", fp2, reads=["t1", "t2", "pval_s"], writes=["Kkm", "t3"], steps=2)
    em.stage("act", lambda e: e.copy(T["Vkm2"][:], ps["P2"][:, 0:1024]),
             reads=["P2"], writes=["Vkm2"])

    if own:
        def fq(e):
            for pr in range(NCH):
                for c in range(NCH):
                    i = e.matmul(ps["P1"][:, ts(pr)],
                                 A[:, WQ + c * 1024 + pr * 128: WQ + c * 1024 + (pr + 1) * 128],
                                 h1f[:, c], start=(c == 0), stop=(c == NCH - 1))
            return i
        em.stage("pe", fq, reads=["h1fm", "slotW1"], writes=["P1"])

        def fq1(e):
            e.tensor_scalar_min(T["t1"][:], ps["P1"][:, 0:1024], 0.0)
            return e.tensor_scalar_add(T["t2"][:], ps["P1"][:, 0:1024], 1.0)
        em.stage("dve", fq1, reads=["P1"], writes=["t1", "t2"])
        em.stage("act", lambda e: e.activation(T["t1"][:], T["t1"][:], AF.Exp),
                 reads=["t1"], writes=["t1"])
        em.stage("dve", lambda e: e.tensor_max(T["Qfm"][:], T["t1"][:], T["t2"][:]),
                 reads=["t1", "t2"], writes=["Qfm"])

        _p_transpose8(st, T["Kkm"][:], "Kkm", "trF")
        em.stage("act", lambda e: e.copy(T["Kfm"][:], ps["trF"][:]),
                 reads=["trF"], writes=["Kfm"])

        for h in range(H):
            pb = (h % 2) * 64
            pc = (h // 2) * 128
            asl = h % 2

            em.stage("pe", lambda e, pb=pb, pc=pc, asl=asl: e.matmul(
                ps["P2"][:, 1024 + asl * 128: 1152 + asl * 128],
                T["Kfm"][pb:pb + 64, pc:pc + 128], T["Qfm"][pb:pb + 64, pc:pc + 128],
                start=True, stop=True),
                reads=["Kfm", "Qfm"], writes=[f"ATp{asl}"])

            em.stage("dve", lambda e, asl=asl: e.tensor_tensor(
                T["ATs"][:, asl * 128:(asl + 1) * 128],
                ps["P2"][:, 1024 + asl * 128: 1152 + asl * 128], T["triA_s"][:],
                ALU.mult),
                reads=[f"ATp{asl}", "triA_s"], writes=[f"ATs{asl}"])

            def fnd(e, h=h, pb=pb, pc=pc, asl=asl):
                vh = T["Vkm2"][:].rearrange("p (h d) -> p h d", h=H)[:, h]
                e.matmul(ps["P1"][:, h * 64:(h + 1) * 64],
                         T["ATs"][:, asl * 128:(asl + 1) * 128], vh,
                         start=True, stop=False)
                e.matmul(ps["P1"][:, h * 64:(h + 1) * 64],
                         T["Qfm"][pb:pb + 64, pc:pc + 128],
                         T["SbD"][pb:pb + 64, h * 64:(h + 1) * 64],
                         start=False, stop=True)
                e.matmul(ps["P1"][:, 1024 + h:1024 + h + 1],
                         T["ATs"][:, asl * 128:(asl + 1) * 128], T["ones_c"][:],
                         start=True, stop=False)
                return e.matmul(ps["P1"][:, 1024 + h:1024 + h + 1],
                                T["Qfm"][pb:pb + 64, pc:pc + 128],
                                T["zbD"][pb:pb + 64, h:h + 1],
                                start=False, stop=True)
            em.stage("pe", fnd, reads=[f"ATs{asl}", "Vkm2", "SbD", "zbD", "ones_c",
                                       "Qfm"], writes=["P1"])

        def fdv(e, bump):
            i = e.tensor_scalar_max(T["den_t"][:], ps["P1"][:, 1024:1024 + H], 1e-6)
            bump(i)
            i = e.reciprocal(T["rA"][:], T["den_t"][:])
            bump(i)
            return e.tensor_tensor(
                T["aAkm"][:].rearrange("p (h d) -> p h d", h=H),
                ps["P1"][:, 0:1024].rearrange("p (h d) -> p h d", h=H),
                T["rA"][:].unsqueeze(2).broadcast_to([128, H, DH]), ALU.mult)
        em.stage("dve", fdv, reads=["P1"], writes=["aAkm", "rA", "den_t"], steps=3)

    def fsu(e):
        kk = T["Kkm"][:].rearrange("p (h d) -> p h d", h=H)
        vv = T["Vkm2"][:].rearrange("p (h d) -> p h d", h=H)
        for h in range(H):
            e.matmul(ps["P2"][0:64, h * 64:(h + 1) * 64], kk[:, h], vv[:, h],
                     start=True, stop=True)
            i = e.matmul(ps["P2"][0:64, 1024 + h:1024 + h + 1], kk[:, h],
                         T["ones_c"][:], start=True, stop=True)
        return i
    em.stage("pe", fsu, reads=["Kkm", "Vkm2", "ones_c"], writes=["P2"])

    def fsa(e, bump):
        e.tensor_add(T["Sacc"][:], T["Sacc"][:], ps["P2"][0:64, 0:1024])
        i = e.tensor_add(T["zacc"][:], T["zacc"][:], ps["P2"][0:64, 1024:1024 + H])
        bump(i)
        e.tensor_copy(T["SaccB"][:], T["Sacc"][:])
        return e.tensor_copy(T["zaccB"][:], T["zacc"][:])
    em.stage("dve", fsa, reads=["P2", "Sacc"], writes=["Sacc", "SaccB"], steps=2)

    if j + 1 >= NPS:
        def fdp(e):
            e.matmul(ps["P2"][:, 0:512], T["dupI_s"][:], T["SaccB"][:, 0:512],
                     start=True, stop=True)
            e.matmul(ps["P2"][:, 512:1024], T["dupI_s"][:], T["SaccB"][:, 512:1024],
                     start=True, stop=True)
            return e.matmul(ps["P2"][:, 1024:1024 + H], T["dupI_s"][:], T["zaccB"][:],
                            start=True, stop=True)
        em.stage("pe", fdp, reads=["SaccB", "dupI_s"], writes=["P2"])

        def fdc(e):
            e.tensor_copy(T["SbD"][:], ps["P2"][:, 0:1024])
            return e.tensor_copy(T["zbD"][:], ps["P2"][:, 1024:1024 + H])
        em.stage("dve", fdc, reads=["P2"], writes=["SbD", "zbD"])

    if own:
        if st.get("dbg_suba") == j - NPS:
            for nm, key in (("daA", "aAkm"), ("dQf", "Qfm"), ("dKf", "Kfm"),
                            ("dVk", "Vkm2"), ("dSb", "SbD"), ("dzb", "zbD")):
                em.stage("pool", lambda e, nm=nm, key=key: e.dma_start(
                    out=dram[nm].ap(), in_=T[key][:]), reads=[key], dma=True)
        oj = j - NPS
        gi = oj % 4
        _p_transpose8(st, T["aAkm"][:], "aAkm", "trF")
        em.stage("dve", lambda e, gi=gi: e.tensor_copy(
            T["aAfm"][:].rearrange("p (c g t) -> p c g t", c=NCH, g=4)[:, :, gi],
            ps["trF"][:].rearrange("p (c t) -> p c t", c=NCH)),
            reads=["trF"], writes=["aAfm"])
        em.stage("dve", lambda e, gi=gi: e.tensor_copy(
            T["xag"][:].rearrange("p (c g t) -> p c g t", c=NCH, g=4)[:, :, gi],
            T["xnfm"][:].rearrange("p (c t) -> p c t", c=NCH)),
            reads=["xnfm"], writes=["xag"])

        if gi == 3:
            af = T["aAfm"][:].rearrange("p (c t) -> p c t", c=NCH)
            xg = T["xag"][:].rearrange("p (c t) -> p c t", c=NCH)
            for op in range(4):
                pk = "P2" if op % 2 == 0 else "P1"

                def fwo(e, op=op, pk=pk, af=af):
                    for oc in (2 * op, 2 * op + 1):
                        half = oc % 2
                        for ic in range(NCH):
                            i = e.matmul(ps[pk][:, half * 512:(half + 1) * 512],
                                         A[:, WOA + ic * 1024 + oc * 128:
                                            WOA + ic * 1024 + (oc + 1) * 128],
                                         af[:, ic], start=(ic == 0), stop=(ic == NCH - 1))
                    return i
                em.stage("pe", fwo, reads=["aAfm", "slotW2"], writes=[pk])

                def fad(e, op=op, pk=pk, xg=xg):
                    for oc in (2 * op, 2 * op + 1):
                        half = oc % 2
                        i = e.tensor_add(xg[:, oc], ps[pk][:, half * 512:(half + 1) * 512],
                                         xg[:, oc])
                    return i
                em.stage("dve", fad, reads=[pk, "xag"], writes=["xag"])

            for c8 in range(NCH):
                em.stage("sy", lambda e, oj=oj, xg=xg, c8=c8: e.dma_start(
                    out=dram["xasc"].ap()[oj - 3:oj + 1, c8].rearrange("s p t -> p s t"),
                    in_=xg[:, c8].rearrange("p (s t) -> p s t", s=4)),
                    reads=["xag"], writes=[f"xasc{oj // 4}"] if c8 == 0 else [],
                    accw=[f"xasc{oj // 4}"] if c8 > 0 else [], dma=True)


def _p_affn(st, nlayers):
    em, T, ps, dram = st["em"], st["T"], st["ps"], st["dram"]
    em.barrier()
    A = T["arena"]
    WF1, WF2 = 0, NCH * DF
    yfin = dram["ysc0" if nlayers % 2 == 0 else "ysc1"].ap()
    xin = dram["xin"].ap()

    for c in range(NCH):
        em.stage("pool", lambda e, c=c: e.dma_start(
            out=A[:, WF1 + c * DF: WF1 + (c + 1) * DF], in_=dram["Wf1"].ap()[ts(c), :]),
            writes=["slotW1"] if c == 0 else [], accw=["slotW1"] if c > 0 else [],
            dma=True)
    for c in range(NOC):
        em.stage("pool", lambda e, c=c: e.dma_start(
            out=A[:, WF2 + c * 1024: WF2 + (c + 1) * 1024],
            in_=dram["Wf2"].ap()[ts(c), :]),
            writes=["slotW2"] if c == 0 else [], accw=["slotW2"] if c > 0 else [],
            dma=True)

    for seg in range(8):
        pk = "P1" if seg % 2 == 0 else "P2"

        def fseg(e, seg=seg, pk=pk):
            for c in range(NCH):
                i = e.matmul(ps[pk][0:1, 0:512], T["b2a_s"][:, c:c + 1],
                             A[:, WF1 + c * DF + seg * 512: WF1 + c * DF + (seg + 1) * 512],
                             start=(c == 0), stop=(c == NCH - 1))
            return i
        em.stage("pe", fseg, reads=["b2a_s", "slotW1"], writes=[pk])
        pk2 = "P2" if pk == "P1" else "P1"
        em.stage("act", lambda e, pk=pk: e.copy(T["mrow"][0:1, 0:512],
                                                ps[pk][0:1, 0:512]),
                 reads=[pk], writes=["mrow"])

        def fsp(e, pk2=pk2):
            for u in range(4):
                i = e.matmul(ps[pk2][:, u:u + 1],
                             T["mrow"][0:1, u * 128:(u + 1) * 128],
                             T["ones_c"][0:1, 0:1], start=True, stop=True)
            return i
        em.stage("pe", fsp, reads=["mrow", "ones_c"], writes=[pk2])
        em.stage("dve", lambda e, seg=seg, pk2=pk2: e.tensor_copy(
            T["cbA_s"][:, seg * 4:seg * 4 + 4], ps[pk2][:, 0:4]),
            reads=[pk2], writes=["cbA_s"] if seg == 7 else [])

    fva = T["fA"][:].rearrange("p (k t) -> p k t", k=NOC, t=128)
    TC = 128
    for s0 in range(32):
        xg2 = T["xag2"][:].rearrange("p (c t) -> p c t", c=NCH, t=128)
        ygv = T["yg"][:].rearrange("p (c t) -> p c t", c=NCH, t=128)

        em.stage("sy", lambda e, s0=s0, xg2=xg2: e.dma_start(
            out=xg2, in_=dram["xasc"].ap()[s0].rearrange("c p t -> p c t")),
            reads=[f"xasc{s0 // 4}"], writes=["xag2"], dma=True)

        em.stage("pool", lambda e, xg2=xg2: e.tensor_copy(
            T["xab"][:], xg2.rearrange("p c t -> p (c t)")),
            reads=["xag2"], writes=["xab"])
        xabv = T["xab"][:].rearrange("p (c t) -> p c t", c=NCH, t=128)

        def fst1(e, xabv=xabv):
            for c in range(NCH):
                i = e.matmul(ps["P2"][0:1, 0:TC], T["ones_c"][:], xabv[:, c, 0:TC],
                             start=(c == 0), stop=(c == NCH - 1))
            return i
        em.stage("pe", fst1, reads=["xab", "ones_c"], writes=["P2"])
        for c in range(NCH):
            sq = T["sqA"][:, (c % 2) * 128:((c % 2) + 1) * 128]
            em.stage("pool", lambda e, c=c, sq=sq, xabv=xabv: e.tensor_mul(
                sq[:, 0:TC], xabv[:, c, 0:TC], xabv[:, c, 0:TC]),
                reads=["xab"], writes=[f"sqt{c % 2}"])
            em.stage("pe", lambda e, c=c, sq=sq: e.matmul(
                ps["P2"][0:1, 512:512 + TC], T["ones_c"][:], sq[:, 0:TC],
                start=(c == 0), stop=(c == NCH - 1)),
                reads=[f"sqt{c % 2}", "ones_c"], writes=["P2"])

        def fst2(e, bump):
            e.tensor_scalar_mul(T["trow"][0:1, 0:TC], ps["P2"][0:1, 0:TC], 1.0 / 1024.0)
            i = e.tensor_scalar_mul(T["trow"][0:1, 512:512 + TC],
                                    ps["P2"][0:1, 512:512 + TC], 1.0 / 1024.0)
            bump(i)
            i = e.tensor_mul(T["trow2"][0:1, 0:TC], T["trow"][0:1, 0:TC],
                             T["trow"][0:1, 0:TC])
            bump(i)
            e.tensor_sub(T["trow2"][0:1, 512:512 + TC], T["trow"][0:1, 512:512 + TC],
                         T["trow2"][0:1, 0:TC])
            return e.tensor_copy(T["mrow"][0:1, 0:TC], T["trow"][0:1, 0:TC])
        em.stage("dve", fst2, reads=["P2"], writes=["trow", "mrow"], steps=3)

        def fst3(e, bump):
            i = e.activation(T["trow2"][0:1, 0:TC], T["trow2"][0:1, 512:512 + TC],
                             AF.Ln, bias=T["ceps"][0:1, :])
            bump(i)
            return e.activation(T["rrow"][0:1, 0:TC], T["trow2"][0:1, 0:TC], AF.Exp,
                                scale=-0.5)
        em.stage("act", fst3, reads=["trow", "ceps"], writes=["rrow"], steps=2)

        def fbc(e):
            e.matmul(ps["P2"][:, 0:TC], T["ones_r"][:], T["mrow"][0:1, 0:TC],
                     start=True, stop=True)
            return e.matmul(ps["P2"][:, 512:512 + TC], T["ones_r"][:],
                            T["rrow"][0:1, 0:TC], start=True, stop=True)
        em.stage("pe", fbc, reads=["mrow", "rrow", "ones_r"], writes=["P2"])

        def fxc(e, bump, xabv=xabv):
            xcv = T["xac"][:].rearrange("p (c t) -> p c t", c=NCH, t=128)
            for c in range(NCH):
                i = e.tensor_sub(T["t1"][:, 0:TC], xabv[:, c, 0:TC], ps["P2"][:, 0:TC])
                bump(i)
                i = e.scalar_tensor_tensor(xcv[:, c, 0:TC], T["t1"][:, 0:TC],
                                           T["g2r_s"][:, c:c + 1],
                                           ps["P2"][:, 512:512 + TC], ALU.mult, ALU.mult)
                if c < NCH - 1:
                    bump(i)
            return i
        em.stage("dve", fxc, reads=["P2", "xab", "g2r_s"], writes=["xac", "t1"],
                 steps=16)
        xcv = T["xac"][:].rearrange("p (c t) -> p c t", c=NCH, t=128)

        for oc in range(NOC):
            pk = "P1" if oc % 2 == 0 else "P2"

            def fh(e, oc=oc, pk=pk, xcv=xcv):
                for c in range(NCH):
                    i = e.matmul(ps[pk][:, 0:TC],
                                 A[:, WF1 + c * DF + oc * 128: WF1 + c * DF + (oc + 1) * 128],
                                 xcv[:, c, 0:TC], start=(c == 0), stop=(c == NCH - 1))
                return i
            em.stage("pe", fh, reads=["xac", "slotW1"], writes=[pk])
            em.stage("act", lambda e, oc=oc, pk=pk: e.activation(
                fva[:, oc, 0:TC], ps[pk][:, 0:TC], AF.Gelu,
                bias=T["cbA_s"][:, oc:oc + 1]),
                reads=[pk, "cbA_s"], writes=["fA"])

        for oc in range(NCH):
            pk = "P1" if oc % 2 == 0 else "P2"

            def f2(e, oc=oc, pk=pk):
                for k in range(NOC):
                    i = e.matmul(ps[pk][:, 0:TC],
                                 A[:, WF2 + k * 1024 + oc * 128: WF2 + k * 1024 + (oc + 1) * 128],
                                 fva[:, k, 0:TC], start=(k == 0), stop=(k == NOC - 1))
                return i
            em.stage("pe", f2, reads=["fA", "slotW2"], writes=[pk])
            em.stage("dve", lambda e, oc=oc, pk=pk, xg2=xg2: e.tensor_add(
                xg2[:, oc, 0:TC], ps[pk][:, 0:TC], xg2[:, oc, 0:TC]),
                reads=[pk, "xag2"], writes=["xag2"])

        em.stage("sy", lambda e, s0=s0, ygv=ygv: e.dma_start(
            out=ygv, in_=yfin[OWN0 + s0].rearrange("c p t -> p c t")),
            reads=[f"ysrc_{nlayers + 1}_{OWN0 + s0}"], writes=["yg"], dma=True)

        em.stage("act", lambda e: e.activation(
            T["yg"][:], T["yg"][:], AF.Silu),
            reads=["yg"], writes=["yg"])

        em.stage("pool", lambda e: e.tensor_mul(
            T["prodb"][:], T["xag2"][:], T["yg"][:]),
            reads=["xag2", "yg"], writes=["prodb"])

        pv = T["prodb"][:].rearrange("p (c t) -> p c t", c=NCH, t=128)

        def ftp(e, pv=pv):
            for c in range(NCH):
                i = e.transpose(ps["trE"][:, ts(c)], pv[:, c], T["identb_s"][:])
            return i
        em.stage("pe", ftp, reads=["prodb", "identb_s"], writes=["trE"])
        em.stage("dve", lambda e: e.tensor_copy(T["prkm"][:], ps["trE"][:]),
                 reads=["trE"], writes=["prkm"])

        em.stage("sy", lambda e, s0=s0: e.dma_start(
            out=T["xkm"][:], in_=xin[PREF + s0 * 128: PREF + (s0 + 1) * 128, :]),
            writes=["xkm"], dma=True)

        em.stage("dve", lambda e: e.tensor_add(
            T["xkm"][:], T["prkm"][:], T["xkm"][:]),
            reads=["prkm", "xkm"], writes=["xkm"])

        em.stage("sy", lambda e, s0=s0: e.dma_start(
            out=dram["out"].ap()[s0 * 128:(s0 + 1) * 128, :], in_=T["xkm"][:]),
            reads=["xkm"], dma=True)



_CACHE = {}


def _host_prep(inputs):
    x = np.ascontiguousarray(inputs["L0_attn_x"])
    f32 = np.float32

    ii = np.arange(128)
    mA = np.where(ii[:, None] > ii[None, :], 0, 1).astype(f32)
    mC = np.where(ii[:, None] < ii[None, :], 0, 1).astype(f32)
    allm = np.zeros((128, 128), f32)
    zer = np.ones((128, 128), f32)
    triA = (ii[:, None] <= ii[None, :]).astype(f32)
    identf = np.eye(128, dtype=f32)
    dupI = np.concatenate([np.eye(64, dtype=f32), np.eye(64, dtype=f32)], axis=1)

    half = np.arange(0, DH, 2, dtype=f32)
    inv_freq = (f32(1.0) / f32(10000.0) ** (half / f32(DH))).astype(f32)
    tg = np.arange(16384, dtype=f32)
    freqs = np.concatenate([tg[:, None] * inv_freq[None, :]] * 2, -1).astype(f32)
    base = ((half + f32(0.4 * DH)) / f32(1.4 * DH)).astype(f32)
    power = ((tg - 8192.0) / f32(128.0)).astype(f32)
    scale = np.concatenate([base[None, :] ** power[:, None]] * 2, -1).astype(f32)
    cos, sin = np.cos(freqs).astype(f32), np.sin(freqs).astype(f32)
    sgn = np.concatenate([-np.ones(32, f32), np.ones(32, f32)])
    cs8 = (cos * scale * 8).astype(f32)
    ss8s = (sin * scale * 8 * sgn).astype(f32)
    ci = (cos / scale).astype(f32)
    sis = (sin / scale * sgn).astype(f32)

    W1 = inputs["L0_ffn_W1"]; W2 = inputs["L0_ffn_W2"]; b1 = inputs["L0_ffn_b1"]
    W1p = np.zeros((6, D, 2 * DI), W1.dtype)
    b1p = np.zeros((6, 2 * DI), b1.dtype)
    W2p = np.zeros((6, DI, D), W2.dtype)
    for c in range(NIC):
        lo, hi = c * 128, min((c + 1) * 128, 2730)
        n = hi - lo
        if n <= 0:
            continue
        W1p[:, :, 2 * c * 128: 2 * c * 128 + n] = W1[:, :, lo:hi]
        W1p[:, :, (2 * c + 1) * 128: (2 * c + 1) * 128 + n] = W1[:, :, 2730 + lo: 2730 + hi]
        b1p[:, 2 * c * 128: 2 * c * 128 + n] = b1[:, lo:hi]
        b1p[:, (2 * c + 1) * 128: (2 * c + 1) * 128 + n] = b1[:, 2730 + lo: 2730 + hi]
        W2p[:, c * 128: c * 128 + n, :] = W2[:, lo:hi, :]
    b1pr = b1p.reshape(6, 2 * NIC, 128).transpose(0, 2, 1).copy()

    shared = {
        "rot_tab": (cs8, ss8s, ci, sis),
        "mA01": mA, "mC01": mC, "allm01": allm, "ones01": zer,
        "triA": triA.astype(bf16), "identf": identf,
        "identb": identf.astype(bf16), "dupI": dupI.astype(bf16),
        "Wqkv": inputs["L0_ffn_Wqkv"], "Wo6": inputs["L0_ffn_Wo"],
        "W1p": W1p, "W2p": W2p, "b1pr": b1pr,
        "lnbr": inputs["L0_ffn_ln_b"].reshape(6, NCH, 128).transpose(0, 2, 1).copy(),
        "lngr": inputs["L0_ffn_ln_g"].reshape(6, NCH, 128).transpose(0, 2, 1).copy(),
        "b2r": inputs["L0_ffn_b2"].reshape(6, NCH, 128).transpose(0, 2, 1).copy(),
        "qsrep": np.broadcast_to(np.tile(inputs["L0_ffn_q_scale"], (1, H))[:, None, :],
                                 (6, 128, D)).copy(),
        "ksrep": np.broadcast_to(np.tile(inputs["L0_ffn_k_scale"], (1, H))[:, None, :],
                                 (6, 128, D)).copy(),
        "Wq": inputs["L0_attn_Wq"], "Wk": inputs["L0_attn_Wk"],
        "Wv": inputs["L0_attn_Wv"], "WoA": inputs["L0_attn_Wo"],
        "Wf1": inputs["L0_attn_Wf1"], "Wf2": inputs["L0_attn_Wf2"],
        "g1t": np.broadcast_to(inputs["L0_attn_ln1_g"][None, :], (128, D)).copy(),
        "b1t": np.broadcast_to(inputs["L0_attn_ln1_b"][None, :], (128, D)).copy(),
        "g2r": inputs["L0_attn_ln2_g"].reshape(NCH, 128).T.copy(),
        "b2ar": inputs["L0_attn_ln2_b"].reshape(NCH, 128).T.copy(),
    }

    maps = []
    for c in range(8):
        b, q = c // 4, c % 4
        s = q * 4096
        xfull = np.zeros((PREF + OWN, D), f32)
        lo = s - PREF
        xfull[max(0, -lo):, :] = x[b, max(0, lo): s + OWN]
        xb = xfull[PREF - HALO:, :].copy()
        gidx = np.clip(np.arange(-HALO, OWN) + s, 0, 16383)
        cs8c, ss8c, cic, sic = (t[gidx] for t in shared["rot_tab"])
        rotc = np.concatenate([cs8c, ss8c, cic, sic], axis=1).reshape(NS, 128, 256)
        pv = np.zeros((128, NPS), f32)
        pv[:, NPS - 32 * q:] = 1.0
        hv = np.ones((128, NS), f32)
        if q == 0:
            hv[:, OWN0] = 0.0
        m0 = shared["allm01"] if q == 0 else shared["mA01"]
        m1 = shared["allm01"] if q == 0 else shared["ones01"]
        m2 = shared["allm01"] if q == 0 else shared["mA01"]
        mh = np.hstack([shared["mA01"].T, shared["ones01"], shared["mC01"].T])
        mh0 = np.hstack([m0.T, m1.T, shared["mC01"].T])
        mh1 = np.hstack([m2.T, shared["ones01"], shared["mC01"].T])
        mp = {k: v for k, v in shared.items() if k not in ("rot_tab", "mA01", "mC01",
                                                           "allm01", "ones01")}
        mp.update({
            "xin": xfull, "xbuf": xb, "rotcat": np.ascontiguousarray(rotc),
            "pvalidT": pv, "hvalrow": hv,
            "mh_t": np.ascontiguousarray(mh.astype(bf16)),
            "mh0_t": np.ascontiguousarray(mh0.astype(bf16)),
            "mh1_t": np.ascontiguousarray(mh1.astype(bf16)),
        })
        maps.append(mp)
    return maps


def kernel(**inputs):
    maps = _host_prep(inputs)
    if "nc" not in _CACHE:
        _CACHE["nc"] = build_program()
    nc = _CACHE["nc"]
    res = run_bass_kernel_spmd(nc, maps, core_ids=list(range(8)))
    out = np.empty((2, 16384, D), np.float32)
    for c in range(8):
        b, q = c // 4, c % 4
        out[b, q * 4096:(q + 1) * 4096] = res.results[c]["out"]
    return out


# version 10
# speedup vs baseline: 1.2076x; 1.0267x over previous
import numpy as np
import ml_dtypes
import concourse.bass as bass
import concourse.mybir as mybir
from concourse.bass_utils import run_bass_kernel_spmd

F32 = mybir.dt.float32
BF16 = mybir.dt.bfloat16
AF = mybir.ActivationFunctionType
ALU = mybir.AluOpType
AX = mybir.AxisListType
bf16 = ml_dtypes.bfloat16

D = 1024
NCH = 8
H = 16
DH = 64
OWN = 4096
HALO = 2432
BUF = HALO + OWN
NS = BUF // 128
OWN0 = HALO // 128
PREF = 12288
NPS = PREF // 128
DI = 2816
NIC = DI // 128
DF = 4096
NOC = DF // 128
BIG = np.float32(3.0e38)


def ts(i, n=128):
    return slice(i * n, (i + 1) * n)


class EM:

    ENGS = ("pe", "dve", "act", "pool", "sy")
    NSYD = 10
    NGPD = 8
    TAGS = (("pe", "dve", "act", "pool", "sy")
            + tuple(f"syd{i}" for i in range(10))
            + tuple(f"gpd{i}" for i in range(8)))

    def __init__(self):
        self.stages = {e: [] for e in self.ENGS}
        self.cnt = {t: 0 for t in self.TAGS}
        self.seen = {e: {} for e in self.ENGS}
        self.buf = {}
        self.dma_rr = {"syd": 0, "gpd": 0}

    def _b(self, k):
        if k not in self.buf:
            self.buf[k] = {"w": [], "r": {}}
        return self.buf[k]

    def stage(self, e, fn, reads=(), writes=(), accw=(), dma=False, steps=1):
        waits = {}
        seen = self.seen[e]

        def need(tag, val):
            if seen.get(tag, 0) < val:
                if waits.get(tag, 0) < val:
                    waits[tag] = val

        rdkeys = list(reads)
        for k in reads:
            b = self._b(k)
            for wt, wv in b["w"]:
                need(wt, wv)
                if wt.startswith(("syd", "gpd")):
                    rdkeys.append("SEM_" + wt)
        for k in writes:
            b = self._b(k)
            for wt, wv in b["w"]:
                need(wt, wv)
            for t, v in b["r"].items():
                need(t, v)
        if dma:
            q = "syd" if e == "sy" else "gpd"
            n = self.NSYD if q == "syd" else self.NGPD
            tag = f"{q}{self.dma_rr[q] % n}"
            self.dma_rr[q] += 1
            inc = 16
            assert steps == 1
            b = self._b("SEM_" + tag)
            for wt, wv in b["w"]:
                need(wt, wv)
            for t, v in b["r"].items():
                need(t, v)
        else:
            tag = e
            inc = steps
        for t, v in waits.items():
            seen[t] = v
        self.cnt[tag] += inc
        v = self.cnt[tag]
        if dma:
            b = self._b("SEM_" + tag)
            b["w"] = [(tag, v)]
            b["r"] = {}
        for k in writes:
            b = self._b(k)
            b["w"] = [(tag, v)]
            b["r"] = {}
        for k in accw:
            b = self._b(k)
            b["w"].append((tag, v))
        for k in rdkeys:
            self._b(k)["r"][tag] = v
        self.stages[e].append((sorted(waits.items()), fn, tag, inc, steps))

    def alias(self, dst, src):
        self.buf[dst] = self._b(src)

    def barrier(self):
        snap = dict(self.cnt)
        for e in self.ENGS:
            waits = sorted((t, v) for t, v in snap.items() if v > 0 and t != e)
            seen = self.seen[e]
            for t, v in waits:
                if seen.get(t, 0) < v:
                    seen[t] = v
            waits = [(t, v) for t, v in waits]
            self.cnt[e] += 1
            self.stages[e].append((waits, lambda eng: eng.nop(), e, 1, 1))

    def replay(self, nc):
        sems = {}
        ctxs = []
        for t in self.TAGS:
            h = nc.semaphore(f"s_{t}")
            sems[t] = h.__enter__()
            ctxs.append(h)
        block = nc.Block().__enter__()
        engmap = {"pe": "tensor", "dve": "vector", "act": "scalar",
                  "pool": "gpsimd", "sy": "sync"}
        rcnt = {t: 0 for t in self.TAGS}
        for e in self.ENGS:
            def body(eng, e=e):
                for waits, fn, tag, inc, steps in self.stages[e]:
                    for t, v in waits:
                        eng.wait_ge(sems[t], v)
                    if steps == 1:
                        inst = fn(eng)
                        inst.then_inc(sems[tag], inc)
                    else:
                        base = rcnt[tag]
                        nb = [0]

                        def bump(pinst):
                            nb[0] += 1
                            pinst.then_inc(sems[tag], 1)
                            eng.wait_ge(sems[tag], base + nb[0])
                        inst = fn(eng, bump)
                        assert nb[0] == steps - 1, (e, nb[0], steps)
                        inst.then_inc(sems[tag], 1)
                    rcnt[tag] += inc
                if e == "sy":
                    for t in self.TAGS:
                        if t.startswith(("syd", "gpd")) and self.cnt[t] > 0:
                            eng.wait_ge(sems[t], self.cnt[t])
            getattr(block, engmap[e])(body)
        block.__exit__(None, None, None)
        for h in ctxs:
            h.__exit__(None, None, None)


def build_program(nlayers=6, do_a=True, dumps=(), bffn=True, dbg_sub=None, dbg_suba=None):
    nc = bass.Bass("TRN2", target_bir_lowering=False)
    dram = {}

    def din(name, shape, dt=F32):
        dram[name] = nc.dram_tensor(name, list(shape), dt, kind="ExternalInput")

    din("xin", [PREF + OWN, D])
    din("xbuf", [BUF, D])
    din("rotcat", [NS, 128, 256])
    din("pvalidT", [128, NPS])
    din("hvalrow", [128, NS])
    din("mh_t", [128, 384], BF16)
    din("mh0_t", [128, 384], BF16)
    din("mh1_t", [128, 384], BF16)
    din("triA", [128, 128], BF16)
    din("identf", [128, 128])
    din("identb", [128, 128], BF16)
    din("dupI", [64, 128], BF16)
    din("Wqkv", [6, D, 3 * D]); din("Wo6", [6, D, D])
    din("W1p", [6, D, 2 * DI]); din("W2p", [6, DI, D])
    din("b1pr", [6, 128, 2 * NIC])
    din("lnbr", [6, 128, NCH])
    din("lngr", [6, 128, NCH]); din("b2r", [6, 128, NCH])
    din("qsrep", [6, 128, D]); din("ksrep", [6, 128, D])
    din("Wq", [D, D]); din("Wk", [D, D]); din("Wv", [D, D]); din("WoA", [D, D])
    din("Wf1", [D, DF]); din("Wf2", [DF, D])
    din("g1t", [128, D]); din("b1t", [128, D])
    din("g2r", [128, NCH]); din("b2ar", [128, NCH])
    dram["out"] = nc.dram_tensor("out", [OWN, D], F32, kind="ExternalOutput")
    dram["ysc0"] = nc.dram_tensor("ysc0", [NS, NCH, 128, 128], F32)
    dram["ysc1"] = nc.dram_tensor("ysc1", [NS, NCH, 128, 128], F32)
    dram["xasc"] = nc.dram_tensor("xasc", [32, NCH, 128, 128], F32)
    dram["cvec"] = nc.dram_tensor("cvec", [DF], F32)
    for d in dumps:
        nm, src = d[0], d[1]
        shp = list(dram[src].shape)
        dram[nm] = nc.dram_tensor(nm, shp, F32, kind="ExternalOutput")

    sbuf = {}

    def S(name, shape, dt=F32):
        sbuf[name] = nc.alloc_sbuf_tensor(name, list(shape), dt).ap()

    S("identf_s", [128, 128]); S("identb_s", [128, 128], BF16)
    S("mh_s", [128, 384], BF16); S("mh0_s", [128, 384], BF16)
    S("mh1_s", [128, 384], BF16)
    S("triA_s", [128, 128], BF16); S("dupI_s", [64, 128], BF16)
    S("ones_c", [128, 1], BF16); S("ones_r", [1, 128], BF16)
    S("pval_s", [128, NPS]); S("hval_s", [128, NS])
    S("rot", [128, 2 * 256])
    S("qs_s", [128, D], BF16); S("ks_s", [128, D], BF16)
    S("lng_s", [128, NCH]); S("lnb_s", [128, NCH], BF16); S("b2r_s", [128, NCH])
    S("b1p_s", [128, 2 * NIC]); S("cb1_s", [128, 2 * NIC])
    S("g2r_s", [128, NCH]); S("b2a_s", [128, NCH], BF16); S("cbA_s", [128, NOC])
    S("trow", [1, 896]); S("trow2", [1, 896])
    S("mrow", [1, 512], BF16); S("rrow", [1, 512], BF16)
    S("ssum", [128, H]); S("rn", [128, H]); S("tt", [128, H])
    S("nmx", [128, H]); S("rsb", [128, H])
    S("den_t", [128, H]); S("rA", [128, H])
    S("ycar", [128, 4])
    S("ceps", [128, 1])
    AR = nc.alloc_sbuf_tensor("arena", [128, 67584], BF16).ap()
    WK = nc.alloc_sbuf_tensor("warena", [128, 26624], BF16).ap()
    sbuf["arena"] = AR

    def view(base, byte_off, dt, ncols, npart=128):
        assert byte_off % 2 == 0
        c0 = byte_off // 2
        if dt == BF16:
            ap = base[0:npart, c0:c0 + ncols]
        else:
            ap = base[0:npart, c0:c0 + 2 * ncols].bitcast(F32)
        return ap

    sbuf["ytile"] = view(AR, 49152, F32, 8 * NCH * 128)
    sbuf["afm"] = view(AR, 81920, BF16, NCH * 512)
    sbuf["kfm"] = view(AR, 106496, BF16, 4 * D)
    sbuf["vkm"] = view(WK, 40960, BF16, 4 * 16 * 65)
    sbuf["Pt"] = view(AR, 122880, BF16, 2 * 4 * 384)
    sbuf["ptsb"] = view(AR, 129024, BF16, 2 * 12 * 128)
    sbuf["xag"] = view(AR, 49152, F32, NCH * 512)
    sbuf["aAfm"] = view(AR, 65536, BF16, NCH * 512)
    sbuf["xld"] = view(AR, 73728, F32, 2 * D)
    sbuf["xnfm"] = view(AR, 81920, F32, NCH * 128)
    sbuf["xnt"] = view(AR, 86016, F32, D)
    sbuf["Qfm"] = view(AR, 106496, BF16, D)
    sbuf["Kfm"] = view(AR, 108544, BF16, D)
    sbuf["Kkm"] = view(AR, 110592, BF16, D)
    sbuf["Vkm2"] = view(AR, 112640, BF16, D)
    sbuf["h1b"] = view(AR, 114688, BF16, D)
    sbuf["h1fm"] = view(AR, 116736, BF16, NCH * 128)
    sbuf["aAkm"] = view(AR, 118784, BF16, D)
    sbuf["ATs"] = view(AR, 120832, BF16, 2 * 128)
    sbuf["SbD"] = view(AR, 121344, BF16, H * 64)
    sbuf["zbD"] = view(AR, 123392, BF16, H)
    sbuf["Sacc"] = view(AR, 123456, F32, H * 64, npart=64)
    sbuf["zacc"] = view(AR, 127552, F32, H, npart=64)
    sbuf["SaccB"] = view(AR, 127616, BF16, H * 64, npart=64)
    sbuf["zaccB"] = view(AR, 129664, BF16, H, npart=64)
    sbuf["prodb"] = view(AR, 131072, BF16, D)
    sbuf["qA"] = view(WK, 0, F32, 2 * D)
    sbuf["kA"] = view(WK, 8192, F32, 2 * D)
    sbuf["t1"] = view(WK, 16384, F32, D)
    sbuf["t2"] = view(WK, 20480, F32, D)
    sbuf["t3"] = view(WK, 24576, F32, D)
    sbuf["qD"] = view(WK, 28672, BF16, D)
    sbuf["kD"] = view(WK, 30720, BF16, D)
    sbuf["qfm"] = view(WK, 32768, BF16, D)
    sbuf["ubf"] = view(WK, 34816, BF16, 2 * NCH * 128)
    sbuf["akm"] = view(WK, 38912, BF16, D)
    sbuf["ymg"] = view(WK, 0, F32, 2 * NCH * 384)
    sbuf["u2g"] = view(WK, 24576, BF16, NCH * 384)
    sbuf["fsb"] = view(WK, 30720, BF16, NIC * 384)
    sbuf["sqt"] = view(WK, 47616, BF16, 2 * 384)
    sbuf["tf1"] = view(WK, 49152, F32, 384)
    sbuf["tf2"] = view(WK, 50688, F32, 384)
    sbuf["g1t_s"] = view(WK, 40960, F32, D)
    sbuf["b1t_s"] = view(WK, 45056, F32, D)
    sbuf["xag2"] = view(WK, 28672, F32, NCH * 128)
    sbuf["yg"] = view(WK, 32768, F32, NCH * 128)
    sbuf["xkm"] = view(WK, 36864, F32, D)
    sbuf["fA"] = view(WK, 40960, BF16, NOC * 128)
    sbuf["xab"] = view(WK, 49152, BF16, NCH * 128)
    sbuf["xac"] = view(WK, 51200, BF16, NCH * 128)
    sbuf["prkm"] = view(WK, 20480, BF16, D)
    sbuf["sqA"] = view(WK, 22528, BF16, 2 * 128)
    sbuf["t1b"] = view(WK, 0, F32, D)
    sbuf["t3b"] = view(WK, 4096, F32, D)
    sbuf["h1fm2"] = view(WK, 8192, BF16, NCH * 128)
    sbuf["ssb"] = sbuf["xag2"]
    sbuf["gsb"] = sbuf["yg"]
    sbuf["outk"] = sbuf["xkm"]

    ps = {}
    ps["P1"] = nc.alloc_psum_tensor("psP1", [128, 1536], F32)
    ps["P2"] = nc.alloc_psum_tensor("psP2", [128, 1536], F32)
    ps["trE"] = nc.alloc_psum_tensor("psE", [128, 1024], BF16)
    ps["trF"] = nc.alloc_psum_tensor("psF", [128, 1024], BF16)

    em = EM()
    st = dict(em=em, T=sbuf, ps=ps, dram=dram)
    st["bffn"] = bffn
    st["dbg_sub"] = dbg_sub
    st["dbg_suba"] = dbg_suba
    if dbg_suba is not None:
        for nm, shp in (("daA", [128, D]), ("dQf", [128, D]), ("dKf", [128, D]),
                        ("dVk", [128, D]), ("dSb", [128, H * 64]), ("dzb", [128, H])):
            dram[nm] = nc.dram_tensor(nm, shp, F32, kind="ExternalOutput")
    if dbg_sub is not None:
        for nm, cols in (("dakm", D), ("dqfm", D), ("dkfm", 3 * D), ("dvkm", 3 * D)):
            dram[nm] = nc.dram_tensor(nm, [128, cols], F32, kind="ExternalOutput")
    _p_consts(st)
    _p_layer0(st)
    for l in range(1, nlayers + 1):
        _p_blayer(st, l)
    if do_a:
        _p_abranch(st, nlayers)
    for d in dumps:
        _p_dump(st, d[0], d[1], d[2] if len(d) > 2 else ())
    em.replay(nc)
    return nc


def _p_consts(st):
    em, T, dram = st["em"], st["T"], st["dram"]
    for dst, src in (("identf_s", "identf"), ("identb_s", "identb"),
                     ("mh_s", "mh_t"), ("mh0_s", "mh0_t"), ("mh1_s", "mh1_t"),
                     ("triA_s", "triA"), ("pval_s", "pvalidT"), ("hval_s", "hvalrow"),
                     ("g2r_s", "g2r")):
        em.stage("sy", lambda e, d=dst, s=src: e.dma_start(out=T[d][:], in_=dram[s].ap()),
                 writes=[dst], dma=True)
    em.stage("pool", lambda e: e.dma_start(out=T["dupI_s"][:], in_=dram["dupI"].ap()),
             writes=["dupI_s"], dma=True)
    em.stage("pool", lambda e: e.dma_start(out=T["b2a_s"][:], in_=dram["b2ar"].ap()),
             writes=["b2a_s"], dma=True)
    em.stage("pool", lambda e: e.memset(T["ones_c"][:], 1.0), writes=["ones_c"])
    em.stage("pool", lambda e: e.memset(T["ones_r"][:], 1.0), writes=["ones_r"])
    em.stage("pool", lambda e: e.memset(T["ceps"][:], 1e-5), writes=["ceps"])


def _p_dump(st, nm, src, keys=()):
    em, dram = st["em"], st["dram"]
    em.stage("sy", lambda e: e.dma_start(out=dram[nm].ap().flatten(),
                                         in_=dram[src].ap().flatten()),
             reads=list(keys), dma=True)


def _p_ln(st, src_t, dst_t, ksrc, kdst, g=None, b=None, t3n="t3"):
    em, T = st["em"], st["T"]
    su, tt, rn = T["ssum"], T["tt"], T["rn"]

    em.stage("pool", lambda e: e.tensor_mul(T[t3n][:], src_t, src_t),
             reads=[ksrc], writes=[t3n])

    def f1(e, bump):
        e.reduce_sum(su[:, 0:1].unsqueeze(1), src_t.unsqueeze(1), AX.X)
        i = e.reduce_sum(su[:, 1:2].unsqueeze(1), T[t3n][:].unsqueeze(1), AX.X)
        bump(i)
        e.tensor_scalar_mul(su[:, 2:3], su[:, 0:1], 1.0 / 1024.0)
        i = e.tensor_scalar_mul(su[:, 3:4], su[:, 1:2], 1.0 / 1024.0)
        bump(i)
        i = e.tensor_mul(tt[:, 0:1], su[:, 2:3], su[:, 2:3])
        bump(i)
        return e.tensor_sub(tt[:, 1:2], su[:, 3:4], tt[:, 0:1])
    em.stage("dve", f1, reads=[ksrc, t3n], writes=["lnstat"], steps=4)

    def f2(e, bump):
        i = e.activation(tt[:, 2:3], tt[:, 1:2], AF.Ln, bias=T["ceps"][:])
        bump(i)
        return e.activation(rn[:, 0:1], tt[:, 2:3], AF.Exp, scale=-0.5)
    em.stage("act", f2, reads=["lnstat", "ceps"], writes=["lnr"], steps=2)

    def f3(e, bump):
        i = e.tensor_scalar_mul(tt[:, 3:4], su[:, 2:3], -1.0)
        bump(i)
        return e.tensor_scalar(dst_t, src_t, tt[:, 3:4], rn[:, 0:1], ALU.add, ALU.mult)
    em.stage("dve", f3, reads=["lnstat", "lnr", ksrc], writes=[kdst], steps=2)

    if g is not None:
        def f4(e, bump):
            i = e.tensor_mul(T[t3n][:], dst_t, g)
            bump(i)
            return e.tensor_add(dst_t, T[t3n][:], b)
        em.stage("pool", f4, reads=[kdst, "g1t_s", "b1t_s"], writes=[kdst, t3n],
                 steps=2)


def _p_transpose8(st, src_t, kin, pkey, dt="b"):
    em, T, ps = st["em"], st["T"], st["ps"]
    ident = T["identb_s"] if dt == "b" else T["identf_s"]
    ikey = "identb_s" if dt == "b" else "identf_s"

    def f(e):
        for c in range(NCH):
            i = e.transpose(ps[pkey][:, ts(c)], src_t[:, ts(c)], ident[:])
        return i
    em.stage("pe", f, reads=[kin, ikey], writes=[pkey])


def _p_layer0(st):
    em, T, ps, dram = st["em"], st["T"], st["ps"], st["dram"]
    xbuf = dram["xbuf"].ap()
    y0, y1 = dram["ysc0"].ap(), dram["ysc1"].ap()
    for m in range(NS):
        slot = m % 2
        xl = T["xld"][:, slot * D:(slot + 1) * D]
        em.stage("sy", lambda e, m=m, xl=xl: e.dma_start(out=xl, in_=xbuf[ts(m), :]),
                 writes=[f"xld{slot}"], dma=True)
        _p_ln(st, xl[:, :], T["xnt"][:, :], f"xld{slot}", "xnt")
        _p_transpose8(st, T["xnt"][:], "xnt", "P1", dt="f")
        em.stage("dve", lambda e: e.tensor_copy(T["xnfm"][:], ps["P1"][:, 0:1024]),
                 reads=["P1"], writes=["xnfm"])
        em.stage("sy", lambda e, m=m: e.dma_start(
            out=y0[m].rearrange("c p t -> p c t"),
            in_=T["xnfm"][:].rearrange("p (c t) -> p c t", c=NCH)),
            reads=["xnfm"], writes=[f"ysrc_1_{m}"], dma=True)
        if m < OWN0:
            em.stage("sy", lambda e, m=m: e.dma_start(
                out=y1[m].rearrange("c p t -> p c t"),
                in_=T["xnfm"][:].rearrange("p (c t) -> p c t", c=NCH)),
                reads=["xnfm"], writes=[f"y1init_{m}"], dma=True)


def _p_blayer(st, l):
    em, T, ps, dram = st["em"], st["T"], st["ps"], st["dram"]
    em.barrier()
    G = 3 * l + 1
    kv0 = G - 3
    q0 = G - 1
    ysrc = dram["ysc0" if l % 2 == 1 else "ysc1"].ap()
    ydst = dram["ysc1" if l % 2 == 1 else "ysc0"].ap()
    A = T["arena"]
    WQKV, WO = 0, 45056
    yt = T["ytile"][:].rearrange("p (s c t) -> p s c t", s=8, c=NCH)
    em.stage("pool", lambda e: e.memset(
        T["vkm"][:].rearrange("p (g c) -> p g c", c=65)[:, :, 64:65], 1.0),
        writes=["vkm_ones"])

    for dst, src in (("lng_s", "lngr"), ("b2r_s", "b2r"), ("b1p_s", "b1pr")):
        em.stage("sy", lambda e, d=dst, s=src: e.dma_start(
            out=T[d][:], in_=dram[s].ap()[l - 1]), writes=[dst], dma=True)
    for dst, src in (("qs_s", "qsrep"), ("ks_s", "ksrep")):
        em.stage("pool", lambda e, d=dst, s=src: e.dma_start(
            out=T[d][:], in_=dram[s].ap()[l - 1]), writes=[dst], dma=True)
    em.stage("pool", lambda e: e.dma_start(
        out=T["lnb_s"][:], in_=dram["lnbr"].ap()[l - 1]), writes=["lnb_s"], dma=True)

    wq, wo = dram["Wqkv"].ap()[l - 1], dram["Wo6"].ap()[l - 1]
    for c in range(NCH):
        em.stage("pool", lambda e, c=c: e.dma_start(
            out=A[:, WQKV + c * 3072: WQKV + (c + 1) * 3072], in_=wq[ts(c), :]),
            writes=["slotW1"] if c == 0 else [], accw=["slotW1"] if c > 0 else [],
            dma=True)
    for c in range(NCH):
        em.stage("pool", lambda e, c=c: e.dma_start(
            out=A[:, WO + c * 1024: WO + (c + 1) * 1024], in_=wo[ts(c), :]),
            writes=["slotW2"] if c == 0 else [], accw=["slotW2"] if c > 0 else [],
            dma=True)

    def load_y(mm):
        if mm >= NS:
            return
        sl = (mm - G) % 8
        if mm == kv0 - 1 and l > 1:
            key = f"ymid_{l - 1}_{mm}"
        else:
            key = f"ysrc_{l}_{mm}"
        em.stage("sy", lambda e, mm=mm, sl=sl: e.dma_start(
            out=yt[:, sl],
            in_=ysrc[mm].rearrange("c p t -> p c t")),
            reads=[key], writes=[f"yt{sl}"], dma=True)

    def build_u(mm):
        if mm >= NS:
            return
        ysl = (mm - G) % 8
        psl = (mm - 1 - G) % 8
        usl = mm % 2
        ub = T["ubf"][:, usl * 1024:(usl + 1) * 1024].rearrange("p (c t) -> p c t", c=NCH)

        def fu(e, ub=ub, ysl=ysl, psl=psl, mm=mm):
            e.tensor_copy(ub[:, 0:4], yt[:, ysl, 0:4])
            e.tensor_copy(ub[:, 4:8, 1:128], yt[:, ysl, 4:8, 0:127])
            return e.tensor_scalar_mul(ub[:, 4:8, 0:1], yt[:, psl, 4:8, 127:128],
                                       T["hval_s"][:, mm:mm + 1])
        em.stage("pool", fu, reads=[f"yt{ysl}", f"yt{psl}", "hval_s"],
                 writes=[f"ubf{usl}"])

    def emit_rot(mm):
        if mm >= NS:
            return
        us = mm % 2
        em.stage("pool", lambda e, mm=mm, us=us: e.dma_start(
            out=T["rot"][:, us * 256:(us + 1) * 256], in_=dram["rotcat"].ap()[mm]),
            writes=[f"rot{us}"], dma=True)

    def emit_qkv_part(mm, part):
        if mm >= NS:
            return
        us = mm % 2
        ubv = T["ubf"][:, us * 1024:(us + 1) * 1024].rearrange("p (c t) -> p c t",
                                                              c=NCH)
        off = 0 if part == "q" else 1536

        def fmm(e, off=off, ubv=ubv):
            for c in range(NCH):
                for sl in range(3):
                    i = e.matmul(ps["P1"][:, sl * 512:(sl + 1) * 512],
                                 ubv[:, c],
                                 A[:, WQKV + c * 3072 + off + sl * 512:
                                    WQKV + c * 3072 + off + (sl + 1) * 512],
                                 start=(c == 0), stop=(c == NCH - 1))
            return i
        em.stage("pe", fmm, reads=[f"ubf{us}", "slotW1"], writes=["P1"])
        if part == "q":
            em.stage("dve", lambda e, us=us: e.tensor_copy(
                T["qA"][:, us * D:(us + 1) * D], ps["P1"][:, 0:1024]),
                reads=["P1"], writes=[f"qA{us}"])
            em.stage("act", lambda e, us=us: e.copy(
                T["kA"][:, us * D:us * D + 512], ps["P1"][:, 1024:1536]),
                reads=["P1"], writes=[f"kAlo{us}"])
        else:
            em.stage("act", lambda e, us=us: e.copy(
                T["kA"][:, us * D + 512:(us + 1) * D], ps["P1"][:, 0:512]),
                reads=["P1"], writes=[f"kA{us}"])
            vkmr = T["vkm"][:, (mm % 4) * 1040:((mm % 4) + 1) * 1040].rearrange(
                "p (h c) -> p h c", c=65)[:, :, 0:64]
            em.stage("dve", lambda e, v=vkmr, mm=mm: e.tensor_copy(
                v, ps["P1"][:, 512:1536].rearrange("p (h c) -> p h c", c=64)),
                reads=["P1"], writes=[f"vkm{mm % 4}"])

    def emit_chain(m, which):
        us = m % 2
        rot = T["rot"][:, us * 256:(us + 1) * 256]
        sk = "qA" if which == "q" else "kA"
        skkey = f"{sk}{us}"
        skeys = [skkey] + ([f"kAlo{us}"] if which == "k" else [])
        dst = "qD" if which == "q" else "kD"
        sct = "qs_s" if which == "q" else "ks_s"
        coff = 0 if which == "q" else 128
        soff = 64 if which == "q" else 192
        src_t = T[sk][:, us * D:(us + 1) * D]
        seg = src_t.rearrange("p (h d) -> p h d", h=H)

        em.stage("pool", lambda e, s=src_t: e.tensor_mul(T["t3"][:], s, s),
                 reads=skeys, writes=["t3"])

        def fr(e, bump):
            i = e.reduce_sum(T["ssum"][:].unsqueeze(2),
                             T["t3"][:].rearrange("p (h d) -> p h d", h=H), AX.X)
            bump(i)
            return e.tensor_scalar_max(T["ssum"][:], T["ssum"][:], 1e-24)
        em.stage("dve", fr, reads=["t3"], writes=["ssum"], steps=2)

        def fe(e, bump):
            i = e.activation(T["tt"][:], T["ssum"][:], AF.Ln)
            bump(i)
            return e.activation(T["rn"][:], T["tt"][:], AF.Exp, scale=-0.5)
        em.stage("act", fe, reads=["ssum"], writes=["rn"], steps=2)

        def fm1(e, seg=seg):
            return e.tensor_tensor(
                T["t1"][:].rearrange("p (h d) -> p h d", h=H), seg,
                T["rn"][:].unsqueeze(2).broadcast_to([128, H, DH]), ALU.mult)
        em.stage("dve", fm1, reads=skeys + ["rn"], writes=["t1"])

        def fm2(e, bump, sct=sct, coff=coff, rot=rot):
            t1s = T["t1"][:].rearrange("p (h d) -> p h d", h=H)
            i = e.tensor_tensor(T["t1"][:], T["t1"][:], T[sct][:], ALU.mult)
            bump(i)
            return e.tensor_tensor(
                T["t2"][:].rearrange("p (h d) -> p h d", h=H), t1s,
                rot[:, coff:coff + 64].unsqueeze(1).broadcast_to([128, H, DH]),
                ALU.mult)
        em.stage("pool", fm2, reads=["t1", sct, f"rot{us}"],
                 writes=["t1", "t2"], steps=2)

        def fm3(e, bump, dst=dst, soff=soff, rot=rot):
            sg = T[dst][:].rearrange("p (h d) -> p h d", h=H)
            t1s = T["t1"][:].rearrange("p (h d) -> p h d", h=H)
            t2s = T["t2"][:].rearrange("p (h d) -> p h d", h=H)
            t3s = T["t3"][:].rearrange("p (h d) -> p h d", h=H)
            e.tensor_tensor(t3s[:, :, 0:32], t1s[:, :, 32:64],
                            rot[:, soff:soff + 32].unsqueeze(1).broadcast_to([128, H, 32]),
                            ALU.mult)
            i = e.tensor_tensor(t3s[:, :, 32:64], t1s[:, :, 0:32],
                                rot[:, soff + 32:soff + 64].unsqueeze(1).broadcast_to([128, H, 32]),
                                ALU.mult)
            bump(i)
            return e.tensor_add(sg, t2s, t3s)
        em.stage("dve", fm3, reads=["t1", "t2", f"rot{us}"],
                 writes=[dst, "t3"], steps=2)

    def collect(emitfn):
        rec = []
        real = em.stage
        em.stage = lambda *aa, **kk: rec.append((aa, kk))
        try:
            emitfn()
        finally:
            em.stage = real
        return rec

    def flush(rec):
        for aa, kk in rec:
            em.stage(*aa, **kk)

    def zip_flush(wv, ch, ratio=2):
        wi = ci = 0
        while wi < len(wv) or ci < len(ch):
            for _ in range(ratio):
                if wi < len(wv):
                    aa, kk = wv[wi]; wi += 1
                    em.stage(*aa, **kk)
            if ci < len(ch):
                aa, kk = ch[ci]; ci += 1
                em.stage(*aa, **kk)

    for m in range(kv0, NS):
        yslot = (m - G) % 8
        if m == kv0:
            for mm in (m - 1, m, m + 1, m + 2):
                load_y(mm)
            build_u(m)
            emit_rot(m)
            emit_qkv_part(m, "q")
            emit_qkv_part(m, "kv")
            emit_chain(m, "q")
            emit_chain(m, "k")
        else:
            load_y(m + 2)
        build_u(m + 1)
        emit_rot(m + 1)

        uslot = m % 2
        kfmr = T["kfm"][:, (m % 4) * D:((m % 4) + 1) * D]

        emit_qkv_part(m + 1, "q")
        emit_qkv_part(m + 1, "kv")

        _p_transpose8(st, T["qD"][:], "qD", "trE")
        em.stage("dve", lambda e: e.tensor_copy(T["qfm"][:], ps["trE"][:]),
                 reads=["trE"], writes=["qfm"])
        _p_transpose8(st, T["kD"][:], "kD", "trF")
        em.stage("act", lambda e, k=kfmr: e.copy(k[:], ps["trF"][:]),
                 reads=["trF"], writes=[f"kfm{m % 4}"])

        chain_rec = collect(lambda: (emit_chain(m + 1, "q"),
                                     emit_chain(m + 1, "k"))) if m + 1 < NS else []

        if m < q0:
            flush(chain_rec)
            continue

        def wv_parts(w):
            pkey = "P1" if w % 2 == 0 else "P2"
            pw = ps[pkey]
            wslot = w % 2
            return pkey, pw, wslot

        def wv_sim(w, m=m):
            pkey, pw, wslot = wv_parts(w)
            kreads = ["qfm"] + [f"kfm{(m - 2 + kc) % 4}" for kc in range(3)]

            def fsim(e, w=w, pw=pw, m=m):
                prev_pb = None
                for hi in (0, 2, 1, 3):
                    h = 4 * w + hi
                    pb = (h % 2) * 64
                    pc = (h // 2) * 128

                    def kv(kc):
                        return T["kfm"][:, ((m - 2 + kc) % 4) * D:
                                        (((m - 2 + kc) % 4) + 1) * D][pb:pb + 64,
                                                                     pc:pc + 128]
                    qv = T["qfm"][pb:pb + 64, pc:pc + 128]
                    if prev_pb is not None and prev_pb != pb:
                        e.ldweights(T["identb_s"][:])
                    prev_pb = pb
                    for kc in range(3):
                        i = e.matmul(pw[:, hi * 384 + kc * 128: hi * 384 + (kc + 1) * 128],
                                     qv, kv(kc), start=True, stop=True)
                return i
            em.stage("pe", fsim, reads=kreads, writes=[pkey])

        def wv_maxexp(w):
            pkey, pw, wslot = wv_parts(w)
            em.stage("dve", lambda e, w=w, pw=pw: e.reduce_max(
                T["nmx"][:, 4 * w:4 * w + 4].unsqueeze(2),
                pw[:, 0:1536].rearrange("p (h x) -> p h x", h=4), AX.X, negate=True),
                reads=[pkey], writes=["nmx"])

            def fexp(e, w=w, pw=pw, wslot=wslot):
                for hi in range(4):
                    i = e.activation(T["Pt"][:, wslot * 1536:(wslot + 1) * 1536][:, hi * 384:(hi + 1) * 384],
                                     pw[:, hi * 384:(hi + 1) * 384], AF.Exp,
                                     bias=T["nmx"][:, 4 * w + hi:4 * w + hi + 1])
                return i
            em.stage("act", fexp, reads=[pkey, "nmx"], writes=[f"Pt{wslot}"])

        def wv_ptav(w, m=m):
            pkey, pw, wslot = wv_parts(w)
            for half in range(2):
                tk = "trE" if half == 0 else "trF"

                def ftr(e, half=half, wslot=wslot, tk=tk):
                    for j in range(6):
                        i = e.transpose(ps[tk][:, j * 128:(j + 1) * 128],
                                        T["Pt"][:, wslot * 1536:(wslot + 1) * 1536][:, half * 768 + j * 128:
                                                       half * 768 + (j + 1) * 128],
                                        T["identb_s"][:])
                    return i
                em.stage("pe", ftr, reads=[f"Pt{wslot}", "identb_s"], writes=[tk])
                mh = (T["mh0_s"] if m == OWN0 else
                      (T["mh1_s"] if m == OWN0 + 1 else T["mh_s"]))

                def fev(e, half=half, wslot=wslot, tk=tk, mh=mh):
                    pb = T["ptsb"][:, wslot * 1536:(wslot + 1) * 1536][:, half * 768:(half + 1) * 768]
                    for seg in range(2):
                        i = e.tensor_tensor(pb[:, seg * 384:(seg + 1) * 384],
                                            ps[tk][:, seg * 384:(seg + 1) * 384],
                                            mh[:], ALU.mult)
                    return i
                em.stage("dve", fev, reads=[tk, "mh_s", "mh0_s", "mh1_s"],
                         writes=[f"ptsb{wslot}"])

            vreads = [f"ptsb{wslot}", "vkm_ones"] + [f"vkm{(m - 2 + kc) % 4}"
                                                     for kc in range(3)]

            def fav(e, w=w, pw=pw, wslot=wslot, m=m):
                for hi in range(4):
                    for kc in range(3):
                        pt = T["ptsb"][:, wslot * 1536:(wslot + 1) * 1536][:, (hi * 3 + kc) * 128:(hi * 3 + kc + 1) * 128]
                        vk = T["vkm"][:, ((m - 2 + kc) % 4) * 1040:
                                      (((m - 2 + kc) % 4) + 1) * 1040].rearrange(
                            "p (h c) -> p h c", c=65)[:, 4 * w + hi]
                        i = e.matmul(pw[:, hi * 65:(hi + 1) * 65], pt, vk,
                                     start=(kc == 0), stop=(kc == 2))
                return i
            em.stage("pe", fav, reads=vreads, writes=[pkey])

            def fdiv(e, bump, w=w, pw=pw):
                pv = pw[:, 0:260].rearrange("p (h c) -> p h c", c=65)
                i = e.reciprocal(T["rsb"][:, 4 * w:4 * w + 4], pv[:, :, 64])
                bump(i)
                return e.tensor_tensor(
                    T["akm"][:].rearrange("p (h d) -> p h d", h=H)[:, 4 * w:4 * w + 4],
                    pv[:, :, 0:64],
                    T["rsb"][:, 4 * w:4 * w + 4].unsqueeze(2).broadcast_to([128, 4, DH]),
                    ALU.mult)
            em.stage("dve", fdiv, reads=[pkey], writes=["akm", "rsb"], steps=2)

        def all_waves():
            wv_sim(0)
            wv_maxexp(0)
            for w in range(4):
                if w + 1 < 4:
                    wv_sim(w + 1)
                    wv_maxexp(w + 1)
                wv_ptav(w)
        wave_rec = collect(all_waves)
        zip_flush(wave_rec, chain_rec, ratio=2)

        gi = (m - G) % 4
        _p_transpose8(st, T["akm"][:], "akm", "trF")
        em.stage("dve", lambda e, gi=gi: e.tensor_copy(
            T["afm"][:].rearrange("p (c g t) -> p c g t", c=NCH, g=4)[:, :, gi],
            ps["trF"][:].rearrange("p (c t) -> p c t", c=NCH)),
            reads=["trF"], writes=["afm"])

        if m == q0 or (m - G) % 4 == 3 or m == NS - 1:
            ng = 1 if m == q0 else (m - G) % 4 + 1
            gi0 = (m - ng + 1 - G) % 4
            af = T["afm"][:].rearrange("p (c t) -> p c t", c=NCH)
            g0s = (m - ng + 1 - G) % 8
            yrd = [f"yt{(g0s + j) % 8}" for j in range(ng)]
            for op in range(4):
                pk = "P1" if op % 2 == 0 else "P2"

                def fwo(e, op=op, pk=pk, ng=ng, af=af, gi0=gi0):
                    for oc in (2 * op, 2 * op + 1):
                        half = oc % 2
                        for ic in range(NCH):
                            i = e.matmul(ps[pk][:, half * 512: half * 512 + ng * 128],
                                         A[:, WO + ic * 1024 + oc * 128:
                                            WO + ic * 1024 + (oc + 1) * 128],
                                         af[:, ic, gi0 * 128:(gi0 + ng) * 128],
                                         start=(ic == 0), stop=(ic == NCH - 1))
                    return i
                em.stage("pe", fwo, reads=["afm", "slotW2"], writes=[pk])

                def fadd(e, op=op, pk=pk, ng=ng, g0s=g0s):
                    for oc in (2 * op, 2 * op + 1):
                        half = oc % 2
                        i = e.tensor_tensor(
                            yt[:, g0s:g0s + ng, oc],
                            ps[pk][:, half * 512: half * 512 + ng * 128]
                            .rearrange("p (s t) -> p s t", s=ng),
                            yt[:, g0s:g0s + ng, oc],
                            ALU.add)
                    return i
                em.stage("dve", fadd, reads=[pk] + yrd, writes=yrd)

            for j in range(ng):
                mm = m - ng + 1 + j
                sl = (g0s + j) % 8
                em.stage("sy", lambda e, mm=mm, sl=sl: e.dma_start(
                    out=ydst[mm].rearrange("c p t -> p c t"), in_=yt[:, sl]),
                    reads=yrd, writes=[f"ymid_{l}_{mm}"], dma=True)

    if st.get("bffn", True):
        _p_bffn(st, l, ydst)


def _p_bffn(st, l, ydst):
    em, T, ps, dram = st["em"], st["T"], st["ps"], st["dram"]
    em.barrier()
    G = 3 * l + 1
    A = T["arena"]
    W1, W2 = 0, 45056
    w1, w2 = dram["W1p"].ap()[l - 1], dram["W2p"].ap()[l - 1]

    for c in range(NCH):
        em.stage("pool", lambda e, c=c: e.dma_start(
            out=A[:, W1 + c * 5632: W1 + (c + 1) * 5632], in_=w1[ts(c), :]),
            writes=["slotW1"] if c == 0 else [], accw=["slotW1"] if c > 0 else [],
            dma=True)
    for c in range(NIC):
        em.stage("pool", lambda e, c=c: e.dma_start(
            out=A[:, W2 + c * 1024: W2 + (c + 1) * 1024], in_=w2[ts(c), :]),
            writes=["slotW2"] if c == 0 else [], accw=["slotW2"] if c > 0 else [],
            dma=True)

    for seg in range(11):
        pk, pk2 = ("P1", "P2") if seg % 2 == 0 else ("P2", "P1")

        def fseg(e, seg=seg, pk=pk):
            for c in range(NCH):
                i = e.matmul(ps[pk][0:1, 0:512], T["lnb_s"][:, c:c + 1],
                             A[:, W1 + c * 5632 + seg * 512: W1 + c * 5632 + (seg + 1) * 512],
                             start=(c == 0), stop=(c == NCH - 1))
            return i
        em.stage("pe", fseg, reads=["lnb_s", "slotW1"], writes=[pk])
        em.stage("act", lambda e, pk=pk: e.copy(T["mrow"][0:1, 0:512],
                                                ps[pk][0:1, 0:512]),
                 reads=[pk], writes=["mrow"])

        def fsp2(e, pk2=pk2):
            for u in range(4):
                i = e.matmul(ps[pk2][:, u:u + 1],
                             T["mrow"][0:1, u * 128:(u + 1) * 128],
                             T["ones_c"][0:1, 0:1], start=True, stop=True)
            return i
        em.stage("pe", fsp2, reads=["mrow", "ones_c"], writes=[pk2])
        em.stage("dve", lambda e, seg=seg, pk2=pk2: e.tensor_copy(
            T["cb1_s"][:, seg * 4:(seg * 4) + 4], ps[pk2][:, 0:4]),
            reads=[pk2], writes=["cb1_s"] if seg == 10 else [])
    em.stage("dve", lambda e: e.tensor_add(T["cb1_s"][:], T["cb1_s"][:], T["b1p_s"][:]),
             reads=["cb1_s", "b1p_s"], writes=["cb1_s"])

    TG = 3
    ngroups = (NS - G + TG - 1) // TG

    def grp(jg):
        s0 = G + jg * TG
        nsub = min(TG, NS - s0)
        return s0, nsub, nsub * 128, jg % 2

    def emit_u2_loads(jg):
        s0, nsub, TC, gslot = grp(jg)
        u2 = T["u2g"][:].rearrange("p (c t) -> p c t", c=NCH, t=384)
        ym = T["ymg"][:, gslot * (NCH * 384):(gslot + 1) * (NCH * 384)].rearrange(
            "p (c t) -> p c t", c=NCH, t=384)
        ymk = [f"ymid_{l}_{s0 + j}" for j in range(nsub)]
        for c4 in range(4):
            em.stage("pool", lambda e, s0=s0, nsub=nsub, c4=c4, u2=u2: e.dma_start(
                out=u2[:, c4, 0:nsub * 128].rearrange("p (s t) -> p s t", s=nsub),
                in_=ydst[s0:s0 + nsub, c4].rearrange("s p t -> p s t")),
                reads=ymk, writes=["u2g"] if c4 == 0 else [],
                accw=["u2g"] if c4 > 0 else [], dma=True)
        for c4 in range(4, 8):
            em.stage("pool", lambda e, s0=s0, nsub=nsub, c4=c4, u2=u2: e.dma_start(
                out=u2[:, c4, 0:nsub * 128].rearrange("p (s t) -> p s t", s=nsub)[:, :, 1:128],
                in_=ydst[s0:s0 + nsub, c4].rearrange("s p t -> p s t")[:, :, 0:127]),
                reads=ymk, accw=["u2g"], dma=True)
        if nsub > 1:
            for c4 in range(4, 8):
                em.stage("pool", lambda e, s0=s0, nsub=nsub, c4=c4, u2=u2: e.dma_start(
                    out=u2[:, c4, 128:nsub * 128].rearrange(
                        "p (s t) -> p s t", s=nsub - 1)[:, :, 0:1],
                    in_=ydst[s0:s0 + nsub - 1, c4].rearrange(
                        "s p t -> p s t")[:, :, 127:128],
                    allow_slow_non_contiguous=True),
                    reads=ymk, accw=["u2g"], dma=True)
        if jg == 0:
            def fu2d(e, s0=s0, u2=u2):
                return e.dma_start(
                    out=u2[:, 4:8, 0:1],
                    in_=ydst[s0 - 1, 4:8].rearrange("c p t -> p c t")[:, :, 127:128],
                    allow_slow_non_contiguous=True)
            em.stage("pool", fu2d, reads=[f"ymid_{l}_{s0 - 1}"], accw=["u2g"],
                     dma=True)
        else:
            em.stage("pool", lambda e, u2=u2: e.tensor_copy(
                u2[:, 4:8, 0:1], T["ycar"][:].unsqueeze(2)),
                reads=["ycar"], accw=["u2g"])
        if s0 <= OWN0 < s0 + nsub:
            o = (OWN0 - s0) * 128
            em.stage("pool", lambda e, o=o, u2=u2: e.tensor_scalar_mul(
                u2[:, 4:8, o:o + 1], u2[:, 4:8, o:o + 1],
                T["hval_s"][:, OWN0:OWN0 + 1]),
                reads=["u2g", "hval_s"], writes=["u2g"])
        for c8 in range(NCH):
            em.stage("sy", lambda e, s0=s0, nsub=nsub, ym=ym, c8=c8: e.dma_start(
                out=ym[:, c8, 0:nsub * 128].rearrange("p (s t) -> p s t", s=nsub),
                in_=ydst[s0:s0 + nsub, c8].rearrange("s p t -> p s t")),
                reads=ymk, writes=[f"ymg{gslot}"] if c8 == 0 else [],
                accw=[f"ymg{gslot}"] if c8 > 0 else [], dma=True)
        em.stage("pool", lambda e, ym=ym, TC=TC: e.tensor_copy(
            T["ycar"][:].unsqueeze(2), ym[:, 4:8, TC - 1:TC]),
            reads=[f"ymg{gslot}"], writes=["ycar"])

    def emit_stats_u2rg(jg):
        s0, nsub, TC, gslot = grp(jg)
        u2 = T["u2g"][:].rearrange("p (c t) -> p c t", c=NCH, t=384)

        def fst1(e, TC=TC, u2=u2):
            for c in range(NCH):
                i = e.matmul(ps["P2"][0:1, 0:TC], T["ones_c"][:], u2[:, c, 0:TC],
                             start=(c == 0), stop=(c == NCH - 1))
            return i
        em.stage("pe", fst1, reads=["u2g", "ones_c"], writes=["P2"])
        for c in range(NCH):
            sq = T["sqt"][:, (c % 2) * 384:((c % 2) + 1) * 384]
            em.stage("pool", lambda e, c=c, sq=sq, TC=TC, u2=u2: e.tensor_mul(
                sq[:, 0:TC], u2[:, c, 0:TC], u2[:, c, 0:TC]),
                reads=["u2g"], writes=[f"sqt{c % 2}"])
            em.stage("pe", lambda e, c=c, sq=sq, TC=TC: e.matmul(
                ps["P2"][0:1, 512:512 + TC], T["ones_c"][:], sq[:, 0:TC],
                start=(c == 0), stop=(c == NCH - 1)),
                reads=[f"sqt{c % 2}", "ones_c"], writes=["P2"])

        def fst2(e, bump, TC=TC):
            e.tensor_scalar_mul(T["trow"][0:1, 0:TC], ps["P2"][0:1, 0:TC], 1.0 / 1024.0)
            i = e.tensor_scalar_mul(T["trow"][0:1, 512:512 + TC],
                                    ps["P2"][0:1, 512:512 + TC], 1.0 / 1024.0)
            bump(i)
            i = e.tensor_mul(T["trow2"][0:1, 0:TC], T["trow"][0:1, 0:TC],
                             T["trow"][0:1, 0:TC])
            bump(i)
            e.tensor_sub(T["trow2"][0:1, 512:512 + TC], T["trow"][0:1, 512:512 + TC],
                         T["trow2"][0:1, 0:TC])
            return e.tensor_copy(T["mrow"][0:1, 0:TC], T["trow"][0:1, 0:TC])
        em.stage("dve", fst2, reads=["P2"], writes=["trow", "mrow"], steps=3)

        def fst3(e, bump, TC=TC):
            i = e.activation(T["trow2"][0:1, 0:TC], T["trow2"][0:1, 512:512 + TC],
                             AF.Ln, bias=T["ceps"][0:1, :])
            bump(i)
            return e.activation(T["rrow"][0:1, 0:TC], T["trow2"][0:1, 0:TC], AF.Exp,
                                scale=-0.5)
        em.stage("act", fst3, reads=["trow", "ceps"], writes=["rrow"], steps=2)

        def fbc(e, TC=TC):
            e.matmul(ps["P2"][:, 0:TC], T["ones_r"][:], T["mrow"][0:1, 0:TC],
                     start=True, stop=True)
            return e.matmul(ps["P2"][:, 512:512 + TC], T["ones_r"][:],
                            T["rrow"][0:1, 0:TC], start=True, stop=True)
        em.stage("pe", fbc, reads=["mrow", "rrow", "ones_r"], writes=["P2"])

        def fu2rg(e, bump, TC=TC, u2=u2):
            for c in range(NCH):
                i = e.tensor_sub(T["tf1"][:, 0:TC], u2[:, c, 0:TC], ps["P2"][:, 0:TC])
                bump(i)
                i = e.scalar_tensor_tensor(u2[:, c, 0:TC], T["tf1"][:, 0:TC],
                                           T["lng_s"][:, c:c + 1],
                                           ps["P2"][:, 512:512 + TC], ALU.mult, ALU.mult)
                if c < NCH - 1:
                    bump(i)
            return i
        em.stage("dve", fu2rg, reads=["P2", "u2g", "lng_s"], writes=["u2g", "tf1"],
                 steps=16)

    def emit_w1(jg):
        s0, nsub, TC, gslot = grp(jg)
        u2 = T["u2g"][:].rearrange("p (c t) -> p c t", c=NCH, t=384)
        fv = T["fsb"][:].rearrange("p (k t) -> p k t", k=NIC, t=384)
        for pc in range(NIC):
            pk = "P1" if pc % 2 == 0 else "P2"
            pw = ps[pk]

            def fw1(e, pc=pc, pw=pw, TC=TC, u2=u2):
                for c in range(NCH):
                    e.matmul(pw[:, 0:TC],
                             A[:, W1 + c * 5632 + 2 * pc * 128: W1 + c * 5632 + (2 * pc + 1) * 128],
                             u2[:, c, 0:TC], start=(c == 0), stop=(c == NCH - 1))
                for c in range(NCH):
                    i = e.matmul(pw[:, 512:512 + TC],
                                 A[:, W1 + c * 5632 + (2 * pc + 1) * 128: W1 + c * 5632 + (2 * pc + 2) * 128],
                                 u2[:, c, 0:TC], start=(c == 0), stop=(c == NCH - 1))
                return i
            em.stage("pe", fw1, reads=["u2g", "slotW1"], writes=[pk])

            em.stage("act", lambda e, pc=pc, pw=pw, TC=TC: e.activation(
                T["tf2"][:, 0:TC], pw[:, 512:512 + TC], AF.Gelu,
                bias=T["cb1_s"][:, 2 * pc + 1:2 * pc + 2]),
                reads=[pk, "cb1_s"], writes=["tf2"])

            em.stage("dve", lambda e, pc=pc, pw=pw, TC=TC, fv=fv: e.scalar_tensor_tensor(
                fv[:, pc, 0:TC], pw[:, 0:TC], T["cb1_s"][:, 2 * pc:2 * pc + 1],
                T["tf2"][:, 0:TC], ALU.add, ALU.mult),
                reads=[pk, "tf2", "cb1_s"], writes=["fsb"])

    def emit_w2(jg):
        s0, nsub, TC, gslot = grp(jg)
        ym = T["ymg"][:, gslot * (NCH * 384):(gslot + 1) * (NCH * 384)].rearrange(
            "p (c t) -> p c t", c=NCH, t=384)
        fv = T["fsb"][:].rearrange("p (k t) -> p k t", k=NIC, t=384)
        for oc in range(NCH):
            def fw2(e, oc=oc, TC=TC, fv=fv):
                for k in range(NIC):
                    i = e.matmul(ps["P1"][:, 0:TC],
                                 A[:, W2 + k * 1024 + oc * 128: W2 + k * 1024 + (oc + 1) * 128],
                                 fv[:, k, 0:TC], start=(k == 0), stop=(k == NIC - 1))
                return i
            em.stage("pe", fw2, reads=["fsb", "slotW2"], writes=["P1"])
            em.stage("dve", lambda e, oc=oc, TC=TC, ym=ym: e.scalar_tensor_tensor(
                ym[:, oc, 0:TC], ps["P1"][:, 0:TC], T["b2r_s"][:, oc:oc + 1],
                ym[:, oc, 0:TC], ALU.add, ALU.add),
                reads=["P1", f"ymg{gslot}", "b2r_s"], writes=[f"ymg{gslot}"])
        wkeys = [f"ysrc_{l + 1}_{s0 + j}" for j in range(nsub)]
        for c8 in range(NCH):
            em.stage("sy", lambda e, s0=s0, nsub=nsub, ym=ym, c8=c8: e.dma_start(
                out=ydst[s0:s0 + nsub, c8].rearrange("s p t -> p s t"),
                in_=ym[:, c8, 0:nsub * 128].rearrange("p (s t) -> p s t", s=nsub)),
                reads=[f"ymg{gslot}"], writes=wkeys if c8 == 0 else [],
                accw=wkeys if c8 > 0 else [], dma=True)

    emit_u2_loads(0)
    emit_stats_u2rg(0)
    for jg in range(ngroups):
        emit_w1(jg)
        if jg + 1 < ngroups:
            emit_u2_loads(jg + 1)
            emit_stats_u2rg(jg + 1)
        emit_w2(jg)


def _p_abranch(st, nlayers):
    em, T, ps, dram = st["em"], st["T"], st["ps"], st["dram"]
    em.barrier()
    A = T["arena"]
    WQ, WK, WV, WOA = 0, 8192, 16384, 45056
    xin = dram["xin"].ap()

    first = True
    for nm, off in (("Wq", WQ), ("Wk", WK), ("Wv", WV)):
        for c in range(NCH):
            em.stage("pool", lambda e, nm=nm, off=off, c=c: e.dma_start(
                out=A[:, off + c * 1024: off + (c + 1) * 1024],
                in_=dram[nm].ap()[ts(c), :]),
                writes=["slotW1"] if first else [],
                accw=[] if first else ["slotW1"], dma=True)
            first = False
    for c in range(NCH):
        em.stage("pool", lambda e, c=c: e.dma_start(
            out=A[:, WOA + c * 1024: WOA + (c + 1) * 1024],
            in_=dram["WoA"].ap()[ts(c), :]),
            writes=["slotW2"] if c == 0 else [], accw=["slotW2"] if c > 0 else [],
            dma=True)

    for dst, srcn in (("g1t_s", "g1t"), ("b1t_s", "b1t")):
        em.stage("sy", lambda e, d=dst, s=srcn: e.dma_start(
            out=T[d][:], in_=dram[s].ap()), writes=[dst], dma=True)

    def fz(e):
        e.memset(T["Sacc"][:], 0.0)
        e.memset(T["zacc"][:], 0.0)
        e.memset(T["SbD"][:], 0.0)
        return e.memset(T["zbD"][:], 0.0)
    em.stage("dve", fz, writes=["Sacc", "SbD"])

    _p_afront(st, 0)
    for j in range(NPS):
        if j + 1 < NPS:
            _p_afront(st, j + 1)
        _p_aback(st, j)
    for j in range(NPS, NPS + 32):
        _p_aownfront(st, j)
        _p_aback(st, j, own=True)

    _p_affn(st, nlayers)


def _p_afront(st, j):
    em, T, ps, dram = st["em"], st["T"], st["ps"], st["dram"]
    xin = dram["xin"].ap()
    slot = j % 2

    em.stage("sy", lambda e, j=j, slot=slot: e.dma_start(
        out=T["xld"][:, slot * D:(slot + 1) * D], in_=xin[ts(j), :]),
        writes=[f"xld{slot}"], dma=True)
    _p_ln(st, T["xld"][:, slot * D:(slot + 1) * D][:, :], T["xnt"][:, :],
          f"xld{slot}", "xnt", t3n="t3b")
    _p_ln(st, T["xnt"][:, :], T["t1b"][:, :], "xnt", "h1a",
          g=T["g1t_s"][:], b=T["b1t_s"][:], t3n="t3b")
    em.stage("dve", lambda e: e.tensor_copy(T["h1b"][:], T["t1b"][:]),
             reads=["h1a"], writes=["h1b"])
    _p_transpose8(st, T["h1b"][:], "h1b", "trE")
    hdst = T["h1fm"] if slot == 0 else T["h1fm2"]
    em.stage("dve", lambda e, hdst=hdst: e.tensor_copy(hdst[:], ps["trE"][:]),
             reads=["trE"], writes=[f"h1fm{slot}"])


def _p_aownfront(st, j):
    em, T, ps, dram = st["em"], st["T"], st["ps"], st["dram"]
    xin = dram["xin"].ap()
    slot = 0

    em.stage("sy", lambda e, j=j, slot=slot: e.dma_start(
        out=T["xld"][:, slot * D:(slot + 1) * D], in_=xin[ts(j), :]),
        writes=[f"xld{slot}"], dma=True)
    _p_ln(st, T["xld"][:, slot * D:(slot + 1) * D][:, :], T["xnt"][:, :],
          f"xld{slot}", "xnt")
    _p_ln(st, T["xnt"][:, :], T["t1"][:, :], "xnt", "h1a",
          g=T["g1t_s"][:], b=T["b1t_s"][:])
    em.stage("dve", lambda e: e.tensor_copy(T["h1b"][:], T["t1"][:]),
             reads=["h1a"], writes=["h1b"])
    _p_transpose8(st, T["h1b"][:], "h1b", "trE")
    em.stage("dve", lambda e: e.tensor_copy(T["h1fm"][:], ps["trE"][:]),
             reads=["trE"], writes=["h1fm0"])
    _p_transpose8(st, T["xnt"][:], "xnt", "P1", dt="f")
    em.stage("dve", lambda e: e.tensor_copy(T["xnfm"][:], ps["P1"][:, 0:1024]),
             reads=["P1"], writes=["xnfm"])


def _p_aback(st, j, own=False):
    em, T, ps, dram = st["em"], st["T"], st["ps"], st["dram"]
    A = T["arena"]
    WQ, WK, WV, WOA = 0, 8192, 16384, 45056
    slot = 0 if own else j % 2
    hsrc = T["h1fm"] if slot == 0 else T["h1fm2"]
    h1f = hsrc[:].rearrange("p (c t) -> p c t", c=NCH)
    hkey = f"h1fm{slot}"

    def fkv(e):
        for half in range(2):
            for c in range(NCH):
                e.matmul(ps["P1"][:, half * 512:(half + 1) * 512], h1f[:, c],
                         A[:, WK + c * 1024 + half * 512: WK + c * 1024 + (half + 1) * 512],
                         start=(c == 0), stop=(c == NCH - 1))
        for half in range(2):
            for c in range(NCH):
                i = e.matmul(ps["P2"][:, half * 512:(half + 1) * 512], h1f[:, c],
                             A[:, WV + c * 1024 + half * 512: WV + c * 1024 + (half + 1) * 512],
                             start=(c == 0), stop=(c == NCH - 1))
        return i
    em.stage("pe", fkv, reads=[hkey, "slotW1"], writes=["P1", "P2"])

    def fp1(e):
        e.tensor_scalar_min(T["t1"][:], ps["P1"][:, 0:1024], 0.0)
        return e.tensor_scalar_add(T["t2"][:], ps["P1"][:, 0:1024], 1.0)
    em.stage("dve", fp1, reads=["P1"], writes=["t1", "t2"])
    em.stage("act", lambda e: e.activation(T["t1"][:], T["t1"][:], AF.Exp),
             reads=["t1"], writes=["t1"])

    def fp2(e, bump, j=j, own=own):
        i = e.tensor_max(T["t3"][:], T["t1"][:], T["t2"][:])
        bump(i)
        if not own:
            return e.tensor_scalar_mul(T["Kkm"][:], T["t3"][:], T["pval_s"][:, j:j + 1])
        return e.tensor_copy(T["Kkm"][:], T["t3"][:])
    em.stage("dve", fp2, reads=["t1", "t2", "pval_s"], writes=["Kkm", "t3"], steps=2)
    em.stage("act", lambda e: e.copy(T["Vkm2"][:], ps["P2"][:, 0:1024]),
             reads=["P2"], writes=["Vkm2"])

    if own:
        def fq(e):
            for pr in range(NCH):
                for c in range(NCH):
                    i = e.matmul(ps["P1"][:, ts(pr)],
                                 A[:, WQ + c * 1024 + pr * 128: WQ + c * 1024 + (pr + 1) * 128],
                                 h1f[:, c], start=(c == 0), stop=(c == NCH - 1))
            return i
        em.stage("pe", fq, reads=[hkey, "slotW1"], writes=["P1"])

        def fq1(e):
            e.tensor_scalar_min(T["t1"][:], ps["P1"][:, 0:1024], 0.0)
            return e.tensor_scalar_add(T["t2"][:], ps["P1"][:, 0:1024], 1.0)
        em.stage("dve", fq1, reads=["P1"], writes=["t1", "t2"])
        em.stage("act", lambda e: e.activation(T["t1"][:], T["t1"][:], AF.Exp),
                 reads=["t1"], writes=["t1"])
        em.stage("dve", lambda e: e.tensor_max(T["Qfm"][:], T["t1"][:], T["t2"][:]),
                 reads=["t1", "t2"], writes=["Qfm"])

        _p_transpose8(st, T["Kkm"][:], "Kkm", "trF")
        em.stage("act", lambda e: e.copy(T["Kfm"][:], ps["trF"][:]),
                 reads=["trF"], writes=["Kfm"])

        for h in range(H):
            pb = (h % 2) * 64
            pc = (h // 2) * 128
            asl = h % 2

            em.stage("pe", lambda e, pb=pb, pc=pc, asl=asl: e.matmul(
                ps["P2"][:, 1024 + asl * 128: 1152 + asl * 128],
                T["Kfm"][pb:pb + 64, pc:pc + 128], T["Qfm"][pb:pb + 64, pc:pc + 128],
                start=True, stop=True),
                reads=["Kfm", "Qfm"], writes=[f"ATp{asl}"])

            em.stage("dve", lambda e, asl=asl: e.tensor_tensor(
                T["ATs"][:, asl * 128:(asl + 1) * 128],
                ps["P2"][:, 1024 + asl * 128: 1152 + asl * 128], T["triA_s"][:],
                ALU.mult),
                reads=[f"ATp{asl}", "triA_s"], writes=[f"ATs{asl}"])

            def fnd(e, h=h, pb=pb, pc=pc, asl=asl):
                vh = T["Vkm2"][:].rearrange("p (h d) -> p h d", h=H)[:, h]
                e.matmul(ps["P1"][:, h * 64:(h + 1) * 64],
                         T["ATs"][:, asl * 128:(asl + 1) * 128], vh,
                         start=True, stop=False)
                e.matmul(ps["P1"][:, h * 64:(h + 1) * 64],
                         T["Qfm"][pb:pb + 64, pc:pc + 128],
                         T["SbD"][pb:pb + 64, h * 64:(h + 1) * 64],
                         start=False, stop=True)
                e.matmul(ps["P1"][:, 1024 + h:1024 + h + 1],
                         T["ATs"][:, asl * 128:(asl + 1) * 128], T["ones_c"][:],
                         start=True, stop=False)
                return e.matmul(ps["P1"][:, 1024 + h:1024 + h + 1],
                                T["Qfm"][pb:pb + 64, pc:pc + 128],
                                T["zbD"][pb:pb + 64, h:h + 1],
                                start=False, stop=True)
            em.stage("pe", fnd, reads=[f"ATs{asl}", "Vkm2", "SbD", "zbD", "ones_c",
                                       "Qfm"], writes=["P1"])

        def fdv(e, bump):
            i = e.tensor_scalar_max(T["den_t"][:], ps["P1"][:, 1024:1024 + H], 1e-6)
            bump(i)
            i = e.reciprocal(T["rA"][:], T["den_t"][:])
            bump(i)
            return e.tensor_tensor(
                T["aAkm"][:].rearrange("p (h d) -> p h d", h=H),
                ps["P1"][:, 0:1024].rearrange("p (h d) -> p h d", h=H),
                T["rA"][:].unsqueeze(2).broadcast_to([128, H, DH]), ALU.mult)
        em.stage("dve", fdv, reads=["P1"], writes=["aAkm", "rA", "den_t"], steps=3)

    def fsu(e):
        kk = T["Kkm"][:].rearrange("p (h d) -> p h d", h=H)
        vv = T["Vkm2"][:].rearrange("p (h d) -> p h d", h=H)
        for h in range(H):
            e.matmul(ps["P2"][0:64, h * 64:(h + 1) * 64], kk[:, h], vv[:, h],
                     start=True, stop=True)
            i = e.matmul(ps["P2"][0:64, 1024 + h:1024 + h + 1], kk[:, h],
                         T["ones_c"][:], start=True, stop=True)
        return i
    em.stage("pe", fsu, reads=["Kkm", "Vkm2", "ones_c"], writes=["P2"])

    def fsa(e, bump):
        e.tensor_add(T["Sacc"][:], T["Sacc"][:], ps["P2"][0:64, 0:1024])
        i = e.tensor_add(T["zacc"][:], T["zacc"][:], ps["P2"][0:64, 1024:1024 + H])
        bump(i)
        e.tensor_copy(T["SaccB"][:], T["Sacc"][:])
        return e.tensor_copy(T["zaccB"][:], T["zacc"][:])
    em.stage("dve", fsa, reads=["P2", "Sacc"], writes=["Sacc", "SaccB"], steps=2)

    if j + 1 >= NPS:
        def fdp(e):
            e.matmul(ps["P2"][:, 0:512], T["dupI_s"][:], T["SaccB"][:, 0:512],
                     start=True, stop=True)
            e.matmul(ps["P2"][:, 512:1024], T["dupI_s"][:], T["SaccB"][:, 512:1024],
                     start=True, stop=True)
            return e.matmul(ps["P2"][:, 1024:1024 + H], T["dupI_s"][:], T["zaccB"][:],
                            start=True, stop=True)
        em.stage("pe", fdp, reads=["SaccB", "dupI_s"], writes=["P2"])

        def fdc(e):
            e.tensor_copy(T["SbD"][:], ps["P2"][:, 0:1024])
            return e.tensor_copy(T["zbD"][:], ps["P2"][:, 1024:1024 + H])
        em.stage("dve", fdc, reads=["P2"], writes=["SbD", "zbD"])

    if own:
        if st.get("dbg_suba") == j - NPS:
            for nm, key in (("daA", "aAkm"), ("dQf", "Qfm"), ("dKf", "Kfm"),
                            ("dVk", "Vkm2"), ("dSb", "SbD"), ("dzb", "zbD")):
                em.stage("pool", lambda e, nm=nm, key=key: e.dma_start(
                    out=dram[nm].ap(), in_=T[key][:]), reads=[key], dma=True)
        oj = j - NPS
        gi = oj % 4
        _p_transpose8(st, T["aAkm"][:], "aAkm", "trF")
        em.stage("dve", lambda e, gi=gi: e.tensor_copy(
            T["aAfm"][:].rearrange("p (c g t) -> p c g t", c=NCH, g=4)[:, :, gi],
            ps["trF"][:].rearrange("p (c t) -> p c t", c=NCH)),
            reads=["trF"], writes=["aAfm"])
        em.stage("dve", lambda e, gi=gi: e.tensor_copy(
            T["xag"][:].rearrange("p (c g t) -> p c g t", c=NCH, g=4)[:, :, gi],
            T["xnfm"][:].rearrange("p (c t) -> p c t", c=NCH)),
            reads=["xnfm"], writes=["xag"])

        if gi == 3:
            af = T["aAfm"][:].rearrange("p (c t) -> p c t", c=NCH)
            xg = T["xag"][:].rearrange("p (c t) -> p c t", c=NCH)
            for op in range(4):
                pk = "P2" if op % 2 == 0 else "P1"

                def fwo(e, op=op, pk=pk, af=af):
                    for oc in (2 * op, 2 * op + 1):
                        half = oc % 2
                        for ic in range(NCH):
                            i = e.matmul(ps[pk][:, half * 512:(half + 1) * 512],
                                         A[:, WOA + ic * 1024 + oc * 128:
                                            WOA + ic * 1024 + (oc + 1) * 128],
                                         af[:, ic], start=(ic == 0), stop=(ic == NCH - 1))
                    return i
                em.stage("pe", fwo, reads=["aAfm", "slotW2"], writes=[pk])

                def fad(e, op=op, pk=pk, xg=xg):
                    for oc in (2 * op, 2 * op + 1):
                        half = oc % 2
                        i = e.tensor_add(xg[:, oc], ps[pk][:, half * 512:(half + 1) * 512],
                                         xg[:, oc])
                    return i
                em.stage("dve", fad, reads=[pk, "xag"], writes=["xag"])

            for c8 in range(NCH):
                em.stage("sy", lambda e, oj=oj, xg=xg, c8=c8: e.dma_start(
                    out=dram["xasc"].ap()[oj - 3:oj + 1, c8].rearrange("s p t -> p s t"),
                    in_=xg[:, c8].rearrange("p (s t) -> p s t", s=4)),
                    reads=["xag"], writes=[f"xasc{oj // 4}"] if c8 == 0 else [],
                    accw=[f"xasc{oj // 4}"] if c8 > 0 else [], dma=True)


def _p_affn(st, nlayers):
    em, T, ps, dram = st["em"], st["T"], st["ps"], st["dram"]
    em.barrier()
    A = T["arena"]
    WF1, WF2 = 0, NCH * DF
    yfin = dram["ysc0" if nlayers % 2 == 0 else "ysc1"].ap()
    xin = dram["xin"].ap()

    for c in range(NCH):
        em.stage("pool", lambda e, c=c: e.dma_start(
            out=A[:, WF1 + c * DF: WF1 + (c + 1) * DF], in_=dram["Wf1"].ap()[ts(c), :]),
            writes=["slotW1"] if c == 0 else [], accw=["slotW1"] if c > 0 else [],
            dma=True)
    for c in range(NOC):
        em.stage("pool", lambda e, c=c: e.dma_start(
            out=A[:, WF2 + c * 1024: WF2 + (c + 1) * 1024],
            in_=dram["Wf2"].ap()[ts(c), :]),
            writes=["slotW2"] if c == 0 else [], accw=["slotW2"] if c > 0 else [],
            dma=True)

    for seg in range(8):
        pk = "P1" if seg % 2 == 0 else "P2"

        def fseg(e, seg=seg, pk=pk):
            for c in range(NCH):
                i = e.matmul(ps[pk][0:1, 0:512], T["b2a_s"][:, c:c + 1],
                             A[:, WF1 + c * DF + seg * 512: WF1 + c * DF + (seg + 1) * 512],
                             start=(c == 0), stop=(c == NCH - 1))
            return i
        em.stage("pe", fseg, reads=["b2a_s", "slotW1"], writes=[pk])
        pk2 = "P2" if pk == "P1" else "P1"
        em.stage("act", lambda e, pk=pk: e.copy(T["mrow"][0:1, 0:512],
                                                ps[pk][0:1, 0:512]),
                 reads=[pk], writes=["mrow"])

        def fsp(e, pk2=pk2):
            for u in range(4):
                i = e.matmul(ps[pk2][:, u:u + 1],
                             T["mrow"][0:1, u * 128:(u + 1) * 128],
                             T["ones_c"][0:1, 0:1], start=True, stop=True)
            return i
        em.stage("pe", fsp, reads=["mrow", "ones_c"], writes=[pk2])
        em.stage("dve", lambda e, seg=seg, pk2=pk2: e.tensor_copy(
            T["cbA_s"][:, seg * 4:seg * 4 + 4], ps[pk2][:, 0:4]),
            reads=[pk2], writes=["cbA_s"] if seg == 7 else [])

    fva = T["fA"][:].rearrange("p (k t) -> p k t", k=NOC, t=128)
    TC = 128
    for s0 in range(32):
        xg2 = T["xag2"][:].rearrange("p (c t) -> p c t", c=NCH, t=128)
        ygv = T["yg"][:].rearrange("p (c t) -> p c t", c=NCH, t=128)

        em.stage("sy", lambda e, s0=s0, xg2=xg2: e.dma_start(
            out=xg2, in_=dram["xasc"].ap()[s0].rearrange("c p t -> p c t")),
            reads=[f"xasc{s0 // 4}"], writes=["xag2"], dma=True)

        em.stage("pool", lambda e, xg2=xg2: e.tensor_copy(
            T["xab"][:], xg2.rearrange("p c t -> p (c t)")),
            reads=["xag2"], writes=["xab"])
        xabv = T["xab"][:].rearrange("p (c t) -> p c t", c=NCH, t=128)

        def fst1(e, xabv=xabv):
            for c in range(NCH):
                i = e.matmul(ps["P2"][0:1, 0:TC], T["ones_c"][:], xabv[:, c, 0:TC],
                             start=(c == 0), stop=(c == NCH - 1))
            return i
        em.stage("pe", fst1, reads=["xab", "ones_c"], writes=["P2"])
        for c in range(NCH):
            sq = T["sqA"][:, (c % 2) * 128:((c % 2) + 1) * 128]
            em.stage("pool", lambda e, c=c, sq=sq, xabv=xabv: e.tensor_mul(
                sq[:, 0:TC], xabv[:, c, 0:TC], xabv[:, c, 0:TC]),
                reads=["xab"], writes=[f"sqt{c % 2}"])
            em.stage("pe", lambda e, c=c, sq=sq: e.matmul(
                ps["P2"][0:1, 512:512 + TC], T["ones_c"][:], sq[:, 0:TC],
                start=(c == 0), stop=(c == NCH - 1)),
                reads=[f"sqt{c % 2}", "ones_c"], writes=["P2"])

        def fst2(e, bump):
            e.tensor_scalar_mul(T["trow"][0:1, 0:TC], ps["P2"][0:1, 0:TC], 1.0 / 1024.0)
            i = e.tensor_scalar_mul(T["trow"][0:1, 512:512 + TC],
                                    ps["P2"][0:1, 512:512 + TC], 1.0 / 1024.0)
            bump(i)
            i = e.tensor_mul(T["trow2"][0:1, 0:TC], T["trow"][0:1, 0:TC],
                             T["trow"][0:1, 0:TC])
            bump(i)
            e.tensor_sub(T["trow2"][0:1, 512:512 + TC], T["trow"][0:1, 512:512 + TC],
                         T["trow2"][0:1, 0:TC])
            return e.tensor_copy(T["mrow"][0:1, 0:TC], T["trow"][0:1, 0:TC])
        em.stage("dve", fst2, reads=["P2"], writes=["trow", "mrow"], steps=3)

        def fst3(e, bump):
            i = e.activation(T["trow2"][0:1, 0:TC], T["trow2"][0:1, 512:512 + TC],
                             AF.Ln, bias=T["ceps"][0:1, :])
            bump(i)
            return e.activation(T["rrow"][0:1, 0:TC], T["trow2"][0:1, 0:TC], AF.Exp,
                                scale=-0.5)
        em.stage("act", fst3, reads=["trow", "ceps"], writes=["rrow"], steps=2)

        def fbc(e):
            e.matmul(ps["P2"][:, 0:TC], T["ones_r"][:], T["mrow"][0:1, 0:TC],
                     start=True, stop=True)
            return e.matmul(ps["P2"][:, 512:512 + TC], T["ones_r"][:],
                            T["rrow"][0:1, 0:TC], start=True, stop=True)
        em.stage("pe", fbc, reads=["mrow", "rrow", "ones_r"], writes=["P2"])

        def fxc(e, bump, xabv=xabv):
            xcv = T["xac"][:].rearrange("p (c t) -> p c t", c=NCH, t=128)
            for c in range(NCH):
                i = e.tensor_sub(T["t1"][:, 0:TC], xabv[:, c, 0:TC], ps["P2"][:, 0:TC])
                bump(i)
                i = e.scalar_tensor_tensor(xcv[:, c, 0:TC], T["t1"][:, 0:TC],
                                           T["g2r_s"][:, c:c + 1],
                                           ps["P2"][:, 512:512 + TC], ALU.mult, ALU.mult)
                if c < NCH - 1:
                    bump(i)
            return i
        em.stage("dve", fxc, reads=["P2", "xab", "g2r_s"], writes=["xac", "t1"],
                 steps=16)
        xcv = T["xac"][:].rearrange("p (c t) -> p c t", c=NCH, t=128)

        for oc in range(NOC):
            pk = "P1" if oc % 2 == 0 else "P2"

            def fh(e, oc=oc, pk=pk, xcv=xcv):
                for c in range(NCH):
                    i = e.matmul(ps[pk][:, 0:TC],
                                 A[:, WF1 + c * DF + oc * 128: WF1 + c * DF + (oc + 1) * 128],
                                 xcv[:, c, 0:TC], start=(c == 0), stop=(c == NCH - 1))
                return i
            em.stage("pe", fh, reads=["xac", "slotW1"], writes=[pk])
            em.stage("act", lambda e, oc=oc, pk=pk: e.activation(
                fva[:, oc, 0:TC], ps[pk][:, 0:TC], AF.Gelu,
                bias=T["cbA_s"][:, oc:oc + 1]),
                reads=[pk, "cbA_s"], writes=["fA"])

        for oc in range(NCH):
            pk = "P1" if oc % 2 == 0 else "P2"

            def f2(e, oc=oc, pk=pk):
                for k in range(NOC):
                    i = e.matmul(ps[pk][:, 0:TC],
                                 A[:, WF2 + k * 1024 + oc * 128: WF2 + k * 1024 + (oc + 1) * 128],
                                 fva[:, k, 0:TC], start=(k == 0), stop=(k == NOC - 1))
                return i
            em.stage("pe", f2, reads=["fA", "slotW2"], writes=[pk])
            em.stage("dve", lambda e, oc=oc, pk=pk, xg2=xg2: e.tensor_add(
                xg2[:, oc, 0:TC], ps[pk][:, 0:TC], xg2[:, oc, 0:TC]),
                reads=[pk, "xag2"], writes=["xag2"])

        em.stage("sy", lambda e, s0=s0, ygv=ygv: e.dma_start(
            out=ygv, in_=yfin[OWN0 + s0].rearrange("c p t -> p c t")),
            reads=[f"ysrc_{nlayers + 1}_{OWN0 + s0}"], writes=["yg"], dma=True)

        em.stage("act", lambda e: e.activation(
            T["yg"][:], T["yg"][:], AF.Silu),
            reads=["yg"], writes=["yg"])

        em.stage("pool", lambda e: e.tensor_mul(
            T["prodb"][:], T["xag2"][:], T["yg"][:]),
            reads=["xag2", "yg"], writes=["prodb"])

        pv = T["prodb"][:].rearrange("p (c t) -> p c t", c=NCH, t=128)

        def ftp(e, pv=pv):
            for c in range(NCH):
                i = e.transpose(ps["trE"][:, ts(c)], pv[:, c], T["identb_s"][:])
            return i
        em.stage("pe", ftp, reads=["prodb", "identb_s"], writes=["trE"])
        em.stage("dve", lambda e: e.tensor_copy(T["prkm"][:], ps["trE"][:]),
                 reads=["trE"], writes=["prkm"])

        em.stage("sy", lambda e, s0=s0: e.dma_start(
            out=T["xkm"][:], in_=xin[PREF + s0 * 128: PREF + (s0 + 1) * 128, :]),
            writes=["xkm"], dma=True)

        em.stage("dve", lambda e: e.tensor_add(
            T["xkm"][:], T["prkm"][:], T["xkm"][:]),
            reads=["prkm", "xkm"], writes=["xkm"])

        em.stage("sy", lambda e, s0=s0: e.dma_start(
            out=dram["out"].ap()[s0 * 128:(s0 + 1) * 128, :], in_=T["xkm"][:]),
            reads=["xkm"], dma=True)



_CACHE = {}


def _host_prep(inputs):
    x = np.ascontiguousarray(inputs["L0_attn_x"])
    f32 = np.float32

    ii = np.arange(128)
    mA = np.where(ii[:, None] > ii[None, :], 0, 1).astype(f32)
    mC = np.where(ii[:, None] < ii[None, :], 0, 1).astype(f32)
    allm = np.zeros((128, 128), f32)
    zer = np.ones((128, 128), f32)
    triA = (ii[:, None] <= ii[None, :]).astype(f32)
    identf = np.eye(128, dtype=f32)
    dupI = np.concatenate([np.eye(64, dtype=f32), np.eye(64, dtype=f32)], axis=1)

    half = np.arange(0, DH, 2, dtype=f32)
    inv_freq = (f32(1.0) / f32(10000.0) ** (half / f32(DH))).astype(f32)
    tg = np.arange(16384, dtype=f32)
    freqs = np.concatenate([tg[:, None] * inv_freq[None, :]] * 2, -1).astype(f32)
    base = ((half + f32(0.4 * DH)) / f32(1.4 * DH)).astype(f32)
    power = ((tg - 8192.0) / f32(128.0)).astype(f32)
    scale = np.concatenate([base[None, :] ** power[:, None]] * 2, -1).astype(f32)
    cos, sin = np.cos(freqs).astype(f32), np.sin(freqs).astype(f32)
    sgn = np.concatenate([-np.ones(32, f32), np.ones(32, f32)])
    cs8 = (cos * scale * 8).astype(f32)
    ss8s = (sin * scale * 8 * sgn).astype(f32)
    ci = (cos / scale).astype(f32)
    sis = (sin / scale * sgn).astype(f32)

    W1 = inputs["L0_ffn_W1"]; W2 = inputs["L0_ffn_W2"]; b1 = inputs["L0_ffn_b1"]
    W1p = np.zeros((6, D, 2 * DI), W1.dtype)
    b1p = np.zeros((6, 2 * DI), b1.dtype)
    W2p = np.zeros((6, DI, D), W2.dtype)
    for c in range(NIC):
        lo, hi = c * 128, min((c + 1) * 128, 2730)
        n = hi - lo
        if n <= 0:
            continue
        W1p[:, :, 2 * c * 128: 2 * c * 128 + n] = W1[:, :, lo:hi]
        W1p[:, :, (2 * c + 1) * 128: (2 * c + 1) * 128 + n] = W1[:, :, 2730 + lo: 2730 + hi]
        b1p[:, 2 * c * 128: 2 * c * 128 + n] = b1[:, lo:hi]
        b1p[:, (2 * c + 1) * 128: (2 * c + 1) * 128 + n] = b1[:, 2730 + lo: 2730 + hi]
        W2p[:, c * 128: c * 128 + n, :] = W2[:, lo:hi, :]
    b1pr = b1p.reshape(6, 2 * NIC, 128).transpose(0, 2, 1).copy()

    shared = {
        "rot_tab": (cs8, ss8s, ci, sis),
        "mA01": mA, "mC01": mC, "allm01": allm, "ones01": zer,
        "triA": triA.astype(bf16), "identf": identf,
        "identb": identf.astype(bf16), "dupI": dupI.astype(bf16),
        "Wqkv": inputs["L0_ffn_Wqkv"], "Wo6": inputs["L0_ffn_Wo"],
        "W1p": W1p, "W2p": W2p, "b1pr": b1pr,
        "lnbr": inputs["L0_ffn_ln_b"].reshape(6, NCH, 128).transpose(0, 2, 1).copy(),
        "lngr": inputs["L0_ffn_ln_g"].reshape(6, NCH, 128).transpose(0, 2, 1).copy(),
        "b2r": inputs["L0_ffn_b2"].reshape(6, NCH, 128).transpose(0, 2, 1).copy(),
        "qsrep": np.broadcast_to(np.tile(inputs["L0_ffn_q_scale"], (1, H))[:, None, :],
                                 (6, 128, D)).copy(),
        "ksrep": np.broadcast_to(np.tile(inputs["L0_ffn_k_scale"], (1, H))[:, None, :],
                                 (6, 128, D)).copy(),
        "Wq": inputs["L0_attn_Wq"], "Wk": inputs["L0_attn_Wk"],
        "Wv": inputs["L0_attn_Wv"], "WoA": inputs["L0_attn_Wo"],
        "Wf1": inputs["L0_attn_Wf1"], "Wf2": inputs["L0_attn_Wf2"],
        "g1t": np.broadcast_to(inputs["L0_attn_ln1_g"][None, :], (128, D)).copy(),
        "b1t": np.broadcast_to(inputs["L0_attn_ln1_b"][None, :], (128, D)).copy(),
        "g2r": inputs["L0_attn_ln2_g"].reshape(NCH, 128).T.copy(),
        "b2ar": inputs["L0_attn_ln2_b"].reshape(NCH, 128).T.copy(),
    }

    maps = []
    for c in range(8):
        b, q = c // 4, c % 4
        s = q * 4096
        xfull = np.zeros((PREF + OWN, D), f32)
        lo = s - PREF
        xfull[max(0, -lo):, :] = x[b, max(0, lo): s + OWN]
        xb = xfull[PREF - HALO:, :].copy()
        gidx = np.clip(np.arange(-HALO, OWN) + s, 0, 16383)
        cs8c, ss8c, cic, sic = (t[gidx] for t in shared["rot_tab"])
        rotc = np.concatenate([cs8c, ss8c, cic, sic], axis=1).reshape(NS, 128, 256)
        pv = np.zeros((128, NPS), f32)
        pv[:, NPS - 32 * q:] = 1.0
        hv = np.ones((128, NS), f32)
        if q == 0:
            hv[:, OWN0] = 0.0
        m0 = shared["allm01"] if q == 0 else shared["mA01"]
        m1 = shared["allm01"] if q == 0 else shared["ones01"]
        m2 = shared["allm01"] if q == 0 else shared["mA01"]
        mh = np.hstack([shared["mA01"].T, shared["ones01"], shared["mC01"].T])
        mh0 = np.hstack([m0.T, m1.T, shared["mC01"].T])
        mh1 = np.hstack([m2.T, shared["ones01"], shared["mC01"].T])
        mp = {k: v for k, v in shared.items() if k not in ("rot_tab", "mA01", "mC01",
                                                           "allm01", "ones01")}
        mp.update({
            "xin": xfull, "xbuf": xb, "rotcat": np.ascontiguousarray(rotc),
            "pvalidT": pv, "hvalrow": hv,
            "mh_t": np.ascontiguousarray(mh.astype(bf16)),
            "mh0_t": np.ascontiguousarray(mh0.astype(bf16)),
            "mh1_t": np.ascontiguousarray(mh1.astype(bf16)),
        })
        maps.append(mp)
    return maps


def kernel(**inputs):
    maps = _host_prep(inputs)
    if "nc" not in _CACHE:
        _CACHE["nc"] = build_program()
    nc = _CACHE["nc"]
    res = run_bass_kernel_spmd(nc, maps, core_ids=list(range(8)))
    out = np.empty((2, 16384, D), np.float32)
    for c in range(8):
        b, q = c // 4, c % 4
        out[b, q * 4096:(q + 1) * 4096] = res.results[c]["out"]
    return out
